# version 05
# baseline (speedup 1.0000x reference)
import contextlib
import numpy as np
import ml_dtypes

import concourse.bass as bass
import concourse.mybir as mybir
from concourse.bass_utils import run_bass_kernel_spmd

F32 = mybir.dt.float32
BF16 = mybir.dt.bfloat16
AF = mybir.ActivationFunctionType
OP = mybir.AluOpType
AX = mybir.AxisListType
bf16 = ml_dtypes.bfloat16

T = 4096
D = 1024
NT = 32
KC = 8
PAIRS = 8
QB = 8
KB = 32
NE = 8
HC = 32
NG = 8
NTB = 8
LAM_INIT = float(0.8 - 0.6 * np.exp(-0.3 * 12))

(C_NEG12, C_EIGHTH, C_NINV1024, C_INV1024, C_NINV128, C_INV128, C_EPS,
 C_NEGHALF, C_ONE, C_BIG, C_LAMI) = range(11)
NCONST = 16

DEBUG = False
MAX_PHASE = 4
NT_LIM = NT
BODY_LIM = 7
N_CORES = 8


def _flat(waits):
    out = []
    for w in waits:
        if w is None:
            continue
        if isinstance(w, list):
            out.extend(t for t in w if t is not None)
        else:
            out.append(w)
    return out


def toks(*ws):
    best = {}
    for t in _flat(ws):
        s, v = t
        if best.get(s, -1) < v:
            best[s] = v
    return [(s, v) for s, v in best.items()]


class Emitter:
    ENGS = ("t", "v", "s", "g", "y")

    def __init__(self):
        self.mute = False
        self.ops = {e: [] for e in self.ENGS}
        self.counts = {}
        self.watermark = {}
        self.sems = {}

    def add_sem(self, name, handle):
        self.sems[name] = handle
        self.counts[name] = 0

    def op(self, eng, fn, waits=(), inc=None, inc_by=1):
        if self.mute:
            return (inc, self.counts[inc]) if inc else None
        for sem, val in toks(*waits):
            assert val <= self.counts[sem], f"wait on future: {sem} {val} > {self.counts[sem]}"
            key = (eng, sem)
            if self.watermark.get(key, 0) >= val:
                continue
            self.watermark[key] = val
            h = self.sems[sem]
            self.ops[eng].append(lambda e, h=h, v=val: e.wait_ge(h, v))
        if inc is None:
            self.ops[eng].append(fn)
            return None
        h = self.sems[inc]
        def wrapped(e, fn=fn, h=h, n=inc_by):
            fn(e).then_inc(h, n)
        self.ops[eng].append(wrapped)
        self.counts[inc] += inc_by
        return (inc, self.counts[inc])

    def barrier(self, waits):
        if self.mute:
            return
        for eng in self.ENGS:
            for sem, val in toks(*waits):
                key = (eng, sem)
                if self.watermark.get(key, 0) >= val:
                    continue
                self.watermark[key] = val
                h = self.sems[sem]
                self.ops[eng].append(lambda e, h=h, v=val: e.wait_ge(h, v))

    def emit(self, block):
        ops = self.ops
        @block.tensor
        def _(t):
            for f in ops["t"]: f(t)
        @block.vector
        def _(v):
            for f in ops["v"]: f(v)
        @block.scalar
        def _(s):
            for f in ops["s"]: f(s)
        @block.gpsimd
        def _(g):
            for f in ops["g"]: f(g)
        @block.sync
        def _(y):
            for f in ops["y"]: f(y)


def build_program(debug=False):
    nc = bass.Bass("TRN2", target_bir_lowering=False)

    x_d = nc.dram_tensor("x", [T, D], F32, kind="ExternalInput")
    wq_d = nc.dram_tensor("wq_t", [PAIRS, KC, 128, 128], F32, kind="ExternalInput")
    wk_d = nc.dram_tensor("wk_t", [PAIRS, KC, 128, 128], F32, kind="ExternalInput")
    wv_d = nc.dram_tensor("wv_t", [PAIRS, KC, 128, 128], F32, kind="ExternalInput")
    wo_d = nc.dram_tensor("wo_t", [KC, 128, D], F32, kind="ExternalInput")
    wr_d = nc.dram_tensor("wr_t", [KC, 128, NE], F32, kind="ExternalInput")
    w1_d = nc.dram_tensor("w1_t", [NE, NG, 128, 4096], F32, kind="ExternalInput")
    w2_d = nc.dram_tensor("w2_t", [NE, NG, 128, 4096], F32, kind="ExternalInput")
    b1_d = nc.dram_tensor("b1_t", [128, NE * HC], F32, kind="ExternalInput")
    b2_d = nc.dram_tensor("b2_t", [NE, D], F32, kind="ExternalInput")
    lng_d = nc.dram_tensor("lng_t", [128, 128], F32, kind="ExternalInput")
    lnb_d = nc.dram_tensor("lnb_t", [128, 128], F32, kind="ExternalInput")
    lrow_d = nc.dram_tensor("lrow", [1, 256], F32, kind="ExternalInput")
    consts_d = nc.dram_tensor("consts", [128, NCONST], F32, kind="ExternalInput")
    idf_d = nc.dram_tensor("idf", [128, 128], F32, kind="ExternalInput")
    onesr_d = nc.dram_tensor("onesr", [1, 128], F32, kind="ExternalInput")

    out_d = nc.dram_tensor("out", [T, D], F32, kind="ExternalOutput")

    wsc1_d = nc.dram_tensor("wsc1", [NE, NG, 128, 4096], BF16, kind="Internal")
    wsc2_d = nc.dram_tensor("wsc2", [NE, NG, 128, 4096], BF16, kind="Internal")
    ssc_d = nc.dram_tensor("ssc", [T, D], BF16, kind="Internal")
    combT_d = nc.dram_tensor("combT_sc", [8, T], BF16, kind="Internal")

    dbg = {}
    if debug:
        dbg["xnT"] = nc.dram_tensor("dbg_xnT", [128, KC * T], BF16, kind="ExternalOutput")
        dbg["comb"] = nc.dram_tensor("dbg_comb", [128, NT * 8], F32, kind="ExternalOutput")
        dbg["lam"] = nc.dram_tensor("dbg_lam", [128, 1], F32, kind="ExternalOutput")
        dbg["s"] = nc.dram_tensor("dbg_s", [T, D], BF16, kind="ExternalOutput")
        dbg["oT"] = nc.dram_tensor("dbg_oT", [128, PAIRS * T], BF16, kind="ExternalOutput")

    E = Emitter()

    with contextlib.ExitStack() as ctx:
        sbuf = lambda n, sh, dt: ctx.enter_context(nc.sbuf_tensor(n, sh, dt))
        psum = lambda n, sh, dt: ctx.enter_context(nc.psum_tensor(n, sh, dt))
        _sems = ["tS", "sS", "vS", "sm1", "sm2", "cbt0", "cbt1", "woq", "stq", "xtq", "ctq", "dbq",
                 "cin0", "cin1", "cout0", "cout1", "x0", "x1", "qkv0", "qkv1",
                 "sw0", "sw1", "ow0", "ow1",
                 "w1s0", "w1s1", "w1s2", "w1s3", "w2s0", "w2s1", "w2s2", "w2s3"]
        for sname in _sems:
            E.add_sem(sname, ctx.enter_context(nc.semaphore(sname)))

        pb = [psum(f"pb{i}", [128, 512], F32) for i in range(8)]

        xnT = sbuf("xnT", [128, KC * T], BF16)
        comb = sbuf("comb", [128, NT * 8], F32)
        b1s = sbuf("b1s", [128, NE * HC], F32)
        b2s = sbuf("b2s", [NE, D], BF16)
        wrs = sbuf("wrs", [128, KC * NE], F32)
        cst = sbuf("cst", [128, NCONST], F32)
        lamc = sbuf("lamc", [128, 1], F32)
        lrow = sbuf("lrow_s", [1, 256], F32)
        onesr = sbuf("onesr_s", [1, 128], F32)
        lprod = sbuf("lprod", [1, 128], F32)
        l11 = sbuf("l11", [1, 16], F32)
        stat = sbuf("stat", [128, 64], F32)

        cc = lambda i: cst[:, i:i + 1]

        for dst, src in ((b1s, b1_d), (cst, consts_d), (lrow, lrow_d), (onesr, onesr_d)):
            E.op("g", lambda g, d=dst, s=src: g.dma_start(d[:], s.ap()), inc="sm1", inc_by=16)
        E.op("g", lambda g: g.dma_start(wrs[:].rearrange("p (k e) -> p k e", k=KC), wr_d.ap().rearrange("k p e -> p k e")), inc="sm1", inc_by=16)
        E.op("g", lambda g: g.dma_start(b2s[:], b2_d.ap()), inc="sm1", inc_by=16)
        tok_smalls = ("sm1", E.counts["sm1"])

        t1 = E.op("v", lambda v: v.tensor_tensor(lprod[0:1, 0:64], lrow[0:1, 0:64], lrow[0:1, 64:128], op=OP.mult),
                  waits=[tok_smalls], inc="vS")
        t2 = E.op("v", lambda v: v.tensor_tensor(lprod[0:1, 64:128], lrow[0:1, 128:192], lrow[0:1, 192:256], op=OP.mult),
                  waits=[t1], inc="vS")
        t3 = E.op("v", lambda v: v.reduce_sum(l11[0:1, 0:1], lprod[0:1, 0:64], axis=AX.X), waits=[t2], inc="vS")
        t4 = E.op("v", lambda v: v.reduce_sum(l11[0:1, 1:2], lprod[0:1, 64:128], axis=AX.X), waits=[t3], inc="vS")
        t5 = E.op("s", lambda s: s.activation(l11[0:1, 2:3], l11[0:1, 0:1], AF.Exp), waits=[t3, tok_smalls], inc="sS")
        t6 = E.op("s", lambda s: s.activation(l11[0:1, 3:4], l11[0:1, 1:2], AF.Exp), waits=[t4, t5], inc="sS")
        t7 = E.op("v", lambda v: v.tensor_tensor(l11[0:1, 4:5], l11[0:1, 2:3], l11[0:1, 3:4], op=OP.subtract),
                  waits=[t6], inc="vS")
        t8 = E.op("v", lambda v: v.tensor_scalar(l11[0:1, 5:6], l11[0:1, 4:5], cst[0:1, C_LAMI:C_LAMI + 1], None, op0=OP.add),
                  waits=[t7], inc="vS")
        t9 = E.op("t", lambda t: t.matmul(pb[7][:, 0:1], onesr[0:1, :], l11[0:1, 5:6], start=True, stop=True),
                  waits=[t8, tok_smalls], inc="tS")
        tok_lam = E.op("v", lambda v: v.tensor_copy(lamc[:], pb[7][:, 0:1]), waits=[t9], inc="vS")
        if debug:
            E.op("y", lambda y: y.dma_start(dbg["lam"].ap(), lamc[:]), waits=[tok_lam], inc="dbq", inc_by=16)

        E.mute = MAX_PHASE < 1
        with contextlib.ExitStack() as c23:
            sb23 = lambda n, sh, dt: c23.enter_context(nc.sbuf_tensor(n, sh, dt))
            oT = sb23("oT", [128, PAIRS * T], BF16)
            lngt = sb23("lngt", [128, 128], F32)
            lnbt = sb23("lnbt", [128, 128], F32)
            idf = sb23("idf_s", [128, 128], F32)
            convb = [sb23(f"convb{i}", [128, 2048], BF16) for i in range(2)]
            for dst_, src_ in ((lngt, lng_d), (lnbt, lnb_d), (idf, idf_d)):
                E.op("g", lambda g, d=dst_, s=src_: g.dma_start(d[:], s.ap()), inc="sm2", inc_by=16)
            tok_smalls2 = ("sm2", E.counts["sm2"])

            conv_done_w1 = [None] * NE
            conv_done_w2 = [None] * NE
            cvt_slot = [None, None]
            conv_steps = []
            for ei in range(NE):
                for which in range(2):
                    for gi in range(NG):
                        for hf in range(2):
                            conv_steps.append((which, ei, gi, hf))
            conv_idx = [0]

            def emit_conv(n):
                saved = E.mute
                E.mute = MAX_PHASE < 4
                for _ in range(n):
                    if conv_idx[0] >= len(conv_steps):
                        break
                    which, e_, gi, hf = conv_steps[conv_idx[0]]
                    sl = conv_idx[0] % 2
                    conv_idx[0] += 1
                    src_, dstd = ((w1_d, wsc1_d), (w2_d, wsc2_d))[which]
                    tin = E.op("g", lambda g, s=src_, e=e_, gg=gi, sl=sl, hf=hf:
                               g.dma_start(convb[sl][:], s.ap()[e, gg][:, hf * 2048:(hf + 1) * 2048]),
                               waits=[cvt_slot[sl]], inc=f"cin{sl}", inc_by=16)
                    tout = E.op("y", lambda y, d=dstd, e=e_, gg=gi, sl=sl, hf=hf:
                                y.dma_start(d.ap()[e, gg][:, hf * 2048:(hf + 1) * 2048], convb[sl][:]),
                                waits=[tin], inc=f"cout{sl}", inc_by=16)
                    cvt_slot[sl] = tout
                    if gi == NG - 1 and hf == 1:
                        done = toks(cvt_slot[0], cvt_slot[1])
                        if which == 0:
                            conv_done_w1[e_] = done
                        else:
                            conv_done_w2[e_] = done
                E.mute = saved

            E.mute = MAX_PHASE < 1
            xnT_tok = [None] * NT
            combT_wr = [None] * NT
            s_wr = [None] * NT
            with contextlib.ExitStack() as c1:
                sb1 = lambda n, sh, dt: c1.enter_context(nc.sbuf_tensor(n, sh, dt))
                xbuf = [sb1(f"xbuf{i}", [128, D], F32) for i in range(2)]
                xnf = [sb1(f"xnf{i}", [128, D], F32) for i in range(2)]
                xtr = [sb1(f"xtr{i}", [128, 128], F32) for i in range(2)]
                xsq = [sb1(f"xsq{i}", [128, D], F32) for i in range(2)]
                lg = [sb1(f"lg{i}", [128, 64], F32) for i in range(2)]
                cts = [sb1(f"cts{i}", [8, 128], BF16) for i in range(2)]

                x_tiles = x_d.ap().rearrange("(n p) d -> n p d", p=128)
                xbuf_free = [None, None]
                xnf_free = [None, None]
                xtr_free = [None, None]
                tp_free = [[tok_lam], None]
                sq_free = [None, None]
                lg_free = [None, None]
                cts_free = [None, None]
                logit_psum_free = None
                prev_sq = [None, None]

                for i in range(NT):
                    E.mute = (i >= NT_LIM) or (MAX_PHASE < 1)
                    emit_conv(4)
                    bi = i % 2
                    B0 = 32 * bi
                    tok_x = E.op("g", lambda g, i=i, bi=bi: g.dma_start(xbuf[bi][:], x_tiles[i]),
                                 waits=[xbuf_free[bi]], inc=f"x{bi}", inc_by=16)
                    t_s2 = E.op("s", lambda s, bi=bi: s.activation(xsq[bi][:], xbuf[bi][:], AF.Square,
                                                                   accum_out=stat[:, B0 + 0:B0 + 1]),
                                waits=[tok_x, tok_smalls, sq_free[bi], prev_sq[bi]], inc="sS")
                    prev_sq[bi] = t_s2
                    t_s1 = E.op("v", lambda v, bi=bi: v.reduce_sum(stat[:, B0 + 1:B0 + 2], xbuf[bi][:], axis=AX.X),
                                waits=[tok_x], inc="vS")
                    t_nmu = E.op("v", lambda v: v.tensor_scalar(stat[:, B0 + 2:B0 + 3], stat[:, B0 + 1:B0 + 2], cc(C_NINV1024), None, op0=OP.mult),
                                 waits=[t_s1], inc="vS")
                    t_m2 = E.op("v", lambda v: v.tensor_tensor(stat[:, B0 + 3:B0 + 4], stat[:, B0 + 2:B0 + 3], stat[:, B0 + 2:B0 + 3], op=OP.mult),
                                waits=[t_nmu], inc="vS")
                    t_va = E.op("v", lambda v: v.tensor_scalar(stat[:, B0 + 4:B0 + 5], stat[:, B0 + 0:B0 + 1], cc(C_INV1024), None, op0=OP.mult),
                                waits=[t_s2, t_m2], inc="vS")
                    sq_free[bi] = t_va
                    t_vb = E.op("v", lambda v: v.tensor_tensor(stat[:, B0 + 5:B0 + 6], stat[:, B0 + 4:B0 + 5], stat[:, B0 + 3:B0 + 4], op=OP.subtract),
                                waits=[t_va], inc="vS")
                    t_ln = E.op("s", lambda s: s.activation(stat[:, B0 + 6:B0 + 7], stat[:, B0 + 5:B0 + 6], AF.Ln, bias=cc(C_EPS)),
                                waits=[t_vb], inc="sS")
                    t_rs = E.op("s", lambda s: s.activation(stat[:, B0 + 7:B0 + 8], stat[:, B0 + 6:B0 + 7], AF.Exp, scale=cc(C_NEGHALF)),
                                waits=[t_ln], inc="sS")
                    t_xn = E.op("v", lambda v, bi=bi: v.tensor_scalar(xnf[bi][:], xbuf[bi][:], stat[:, B0 + 2:B0 + 3],
                                                                      stat[:, B0 + 7:B0 + 8], op0=OP.add, op1=OP.mult),
                                waits=[t_rs, xnf_free[bi]], inc="vS")
                    xbuf_free[bi] = t_xn

                    if BODY_LIM < 2:
                        E.mute = True
                    last_sc = None
                    last_vc = None
                    t_mm = None
                    for kc in range(KC):
                        tb_ = kc % 2
                        pbank = pb[7] if tb_ == 0 else pb[4]
                        t_tp = E.op("t", lambda t, bi=bi, kc=kc, pk=pbank: t.transpose(
                            pk[:, 0:128], xnf[bi][:, kc * 128:(kc + 1) * 128], idf[:]),
                            waits=[t_xn, tok_smalls, tok_smalls2, tp_free[tb_]], inc="tS")
                        if BODY_LIM < 3:
                            E.mute = True
                        t_sc = E.op("s", lambda s, i=i, kc=kc, pk=pbank: s.copy(
                            xnT[:, kc * T + i * 128: kc * T + (i + 1) * 128], pk[:, 0:128]),
                            waits=[t_tp], inc="sS")
                        if BODY_LIM >= 2 and not (i >= NT_LIM or MAX_PHASE < 1):
                            E.mute = False
                        if BODY_LIM < 4:
                            E.mute = True
                        t_vc = E.op("s", lambda s, kc=kc, pk=pbank: s.copy(xtr[kc % 2][:], pk[:, 0:128]),
                                    waits=[t_tp, xtr_free[kc % 2]], inc="sS")
                        if BODY_LIM >= 2 and not (i >= NT_LIM or MAX_PHASE < 1):
                            E.mute = False
                        tp_free[tb_] = toks(t_sc, t_vc)
                        if BODY_LIM < 5:
                            E.mute = True
                        t_mm = E.op("t", lambda t, kc=kc: t.matmul(
                            pb[5][:, 0:8], xtr[kc % 2][:], wrs[:, kc * NE:(kc + 1) * NE],
                            start=(kc == 0), stop=(kc == KC - 1)),
                            waits=[t_vc, tok_smalls] + ([logit_psum_free] if kc == 0 else []), inc="tS")
                        if BODY_LIM >= 2 and not (i >= NT_LIM or MAX_PHASE < 1):
                            E.mute = False
                        xtr_free[kc % 2] = t_mm
                        last_sc, last_vc = t_sc, t_vc
                    xnT_tok[i] = last_sc
                    xnf_free[bi] = toks(last_sc, last_vc)

                    if BODY_LIM < 6:
                        E.mute = True
                    t_l = E.op("v", lambda v: v.tensor_copy(lg[bi][:, 0:8], pb[5][:, 0:8]),
                               waits=[t_mm, lg_free[bi]], inc="vS")
                    logit_psum_free = t_l
                    t_m1 = E.op("v", lambda v: v.reduce_max(stat[:, B0 + 8:B0 + 9], lg[bi][:, 0:8], axis=AX.X), waits=[t_l], inc="vS")
                    t_k1 = E.op("v", lambda v: v.tensor_scalar(lg[bi][:, 8:16], lg[bi][:, 0:8], stat[:, B0 + 8:B0 + 9], None, op0=OP.is_ge),
                                waits=[t_m1], inc="vS")
                    t_pen = E.op("v", lambda v: v.tensor_scalar(lg[bi][:, 16:24], lg[bi][:, 8:16], cc(C_BIG), None, op0=OP.mult),
                                 waits=[t_k1], inc="vS")
                    t_msk = E.op("v", lambda v: v.tensor_tensor(lg[bi][:, 24:32], lg[bi][:, 0:8], lg[bi][:, 16:24], op=OP.subtract),
                                 waits=[t_pen], inc="vS")
                    t_m2r = E.op("v", lambda v: v.reduce_max(stat[:, B0 + 9:B0 + 10], lg[bi][:, 24:32], axis=AX.X), waits=[t_msk], inc="vS")
                    t_k2 = E.op("v", lambda v: v.tensor_scalar(lg[bi][:, 32:40], lg[bi][:, 24:32], stat[:, B0 + 9:B0 + 10], None, op0=OP.is_ge),
                                waits=[t_m2r], inc="vS")
                    t_d = E.op("v", lambda v: v.tensor_tensor(stat[:, B0 + 10:B0 + 11], stat[:, B0 + 9:B0 + 10], stat[:, B0 + 8:B0 + 9], op=OP.subtract),
                               waits=[t_m2r], inc="vS")
                    t_e = E.op("s", lambda s: s.activation(stat[:, B0 + 11:B0 + 12], stat[:, B0 + 10:B0 + 11], AF.Exp), waits=[t_d], inc="sS")
                    t_e1 = E.op("v", lambda v: v.tensor_scalar(stat[:, B0 + 12:B0 + 13], stat[:, B0 + 11:B0 + 12], cc(C_ONE), None, op0=OP.add),
                                waits=[t_e], inc="vS")
                    t_p1 = E.op("v", lambda v: v.reciprocal(stat[:, B0 + 13:B0 + 14], stat[:, B0 + 12:B0 + 13]), waits=[t_e1], inc="vS")
                    t_p2 = E.op("v", lambda v: v.tensor_tensor(stat[:, B0 + 14:B0 + 15], stat[:, B0 + 11:B0 + 12], stat[:, B0 + 13:B0 + 14], op=OP.mult),
                                waits=[t_p1], inc="vS")
                    t_c1 = E.op("v", lambda v: v.tensor_scalar(lg[bi][:, 40:48], lg[bi][:, 8:16], stat[:, B0 + 13:B0 + 14], None, op0=OP.mult),
                                waits=[t_p1], inc="vS")
                    t_c2 = E.op("v", lambda v: v.tensor_scalar(lg[bi][:, 48:56], lg[bi][:, 32:40], stat[:, B0 + 14:B0 + 15], None, op0=OP.mult),
                                waits=[t_p2, t_k2], inc="vS")
                    t_cb = E.op("v", lambda v, i=i: v.tensor_tensor(
                        comb[:, i * 8:(i + 1) * 8], lg[bi][:, 40:48], lg[bi][:, 48:56], op=OP.add),
                        waits=[t_c1, t_c2], inc="vS")
                    lg_free[bi] = t_cb

                    if BODY_LIM < 7:
                        E.mute = True
                    csl = i % 2
                    t_ct = E.op("t", lambda t, i=i: t.transpose(pb[6][0:8, 0:128], comb[:, i * 8:(i + 1) * 8], idf[:]),
                                waits=[t_cb, tok_smalls2, cts_free[csl]], inc="tS")
                    t_cs = E.op("s", lambda s, csl=csl: s.copy(cts[csl][0:8, :], pb[6][0:8, 0:128]), waits=[t_ct], inc="sS")
                    combT_wr[i] = E.op("s", lambda s, i=i, csl=csl: s.dma_start(
                        combT_d.ap()[:, i * 128:(i + 1) * 128], cts[csl][0:8, :]),
                        waits=[t_cs], inc=f"cbt{csl}", inc_by=16)
                    cts_free[csl] = toks(t_cs, combT_wr[i])

                E.mute = MAX_PHASE < 1
                p1_end = [("vS", E.counts["vS"]), ("sS", E.counts["sS"]), ("tS", E.counts["tS"]),
                          ("cbt0", E.counts["cbt0"]), ("cbt1", E.counts["cbt1"]), ("x0", E.counts["x0"]), ("x1", E.counts["x1"])]
                if debug:
                    E.op("y", lambda y: y.dma_start(dbg["xnT"].ap(), xnT[:]), waits=p1_end, inc="dbq", inc_by=16)
                    E.op("y", lambda y: y.dma_start(dbg["comb"].ap(), comb[:]), waits=p1_end, inc="dbq", inc_by=16)

            E.mute = MAX_PHASE < 2
            with contextlib.ExitStack() as c2:
                sb2 = lambda n, sh, dt: c2.enter_context(nc.sbuf_tensor(n, sh, dt))
                qTb = sb2("qTb", [128, T], BF16)
                kTb = sb2("kTb", [128, 2 * T], BF16)
                vbuf = sb2("vbuf", [128, KB * 129], BF16)
                wqb = [sb2(f"wqb{i}", [128, KC * 128], BF16) for i in range(2)]
                wkb = [sb2(f"wkb{i}", [128, KC * 128], BF16) for i in range(2)]
                wvb = [sb2(f"wvb{i}", [128, KC * 128], BF16) for i in range(2)]
                e1b = [sb2(f"e1b{i}", [128, 512], BF16) for i in range(2)]
                e2b = [sb2(f"e2b{i}", [128, 512], BF16) for i in range(2)]
                oqb = [sb2(f"oqb{i}", [128, 512], F32) for i in range(2)]
                o1sb = sb2("o1sb", [128, 4 * 129], F32)
                o2sb = sb2("o2sb", [128, 4 * 129], F32)
                ot1 = sb2("ot1", [128, 128], F32)
                ot2 = sb2("ot2", [128, 128], F32)
                ojk = sb2("ojk", [128, 128], F32)

                E.barrier(p1_end)
                tok_ones = E.op("v", lambda v: v.memset(vbuf[:], 1.0), waits=p1_end, inc="vS")
                tok_kz = E.op("v", lambda v: v.memset(kTb[:], 0.0), waits=p1_end, inc="vS")

                wq_tok = [None, None]
                wk_tok = [None, None]
                wv_tok = [None, None]
                wq_free = [None, None]

                def load_pair_w(p):
                    sl = p % 2
                    E.op("g", lambda g, p=p, sl=sl: g.dma_start(
                        wqb[sl][:].rearrange("p (k q) -> p k q", k=KC), wq_d.ap()[p].rearrange("k p q -> p k q")),
                        waits=[wq_free[sl]], inc=f"qkv{sl}", inc_by=16)
                    E.op("g", lambda g, p=p, sl=sl: g.dma_start(
                        wkb[sl][:].rearrange("p (k q) -> p k q", k=KC), wk_d.ap()[p].rearrange("k p q -> p k q")), inc=f"qkv{sl}", inc_by=16)
                    tok3 = E.op("g", lambda g, p=p, sl=sl: g.dma_start(
                        wvb[sl][:].rearrange("p (k q) -> p k q", k=KC), wv_d.ap()[p].rearrange("k p q -> p k q")), inc=f"qkv{sl}", inc_by=16)
                    wq_tok[sl] = tok3
                    wk_tok[sl] = tok3
                    wv_tok[sl] = tok3

                load_pair_w(0)

                pending_tp = [None]

                def emit_tp():
                    if pending_tp[0] is None:
                        return
                    ob_, mtoks, p_, qb_ = pending_tp[0]
                    pending_tp[0] = None
                    tp_sc = [None, None]
                    for qs in range(4):
                        half = qs % 2
                        t_tp = E.op("t", lambda t, ob=ob_, qs=qs, half=half: t.transpose(
                            pb[2 + half][:, 0:128],
                            oqb[ob][:, qs * 128:(qs + 1) * 128], idf[:]),
                            waits=[mtoks[qs], bankw[2 + half], tp_sc[half]], inc="tS")
                        t_sc = E.op("s", lambda s, p=p_, qb=qb_, qs=qs, half=half: s.copy(
                            oT[:, p * T + qb * 512 + qs * 128: p * T + qb * 512 + (qs + 1) * 128],
                            pb[2 + half][:, 0:128]),
                            waits=[t_tp], inc="sS")
                        tp_sc[half] = t_sc
                        oT_tok.append(t_sc)
                    bankw[2] = [tp_sc[0]]
                    bankw[3] = [tp_sc[1]]
                    oqb_free[ob_] = toks(tp_sc[0], tp_sc[1])

                qT_tok = [None] * QB
                kT_tok = [None] * QB
                v_tok = [None] * KB
                bankw = {b: list(p1_end) for b in range(8)}
                oqb_free = [None, None]
                o1sb_free = None
                o2sb_free = None
                oT_tok = []

                for p in range(PAIRS):
                    sl = p % 2
                    for tq in range(QB):
                        bank = tq % 2
                        t_mm = None
                        for kc in range(KC):
                            t_mm = E.op("t", lambda t, kc=kc, tq=tq, sl=sl, bank=bank: t.matmul(
                                pb[bank][:, 0:512], wqb[sl][:, kc * 128:(kc + 1) * 128],
                                xnT[:, kc * T + tq * 512: kc * T + (tq + 1) * 512],
                                start=(kc == 0), stop=(kc == KC - 1)),
                                waits=[wq_tok[sl], xnT_tok[tq * 4 + 3], bankw[bank]] if kc == 0 else [],
                                inc="tS" if kc == KC - 1 else None)
                        t_ev = E.op("s", lambda s, tq=tq, bank=bank: s.activation(
                            qTb[:, tq * 512:(tq + 1) * 512], pb[bank][:, 0:512], AF.Copy, scale=cc(C_EIGHTH)),
                            waits=[t_mm, tok_smalls], inc="sS")
                        qT_tok[tq] = t_ev
                        bankw[bank] = [t_ev]
                    for tq in range(QB):
                        bank = 2 + tq % 2
                        t_mm = None
                        for kc in range(KC):
                            t_mm = E.op("t", lambda t, kc=kc, tq=tq, sl=sl, bank=bank: t.matmul(
                                pb[bank][:, 0:512], wkb[sl][:, kc * 128:(kc + 1) * 128],
                                xnT[:, kc * T + tq * 512: kc * T + (tq + 1) * 512],
                                start=(kc == 0), stop=(kc == KC - 1)),
                                waits=[wk_tok[sl], xnT_tok[tq * 4 + 3], bankw[bank]] if kc == 0 else [],
                                inc="tS" if kc == KC - 1 else None)
                        t_e1 = E.op("s", lambda s, tq=tq, bank=bank: s.copy(
                            kTb[0:64, tq * 512:(tq + 1) * 512], pb[bank][0:64, 0:512]),
                            waits=[t_mm, tok_kz], inc="sS")
                        t_e2 = E.op("s", lambda s, tq=tq, bank=bank: s.copy(
                            kTb[64:128, T + tq * 512: T + (tq + 1) * 512], pb[bank][64:128, 0:512]),
                            waits=[t_e1], inc="sS")
                        kT_tok[tq] = t_e2
                        bankw[bank] = [t_e2]
                    for tv in range(KB):
                        bank = 4 + tv % 2
                        t_mm = None
                        for kc in range(KC):
                            t_mm = E.op("t", lambda t, kc=kc, tv=tv, sl=sl, bank=bank: t.matmul(
                                pb[bank][:, 0:128], xnT[:, kc * T + tv * 128: kc * T + (tv + 1) * 128],
                                wvb[sl][:, kc * 128:(kc + 1) * 128],
                                start=(kc == 0), stop=(kc == KC - 1)),
                                waits=[wv_tok[sl], tok_ones, xnT_tok[tv // 4 * 4 + 3], bankw[bank]] if kc == 0 else [],
                                inc="tS" if kc == KC - 1 else None)
                        t_ev = E.op("s", lambda s, tv=tv, bank=bank: s.copy(
                            vbuf[:, tv * 129: tv * 129 + 128], pb[bank][:, 0:128]),
                            waits=[t_mm], inc="sS")
                        v_tok[tv] = t_ev
                        bankw[bank] = [t_ev]
                    if p + 1 < PAIRS:
                        wq_free[(p + 1) % 2] = [("tS", E.counts["tS"])]
                        load_pair_w(p + 1)

                    for qb in range(QB):
                        emit_conv(2)
                        o1_evac = []
                        pv_last = None
                        for stream in (1, 2):
                            ebuf = e1b if stream == 1 else e2b
                            plo = 0 if stream == 1 else 64
                            e_free = [None, None]
                            pv_tok_prev = [None, None]
                            pend = None
                            for kb in range(KB):
                                eb = kb % 2
                                sb_ = kb % 2
                                coff = 0 if stream == 1 else T
                                t_s = E.op("t", lambda t, kb=kb, qb=qb, sb_=sb_, coff=coff: t.matmul(
                                    pb[sb_][:, 0:512], kTb[:, coff + kb * 128: coff + (kb + 1) * 128],
                                    qTb[:, qb * 512:(qb + 1) * 512],
                                    start=True, stop=True),
                                    waits=[qT_tok[qb], kT_tok[kb // 4], bankw[sb_]], inc="tS")
                                t_x = E.op("s", lambda s, eb=eb, sb_=sb_, ebuf=ebuf: s.activation(
                                    ebuf[eb][:], pb[sb_][:, 0:512], AF.Exp, bias=cc(C_NEG12)),
                                    waits=[t_s, e_free[eb]], inc="sS")
                                bankw[sb_] = [t_x]
                                if pend is not None:
                                    pkb, peb, pt_x = pend
                                    t_pv = None
                                    for qs in range(4):
                                        bnk = 4 + qs
                                        w = [pt_x, v_tok[pkb]]
                                        if pkb == 0:
                                            w.append(bankw[bnk])
                                        t_pv = E.op("t", lambda t, pkb=pkb, peb=peb, qs=qs, ebuf=ebuf, bnk=bnk: t.matmul(
                                            pb[bnk][:, 0:129], ebuf[peb][:, qs * 128:(qs + 1) * 128],
                                            vbuf[:, pkb * 129:(pkb + 1) * 129],
                                            start=(pkb == 0), stop=(pkb == KB - 1)),
                                            waits=w, inc="tS" if qs == 3 else None)
                                    e_free[peb] = t_pv
                                pend = (kb, eb, t_x)
                            pkb, peb, pt_x = pend
                            t_pv = None
                            for qs in range(4):
                                bnk = 4 + qs
                                t_pv = E.op("t", lambda t, pkb=pkb, peb=peb, qs=qs, ebuf=ebuf, bnk=bnk: t.matmul(
                                    pb[bnk][:, 0:129], ebuf[peb][:, qs * 128:(qs + 1) * 128],
                                    vbuf[:, pkb * 129:(pkb + 1) * 129],
                                    start=False, stop=True),
                                    waits=[pt_x, v_tok[pkb]], inc="tS" if qs == 3 else None)
                            e_free[peb] = t_pv
                            pv_last = t_pv
                            if stream == 2:
                                o2_evac = []
                                for qs in range(4):
                                    t_e2 = E.op("v", lambda v, qs=qs: v.tensor_copy(
                                        o2sb[:, qs * 129:(qs + 1) * 129], pb[4 + qs][:, 0:129]),
                                        waits=[pv_last, o2sb_free], inc="vS")
                                    o2_evac.append(t_e2)
                                for qs in range(4):
                                    bankw[4 + qs] = [o2_evac[qs]]
                            if stream == 1:
                                for qs in range(4):
                                    t_e1 = E.op("v", lambda v, qs=qs: v.tensor_copy(
                                        o1sb[:, qs * 129:(qs + 1) * 129], pb[4 + qs][:, 0:129]),
                                        waits=[pv_last, o1sb_free], inc="vS")
                                    o1_evac.append(t_e1)
                                for qs in range(4):
                                    bankw[4 + qs] = [o1_evac[qs]]
                                emit_tp()

                        ob = qb % 2
                        merge_toks = []
                        for qs in range(4):
                            w = [pv_last, o1_evac[qs], tok_lam] + ([oqb_free[ob]] if qs == 0 else [])
                            t_r1 = E.op("v", lambda v, qs=qs: v.reciprocal(
                                stat[:, 16:17], o1sb[:, qs * 129 + 128:qs * 129 + 129]),
                                waits=w, inc="vS")
                            t_r2 = E.op("v", lambda v, qs=qs: v.reciprocal(
                                stat[:, 17:18], o2sb[:, qs * 129 + 128:qs * 129 + 129]),
                                waits=[t_r1, o2_evac[qs]], inc="vS")
                            t_rl = E.op("v", lambda v: v.tensor_scalar(stat[:, 18:19], stat[:, 17:18], lamc[:], None, op0=OP.mult),
                                        waits=[t_r2], inc="vS")
                            t_t1 = E.op("v", lambda v, qs=qs: v.tensor_scalar(
                                ot1[:], o1sb[:, qs * 129:qs * 129 + 128], stat[:, 16:17], None, op0=OP.mult),
                                waits=[t_r1], inc="vS")
                            t_t2 = E.op("v", lambda v, qs=qs: v.tensor_scalar(
                                ot2[:], o2sb[:, qs * 129:qs * 129 + 128], stat[:, 18:19], None, op0=OP.mult),
                                waits=[t_rl], inc="vS")
                            t_om = E.op("v", lambda v, ob=ob, qs=qs: v.tensor_tensor(
                                oqb[ob][:, qs * 128:(qs + 1) * 128], ot1[:], ot2[:], op=OP.subtract),
                                waits=[t_t1, t_t2], inc="vS")
                            t_sq = E.op("v", lambda v, ob=ob, qs=qs: v.tensor_tensor(
                                ojk[:], oqb[ob][:, qs * 128:(qs + 1) * 128], oqb[ob][:, qs * 128:(qs + 1) * 128], op=OP.mult),
                                waits=[t_om], inc="vS")
                            t_ss = E.op("v", lambda v: v.reduce_sum(stat[:, 19:20], ojk[:], axis=AX.X),
                                        waits=[t_sq], inc="vS")
                            t_s1m = E.op("v", lambda v, ob=ob, qs=qs: v.reduce_sum(
                                stat[:, 20:21], oqb[ob][:, qs * 128:(qs + 1) * 128], axis=AX.X),
                                waits=[t_om], inc="vS")
                            t_nmu = E.op("v", lambda v: v.tensor_scalar(stat[:, 21:22], stat[:, 20:21], cc(C_NINV128), None, op0=OP.mult),
                                         waits=[t_s1m], inc="vS")
                            t_mm2 = E.op("v", lambda v: v.tensor_tensor(stat[:, 22:23], stat[:, 21:22], stat[:, 21:22], op=OP.mult),
                                         waits=[t_nmu], inc="vS")
                            t_va = E.op("v", lambda v: v.tensor_scalar(stat[:, 23:24], stat[:, 19:20], cc(C_INV128), None, op0=OP.mult),
                                        waits=[t_ss], inc="vS")
                            t_vb = E.op("v", lambda v: v.tensor_tensor(stat[:, 24:25], stat[:, 23:24], stat[:, 22:23], op=OP.subtract),
                                        waits=[t_va, t_mm2], inc="vS")
                            t_ln = E.op("s", lambda s: s.activation(stat[:, 25:26], stat[:, 24:25], AF.Ln, bias=cc(C_EPS)),
                                        waits=[t_vb], inc="sS")
                            t_rsd = E.op("s", lambda s: s.activation(stat[:, 26:27], stat[:, 25:26], AF.Exp, scale=cc(C_NEGHALF)),
                                         waits=[t_ln], inc="sS")
                            t_on = E.op("v", lambda v, ob=ob, qs=qs: v.tensor_scalar(
                                ot1[:], oqb[ob][:, qs * 128:(qs + 1) * 128], stat[:, 21:22], stat[:, 26:27],
                                op0=OP.add, op1=OP.mult),
                                waits=[t_rsd], inc="vS")
                            t_lg = E.op("v", lambda v: v.tensor_tensor(ot2[:], ot1[:], lngt[:], op=OP.mult),
                                        waits=[t_on, tok_smalls2], inc="vS")
                            t_fin = E.op("v", lambda v, ob=ob, qs=qs: v.tensor_tensor(
                                oqb[ob][:, qs * 128:(qs + 1) * 128], ot2[:], lnbt[:], op=OP.add),
                                waits=[t_lg], inc="vS")
                            merge_toks.append(t_fin)
                        o1sb_free = toks(*merge_toks)
                        o2sb_free = o1sb_free
                        pending_tp[0] = (ob, merge_toks, p, qb)
                    emit_tp()

                p2_end = [("tS", E.counts["tS"]), ("sS", E.counts["sS"]), ("vS", E.counts["vS"]),
                          ("qkv0", E.counts["qkv0"]), ("qkv1", E.counts["qkv1"])]
                if debug:
                    E.op("y", lambda y: y.dma_start(dbg["oT"].ap(), oT[:]), waits=p2_end, inc="dbq", inc_by=16)

            E.mute = MAX_PHASE < 3
            with contextlib.ExitStack() as c3:
                sb3 = lambda n, sh, dt: c3.enter_context(nc.sbuf_tensor(n, sh, dt))
                wob = sb3("wob", [128, KC * D], BF16)
                sst = [sb3(f"sst{i}", [128, D], BF16) for i in range(2)]

                emit_conv(10**6)
                E.barrier(p2_end)
                tok_wo = E.op("g", lambda g: g.dma_start(wob[:].rearrange("p (k d) -> p k d", k=KC), wo_d.ap().rearrange("k p d -> p k d")),
                              waits=p2_end, inc="woq", inc_by=16)
                sst_free = [None, None]
                pbw = [list(p2_end), list(p2_end)]
                for i in range(NT):
                    bi = i % 2
                    t_ev_last = None
                    for dc in range(2):
                        bank = dc
                        t_mm = None
                        for kc in range(KC):
                            t_mm = E.op("t", lambda t, kc=kc, i=i, dc=dc, bank=bank: t.matmul(
                                pb[bank][:, 0:512], oT[:, kc * T + i * 128: kc * T + (i + 1) * 128],
                                wob[:, kc * D + dc * 512: kc * D + dc * 512 + 512],
                                start=(kc == 0), stop=(kc == KC - 1)),
                                waits=[tok_wo, pbw[bank]] if kc == 0 else [],
                                inc="tS" if kc == KC - 1 else None)
                        t_ev = E.op("s", lambda s, dc=dc, bi=bi, bank=bank: s.copy(
                            sst[bi][:, dc * 512:dc * 512 + 512], pb[bank][:, 0:512]),
                            waits=[t_mm] + ([sst_free[bi]] if dc == 0 else []), inc="sS")
                        pbw[bank] = [t_ev]
                        t_ev_last = t_ev
                    s_wr[i] = E.op("s", lambda s, i=i, bi=bi: s.dma_start(
                        ssc_d.ap()[i * 128:(i + 1) * 128, :], sst[bi][:]),
                        waits=[t_ev_last], inc=f"sw{bi}", inc_by=16)
                    sst_free[bi] = s_wr[i]
                    if debug:
                        tdbg = E.op("s", lambda s, i=i, bi=bi: s.dma_start(
                            dbg["s"].ap()[i * 128:(i + 1) * 128, :], sst[bi][:]),
                            waits=[t_ev_last], inc=f"sw{bi}", inc_by=16)
                        sst_free[bi] = toks(s_wr[i], tdbg)

            p3_end = [("tS", E.counts["tS"]), ("sS", E.counts["sS"]), ("vS", E.counts["vS"]),
                      ("woq", E.counts["woq"]), ("sw0", E.counts["sw0"]), ("sw1", E.counts["sw1"]),
                      ("cin0", E.counts["cin0"]), ("cin1", E.counts["cin1"]),
                      ("cout0", E.counts["cout0"]), ("cout1", E.counts["cout1"]),
                      ("sm2", E.counts["sm2"]), ("dbq", E.counts["dbq"])]

        E.mute = MAX_PHASE < 4
        with contextlib.ExitStack() as c4:
            sb4 = lambda n, sh, dt: c4.enter_context(nc.sbuf_tensor(n, sh, dt))
            hbuf = sb4("hbuf", [128, HC * 512], BF16)
            w1g = [sb4(f"w1g{i}", [128, 4096], BF16) for i in range(2)]
            w2g = [sb4(f"w2g{i}", [128, 4096], BF16) for i in range(2)]
            gacc = sb4("gacc", [128, 4 * D], F32)
            stb = sb4("stb", [128, 4 * D], BF16)
            xtb = sb4("xtb", [128, 4 * D], F32)
            ostg = [sb4(f"ostg{i}", [128, D], F32) for i in range(2)]
            ytmp = [sb4(f"ytmp{i}", [128, 512], F32) for i in range(2)]
            gtmp = sb4("gtmp", [128, D], F32)
            cts4 = sb4("cts4", [8, T], BF16)

            x_tb_ap = x_d.ap().rearrange("(tb ts p) d -> tb p ts d", ts=4, p=128)
            s_tb_ap = ssc_d.ap().rearrange("(tb ts p) d -> tb p ts d", ts=4, p=128)
            xtb_v = xtb[:].rearrange("p (ts d) -> p ts d", ts=4)
            stb_v = stb[:].rearrange("p (ts d) -> p ts d", ts=4)

            E.barrier(p3_end)
            tok_ct4 = E.op("y", lambda y: y.dma_start(cts4[:], combT_d.ap()),
                           waits=[toks(*combT_wr), p3_end], inc="ctq", inc_by=16)

            w1_tok = [None] * 2
            w2_tok = [None] * 2
            w1_use = [list(p3_end) for _ in range(2)]
            w2_use = [list(p3_end) for _ in range(2)]
            pbw4 = {i: list(p3_end) for i in range(8)}

            def dma_w1(e, g):
                sl = g % 2
                w1_tok[sl] = E.op("y", lambda y, e=e, g=g, sl=sl: y.dma_start(
                    w1g[sl][:], wsc1_d.ap()[e, g]),
                    waits=[w1_use[sl], conv_done_w1[e]], inc=f"w1s{sl}", inc_by=16)

            def dma_w2(e, g):
                sl = g % 2
                w2_tok[sl] = E.op("y", lambda y, e=e, g=g, sl=sl: y.dma_start(
                    w2g[sl][:], wsc2_d.ap()[e, g]),
                    waits=[w2_use[sl], conv_done_w2[e]], inc=f"w2s{sl}", inc_by=16)

            gacc_free = None
            stb_free = None
            xtb_free = None
            ostg_free = [None, None]
            ytmp_free = [None, None]
            gtmp_free = None
            out_tok = []

            for tb in range(NTB):
                xtb_tok = E.op("g", lambda g, tb=tb: g.dma_start(xtb_v, x_tb_ap[tb]),
                               waits=[xtb_free], inc="xtq", inc_by=16)
                stb_tok = E.op("y", lambda y, tb=tb: y.dma_start(stb_v, s_tb_ap[tb]),
                               waits=[stb_free] + [s_wr[tb * 4 + j] for j in range(4)], inc="stq", inc_by=16)

                ginit = []
                for ts4 in range(4):
                    for dc in range(2):
                        bank = 6 + (ts4 * 2 + dc) % 2
                        t_mm = E.op("t", lambda t, tb=tb, ts4=ts4, dc=dc, bank=bank: t.matmul(
                            pb[bank][:, 0:512], cts4[0:8, tb * 512 + ts4 * 128: tb * 512 + (ts4 + 1) * 128],
                            b2s[0:8, dc * 512:dc * 512 + 512], start=True, stop=True),
                            waits=[tok_ct4, tok_smalls, pbw4[bank]], inc="tS")
                        t_cp = E.op("s", lambda s, ts4=ts4, dc=dc, bank=bank: s.copy(
                            gacc[:, ts4 * D + dc * 512: ts4 * D + dc * 512 + 512], pb[bank][:, 0:512]),
                            waits=[t_mm] + ([gacc_free] if (ts4 == 0 and dc == 0) else []), inc="sS")
                        pbw4[bank] = [t_cp]
                        ginit.append(t_cp)
                gacc_ready = toks(*ginit)

                for e in range(NE):
                    if e == 0 and tb == 0:
                        for g0 in range(2):
                            dma_w1(e, g0)
                        for g0 in range(2):
                            dma_w2(e, g0)

                    gelu_toks = []
                    for hc in range(HC):
                        g0 = hc // 4
                        sl = g0 % 2
                        bank = hc % 2
                        t_mm = None
                        for kc in range(KC):
                            t_mm = E.op("t", lambda t, hc=hc, kc=kc, tb=tb, sl=sl, bank=bank: t.matmul(
                                pb[bank][:, 0:512],
                                w1g[sl][:, (hc % 4) * 1024 + kc * 128: (hc % 4) * 1024 + (kc + 1) * 128],
                                xnT[:, kc * T + tb * 512: kc * T + (tb + 1) * 512],
                                start=(kc == 0), stop=(kc == KC - 1)),
                                waits=[w1_tok[sl], xnT_tok[NT - 1], pbw4[bank]] if kc == 0 else [],
                                inc="tS" if kc == KC - 1 else None)
                        if hc % 4 == 3:
                            w1_use[sl] = [t_mm]
                            gnext = g0 + 2
                            if gnext < NG:
                                dma_w1(e, gnext)
                            elif e + 1 < NE:
                                dma_w1(e + 1, gnext - NG)
                            elif tb + 1 < NTB:
                                dma_w1(0, gnext - NG)
                        t_ge = E.op("s", lambda s, hc=hc, e=e, bank=bank: s.activation(
                            hbuf[:, hc * 512:(hc + 1) * 512], pb[bank][:, 0:512], AF.Gelu,
                            bias=b1s[:, e * HC + hc: e * HC + hc + 1]),
                            waits=[t_mm, tok_smalls], inc="sS")
                        pbw4[bank] = [t_ge]
                        gelu_toks.append(t_ge)

                    t_mm2_last = {}
                    for hc in range(HC):
                        g0 = hc // 4
                        sl = g0 % 2
                        t_mm = None
                        for ts4 in range(4):
                            for dc in range(2):
                                bank = ts4 * 2 + dc
                                w = [gelu_toks[hc], w2_tok[sl]]
                                if hc == 0:
                                    w.append(pbw4[bank])
                                    w.append(gacc_ready)
                                give_inc = (hc == HC - 1) or (hc % 4 == 3 and ts4 == 3 and dc == 1)
                                t_mm = E.op("t", lambda t, hc=hc, ts4=ts4, dc=dc, sl=sl, bank=bank: t.matmul(
                                    pb[bank][:, 0:512],
                                    hbuf[:, hc * 512 + ts4 * 128: hc * 512 + (ts4 + 1) * 128],
                                    w2g[sl][:, (hc % 4) * 1024 + dc * 512: (hc % 4) * 1024 + dc * 512 + 512],
                                    start=(hc == 0), stop=(hc == HC - 1)),
                                    waits=w, inc="tS" if give_inc else None)
                                if hc == HC - 1:
                                    t_mm2_last[bank] = t_mm
                        if hc % 4 == 3:
                            w2_use[sl] = [t_mm]
                            gnext = g0 + 2
                            if gnext < NG:
                                dma_w2(e, gnext)
                            elif e + 1 < NE:
                                dma_w2(e + 1, gnext - NG)
                            elif tb + 1 < NTB:
                                dma_w2(0, gnext - NG)

                    new_ready = list(gacc_ready)
                    for bank in range(8):
                        ts4, dc = bank // 2, bank % 2
                        yb = bank % 2
                        ci = (tb * 4 + ts4) * 8 + e
                        t_sc = E.op("s", lambda s, bank=bank, ci=ci, yb=yb: s.activation(
                            ytmp[yb][:], pb[bank][:, 0:512], AF.Copy, scale=comb[:, ci:ci + 1]),
                            waits=[t_mm2_last[bank], ytmp_free[yb]], inc="sS")
                        t_ad = E.op("v", lambda v, ts4=ts4, dc=dc, yb=yb: v.tensor_tensor(
                            gacc[:, ts4 * D + dc * 512: ts4 * D + dc * 512 + 512],
                            gacc[:, ts4 * D + dc * 512: ts4 * D + dc * 512 + 512],
                            ytmp[yb][:], op=OP.add),
                            waits=[t_sc, gacc_ready], inc="vS")
                        ytmp_free[yb] = t_ad
                        pbw4[bank] = toks(t_sc, t_ad)
                        new_ready.append(t_ad)
                    gacc_ready = toks(*new_ready)

                tail_v = []
                t_si_last = None
                for ts4 in range(4):
                    ob = ts4 % 2
                    t_si = E.op("s", lambda s, ts4=ts4: s.activation(
                        gtmp[:], gacc[:, ts4 * D:(ts4 + 1) * D], AF.Silu),
                        waits=[gacc_ready, gtmp_free], inc="sS")
                    t_mu = E.op("v", lambda v, ts4=ts4: v.tensor_tensor(
                        gtmp[:], gtmp[:], stb[:, ts4 * D:(ts4 + 1) * D], op=OP.mult),
                        waits=[t_si, stb_tok], inc="vS")
                    t_ou = E.op("v", lambda v, ts4=ts4, ob=ob: v.tensor_tensor(
                        ostg[ob][:], gtmp[:], xtb[:, ts4 * D:(ts4 + 1) * D], op=OP.add),
                        waits=[t_mu, xtb_tok, ostg_free[ob]], inc="vS")
                    t_wr = E.op("g", lambda g, tb=tb, ts4=ts4, ob=ob: g.dma_start(
                        out_d.ap()[(tb * 4 + ts4) * 128:(tb * 4 + ts4 + 1) * 128, :], ostg[ob][:]),
                        waits=[t_ou], inc=f"ow{ob}", inc_by=16)
                    ostg_free[ob] = t_wr
                    gtmp_free = t_ou
                    t_si_last = t_si
                    tail_v.append(t_ou)
                    out_tok.append(t_wr)
                gacc_free = toks(t_si_last)
                xtb_free = toks(*tail_v)
                stb_free = toks(*tail_v)

        E.mute = False
        fin = toks(*out_tok) + [("dbq", E.counts["dbq"]), ("sw0", E.counts["sw0"]),
                                ("sw1", E.counts["sw1"]), ("cbt0", E.counts["cbt0"]),
                                ("cbt1", E.counts["cbt1"])]
        for sem, val in toks(*[t for t in fin if t[1] > 0]):
            h = E.sems[sem]
            E.ops["y"].append(lambda e, h=h, v=val: e.wait_ge(h, v))

        block = ctx.enter_context(nc.Block())
        E.emit(block)

    return nc


def _prep_core_inputs(x_core, inp):
    W1 = inp["L0_ffn_W1"]; W2 = inp["L0_ffn_W2"]
    w1t = np.ascontiguousarray(
        W1.reshape(NE, KC, 128, HC, 128)
          .transpose(0, 3, 2, 1, 4)
          .reshape(NE, NG, 4, 128, KC, 128)
          .transpose(0, 1, 3, 2, 4, 5)
          .reshape(NE, NG, 128, 4096))
    w2t = np.ascontiguousarray(
        W2.reshape(NE, NG, 4, 128, D)
          .transpose(0, 1, 3, 2, 4)
          .reshape(NE, NG, 128, 4096))
    def qk_t(W):
        return np.ascontiguousarray(W.reshape(KC, 128, PAIRS, 128).transpose(2, 0, 1, 3))
    consts = np.zeros((128, NCONST), np.float32)
    consts[:, C_NEG12] = -12.0
    consts[:, C_EIGHTH] = 0.125
    consts[:, C_NINV1024] = -1.0 / 1024
    consts[:, C_INV1024] = 1.0 / 1024
    consts[:, C_NINV128] = -1.0 / 128
    consts[:, C_INV128] = 1.0 / 128
    consts[:, C_EPS] = 1e-5
    consts[:, C_NEGHALF] = -0.5
    consts[:, C_ONE] = 1.0
    consts[:, C_BIG] = 1e30
    consts[:, C_LAMI] = LAM_INIT
    lrow = np.concatenate([inp["L0_attn_lq1"], inp["L0_attn_lk1"],
                           inp["L0_attn_lq2"], inp["L0_attn_lk2"]]).reshape(1, 256).astype(np.float32)
    return dict(
        x=np.ascontiguousarray(x_core),
        wq_t=qk_t(inp["L0_attn_Wq"]),
        wk_t=qk_t(inp["L0_attn_Wk"]),
        wv_t=qk_t(inp["L0_attn_Wv"]),
        wo_t=np.ascontiguousarray(inp["L0_attn_Wo"].reshape(KC, 128, D)),
        wr_t=np.ascontiguousarray(inp["L0_ffn_Wr"].reshape(KC, 128, NE)),
        w1_t=w1t, w2_t=w2t,
        b1_t=np.ascontiguousarray(inp["L0_ffn_b1"].reshape(NE, HC, 128).transpose(2, 0, 1).reshape(128, NE * HC)),
        b2_t=np.ascontiguousarray(inp["L0_ffn_b2"]),
        lng_t=np.ascontiguousarray(np.tile(inp["L0_attn_ln_g"].reshape(1, 128), (128, 1))),
        lnb_t=np.ascontiguousarray(np.tile(inp["L0_attn_ln_b"].reshape(1, 128), (128, 1))),
        lrow=lrow,
        consts=consts,
        idf=np.eye(128, dtype=np.float32),
        onesr=np.ones((1, 128), np.float32),
    )


def kernel(**inputs):
    x = np.asarray(inputs["L0_attn_x"], np.float32)
    inp = {k: np.asarray(v, np.float32) for k, v in inputs.items()}
    nc = build_program(debug=DEBUG)
    in_maps = [_prep_core_inputs(x[c], inp) for c in range(N_CORES)]
    res = run_bass_kernel_spmd(nc, in_maps, core_ids=list(range(N_CORES)))
    out = np.stack([np.asarray(res.results[c]["out"], np.float32) for c in range(N_CORES)])
    return out


# version 06
# speedup vs baseline: 1.0675x; 1.0675x over previous
import contextlib
import numpy as np
import ml_dtypes

import concourse.bass as bass
import concourse.mybir as mybir
from concourse.bass_utils import run_bass_kernel_spmd

F32 = mybir.dt.float32
BF16 = mybir.dt.bfloat16
AF = mybir.ActivationFunctionType
OP = mybir.AluOpType
AX = mybir.AxisListType
bf16 = ml_dtypes.bfloat16

T = 4096
D = 1024
NT = 32
KC = 8
PAIRS = 8
QB = 8
KB = 32
NE = 8
HC = 32
NG = 8
NTB = 8
LAM_INIT = float(0.8 - 0.6 * np.exp(-0.3 * 12))

(C_NEG12, C_EIGHTH, C_NINV1024, C_INV1024, C_NINV128, C_INV128, C_EPS,
 C_NEGHALF, C_ONE, C_BIG, C_LAMI) = range(11)
NCONST = 16

DEBUG = False
MAX_PHASE = 4
NT_LIM = NT
BODY_LIM = 7
N_CORES = 8


def _flat(waits):
    out = []
    for w in waits:
        if w is None:
            continue
        if isinstance(w, list):
            out.extend(t for t in w if t is not None)
        else:
            out.append(w)
    return out


def toks(*ws):
    best = {}
    for t in _flat(ws):
        s, v = t
        if best.get(s, -1) < v:
            best[s] = v
    return [(s, v) for s, v in best.items()]


class Emitter:
    ENGS = ("t", "v", "s", "g", "y")

    def __init__(self):
        self.mute = False
        self.ops = {e: [] for e in self.ENGS}
        self.counts = {}
        self.watermark = {}
        self.sems = {}

    def add_sem(self, name, handle):
        self.sems[name] = handle
        self.counts[name] = 0

    def op(self, eng, fn, waits=(), inc=None, inc_by=1):
        if self.mute:
            return (inc, self.counts[inc]) if inc else None
        for sem, val in toks(*waits):
            assert val <= self.counts[sem], f"wait on future: {sem} {val} > {self.counts[sem]}"
            key = (eng, sem)
            if self.watermark.get(key, 0) >= val:
                continue
            self.watermark[key] = val
            h = self.sems[sem]
            self.ops[eng].append(lambda e, h=h, v=val: e.wait_ge(h, v))
        if inc is None:
            self.ops[eng].append(fn)
            return None
        h = self.sems[inc]
        def wrapped(e, fn=fn, h=h, n=inc_by):
            fn(e).then_inc(h, n)
        self.ops[eng].append(wrapped)
        self.counts[inc] += inc_by
        return (inc, self.counts[inc])

    def barrier(self, waits):
        if self.mute:
            return
        for eng in self.ENGS:
            for sem, val in toks(*waits):
                key = (eng, sem)
                if self.watermark.get(key, 0) >= val:
                    continue
                self.watermark[key] = val
                h = self.sems[sem]
                self.ops[eng].append(lambda e, h=h, v=val: e.wait_ge(h, v))

    def emit(self, block):
        ops = self.ops
        @block.tensor
        def _(t):
            for f in ops["t"]: f(t)
        @block.vector
        def _(v):
            for f in ops["v"]: f(v)
        @block.scalar
        def _(s):
            for f in ops["s"]: f(s)
        @block.gpsimd
        def _(g):
            for f in ops["g"]: f(g)
        @block.sync
        def _(y):
            for f in ops["y"]: f(y)


def build_program(debug=False):
    nc = bass.Bass("TRN2", target_bir_lowering=False)

    x_d = nc.dram_tensor("x", [T, D], F32, kind="ExternalInput")
    wq_d = nc.dram_tensor("wq_t", [PAIRS, KC, 128, 128], F32, kind="ExternalInput")
    wk_d = nc.dram_tensor("wk_t", [PAIRS, KC, 128, 128], F32, kind="ExternalInput")
    wv_d = nc.dram_tensor("wv_t", [PAIRS, KC, 128, 128], F32, kind="ExternalInput")
    wo_d = nc.dram_tensor("wo_t", [KC, 128, D], F32, kind="ExternalInput")
    wr_d = nc.dram_tensor("wr_t", [KC, 128, NE], F32, kind="ExternalInput")
    w1_d = nc.dram_tensor("w1_t", [NE, NG, 128, 4096], F32, kind="ExternalInput")
    w2_d = nc.dram_tensor("w2_t", [NE, NG, 128, 4096], F32, kind="ExternalInput")
    b1_d = nc.dram_tensor("b1_t", [128, NE * HC], F32, kind="ExternalInput")
    b2_d = nc.dram_tensor("b2_t", [NE, D], F32, kind="ExternalInput")
    lng_d = nc.dram_tensor("lng_t", [128, 128], F32, kind="ExternalInput")
    lnb_d = nc.dram_tensor("lnb_t", [128, 128], F32, kind="ExternalInput")
    lrow_d = nc.dram_tensor("lrow", [1, 256], F32, kind="ExternalInput")
    consts_d = nc.dram_tensor("consts", [128, NCONST], F32, kind="ExternalInput")
    idf_d = nc.dram_tensor("idf", [128, 128], F32, kind="ExternalInput")
    onesr_d = nc.dram_tensor("onesr", [1, 128], F32, kind="ExternalInput")

    out_d = nc.dram_tensor("out", [T, D], F32, kind="ExternalOutput")

    wsc1_d = nc.dram_tensor("wsc1", [NE, NG, 128, 4096], BF16, kind="Internal")
    wsc2_d = nc.dram_tensor("wsc2", [NE, NG, 128, 4096], BF16, kind="Internal")
    ssc_d = nc.dram_tensor("ssc", [T, D], BF16, kind="Internal")
    combT_d = nc.dram_tensor("combT_sc", [8, T], BF16, kind="Internal")

    dbg = {}
    if debug:
        dbg["xnT"] = nc.dram_tensor("dbg_xnT", [128, KC * T], BF16, kind="ExternalOutput")
        dbg["comb"] = nc.dram_tensor("dbg_comb", [128, NT * 8], F32, kind="ExternalOutput")
        dbg["lam"] = nc.dram_tensor("dbg_lam", [128, 1], F32, kind="ExternalOutput")
        dbg["s"] = nc.dram_tensor("dbg_s", [T, D], BF16, kind="ExternalOutput")
        dbg["oT"] = nc.dram_tensor("dbg_oT", [128, PAIRS * T], BF16, kind="ExternalOutput")

    E = Emitter()

    with contextlib.ExitStack() as ctx:
        sbuf = lambda n, sh, dt: ctx.enter_context(nc.sbuf_tensor(n, sh, dt))
        psum = lambda n, sh, dt: ctx.enter_context(nc.psum_tensor(n, sh, dt))
        _sems = ["tS", "sS", "vS", "sm1", "sm2", "cbt0", "cbt1", "woq", "stq", "xtq", "ctq", "dbq",
                 "cin0", "cin1", "cout0", "cout1", "x0", "x1", "qkv0", "qkv1",
                 "sw0", "sw1", "ow0", "ow1",
                 "w1s0", "w1s1", "w1s2", "w1s3", "w2s0", "w2s1", "w2s2", "w2s3"]
        for sname in _sems:
            E.add_sem(sname, ctx.enter_context(nc.semaphore(sname)))

        pb = [psum(f"pb{i}", [128, 512], F32) for i in range(8)]

        xnT = sbuf("xnT", [128, KC * T], BF16)
        comb = sbuf("comb", [128, NT * 8], F32)
        b1s = sbuf("b1s", [128, NE * HC], F32)
        b2s = sbuf("b2s", [NE, D], BF16)
        wrs = sbuf("wrs", [128, KC * NE], F32)
        cst = sbuf("cst", [128, NCONST], F32)
        lamc = sbuf("lamc", [128, 1], F32)
        lrow = sbuf("lrow_s", [1, 256], F32)
        onesr = sbuf("onesr_s", [1, 128], F32)
        lprod = sbuf("lprod", [1, 128], F32)
        l11 = sbuf("l11", [1, 16], F32)
        stat = sbuf("stat", [128, 64], F32)

        cc = lambda i: cst[:, i:i + 1]

        for dst, src in ((b1s, b1_d), (cst, consts_d), (lrow, lrow_d), (onesr, onesr_d)):
            E.op("g", lambda g, d=dst, s=src: g.dma_start(d[:], s.ap()), inc="sm1", inc_by=16)
        E.op("g", lambda g: g.dma_start(wrs[:].rearrange("p (k e) -> p k e", k=KC), wr_d.ap().rearrange("k p e -> p k e")), inc="sm1", inc_by=16)
        E.op("g", lambda g: g.dma_start(b2s[:], b2_d.ap()), inc="sm1", inc_by=16)
        tok_smalls = ("sm1", E.counts["sm1"])

        t1 = E.op("v", lambda v: v.tensor_tensor(lprod[0:1, 0:64], lrow[0:1, 0:64], lrow[0:1, 64:128], op=OP.mult),
                  waits=[tok_smalls], inc="vS")
        t2 = E.op("v", lambda v: v.tensor_tensor(lprod[0:1, 64:128], lrow[0:1, 128:192], lrow[0:1, 192:256], op=OP.mult),
                  waits=[t1], inc="vS")
        t3 = E.op("v", lambda v: v.reduce_sum(l11[0:1, 0:1], lprod[0:1, 0:64], axis=AX.X), waits=[t2], inc="vS")
        t4 = E.op("v", lambda v: v.reduce_sum(l11[0:1, 1:2], lprod[0:1, 64:128], axis=AX.X), waits=[t3], inc="vS")
        t5 = E.op("s", lambda s: s.activation(l11[0:1, 2:3], l11[0:1, 0:1], AF.Exp), waits=[t3, tok_smalls], inc="sS")
        t6 = E.op("s", lambda s: s.activation(l11[0:1, 3:4], l11[0:1, 1:2], AF.Exp), waits=[t4, t5], inc="sS")
        t7 = E.op("v", lambda v: v.tensor_tensor(l11[0:1, 4:5], l11[0:1, 2:3], l11[0:1, 3:4], op=OP.subtract),
                  waits=[t6], inc="vS")
        t8 = E.op("v", lambda v: v.tensor_scalar(l11[0:1, 5:6], l11[0:1, 4:5], cst[0:1, C_LAMI:C_LAMI + 1], None, op0=OP.add),
                  waits=[t7], inc="vS")
        t9 = E.op("t", lambda t: t.matmul(pb[7][:, 0:1], onesr[0:1, :], l11[0:1, 5:6], start=True, stop=True),
                  waits=[t8, tok_smalls], inc="tS")
        tok_lam = E.op("v", lambda v: v.tensor_copy(lamc[:], pb[7][:, 0:1]), waits=[t9], inc="vS")
        if debug:
            E.op("y", lambda y: y.dma_start(dbg["lam"].ap(), lamc[:]), waits=[tok_lam], inc="dbq", inc_by=16)

        E.mute = MAX_PHASE < 1
        with contextlib.ExitStack() as c23:
            sb23 = lambda n, sh, dt: c23.enter_context(nc.sbuf_tensor(n, sh, dt))
            oT = sb23("oT", [128, PAIRS * T], BF16)
            lngt = sb23("lngt", [128, 128], F32)
            lnbt = sb23("lnbt", [128, 128], F32)
            idf = sb23("idf_s", [128, 128], F32)
            convb = [sb23(f"convb{i}", [128, 2048], BF16) for i in range(2)]
            for dst_, src_ in ((lngt, lng_d), (lnbt, lnb_d), (idf, idf_d)):
                E.op("g", lambda g, d=dst_, s=src_: g.dma_start(d[:], s.ap()), inc="sm2", inc_by=16)
            tok_smalls2 = ("sm2", E.counts["sm2"])

            conv_done_w1 = [None] * NE
            conv_done_w2 = [None] * NE
            cvt_slot = [None, None]
            conv_steps = []
            for ei in range(NE):
                for which in range(2):
                    for gi in range(NG):
                        for hf in range(2):
                            conv_steps.append((which, ei, gi, hf))
            conv_idx = [0]

            def emit_conv(n):
                saved = E.mute
                E.mute = MAX_PHASE < 4
                for _ in range(n):
                    if conv_idx[0] >= len(conv_steps):
                        break
                    which, e_, gi, hf = conv_steps[conv_idx[0]]
                    sl = conv_idx[0] % 2
                    conv_idx[0] += 1
                    src_, dstd = ((w1_d, wsc1_d), (w2_d, wsc2_d))[which]
                    tin = E.op("g", lambda g, s=src_, e=e_, gg=gi, sl=sl, hf=hf:
                               g.dma_start(convb[sl][:], s.ap()[e, gg][:, hf * 2048:(hf + 1) * 2048]),
                               waits=[cvt_slot[sl]], inc=f"cin{sl}", inc_by=16)
                    tout = E.op("y", lambda y, d=dstd, e=e_, gg=gi, sl=sl, hf=hf:
                                y.dma_start(d.ap()[e, gg][:, hf * 2048:(hf + 1) * 2048], convb[sl][:]),
                                waits=[tin], inc=f"cout{sl}", inc_by=16)
                    cvt_slot[sl] = tout
                    if gi == NG - 1 and hf == 1:
                        done = toks(cvt_slot[0], cvt_slot[1])
                        if which == 0:
                            conv_done_w1[e_] = done
                        else:
                            conv_done_w2[e_] = done
                E.mute = saved

            E.mute = MAX_PHASE < 1
            xnT_tok = [None] * NT
            combT_wr = [None] * NT
            s_wr = [None] * NT
            with contextlib.ExitStack() as c1:
                sb1 = lambda n, sh, dt: c1.enter_context(nc.sbuf_tensor(n, sh, dt))
                xbuf = [sb1(f"xbuf{i}", [128, D], F32) for i in range(2)]
                xnf = [sb1(f"xnf{i}", [128, D], F32) for i in range(2)]
                xtr = [sb1(f"xtr{i}", [128, 128], F32) for i in range(2)]
                xsq = [sb1(f"xsq{i}", [128, D], F32) for i in range(2)]
                lg = [sb1(f"lg{i}", [128, 64], F32) for i in range(2)]
                cts = [sb1(f"cts{i}", [8, 128], BF16) for i in range(2)]

                x_tiles = x_d.ap().rearrange("(n p) d -> n p d", p=128)
                xbuf_free = [None, None]
                xnf_free = [None, None]
                xtr_free = [None, None]
                tp_free = [[tok_lam], None]
                sq_free = [None, None]
                lg_free = [None, None]
                cts_free = [None, None]
                logit_psum_free = None
                prev_sq = [None, None]

                pending_ct = [None]

                def flush_ct():
                    if pending_ct[0] is not None:
                        f = pending_ct[0]
                        pending_ct[0] = None
                        f()

                for i in range(NT):
                    E.mute = (i >= NT_LIM) or (MAX_PHASE < 1)
                    emit_conv(4)
                    flush_ct()
                    bi = i % 2
                    B0 = 32 * bi
                    tok_x = E.op("g", lambda g, i=i, bi=bi: g.dma_start(xbuf[bi][:], x_tiles[i]),
                                 waits=[xbuf_free[bi]], inc=f"x{bi}", inc_by=16)
                    t_s2 = E.op("s", lambda s, bi=bi: s.activation(xsq[bi][:], xbuf[bi][:], AF.Square,
                                                                   accum_out=stat[:, B0 + 0:B0 + 1]),
                                waits=[tok_x, tok_smalls, sq_free[bi], prev_sq[bi]], inc="sS")
                    prev_sq[bi] = t_s2
                    t_s1 = E.op("v", lambda v, bi=bi: v.reduce_sum(stat[:, B0 + 1:B0 + 2], xbuf[bi][:], axis=AX.X),
                                waits=[tok_x], inc="vS")
                    t_nmu = E.op("v", lambda v: v.tensor_scalar(stat[:, B0 + 2:B0 + 3], stat[:, B0 + 1:B0 + 2], cc(C_NINV1024), None, op0=OP.mult),
                                 waits=[t_s1], inc="vS")
                    t_m2 = E.op("v", lambda v: v.tensor_tensor(stat[:, B0 + 3:B0 + 4], stat[:, B0 + 2:B0 + 3], stat[:, B0 + 2:B0 + 3], op=OP.mult),
                                waits=[t_nmu], inc="vS")
                    t_va = E.op("v", lambda v: v.tensor_scalar(stat[:, B0 + 4:B0 + 5], stat[:, B0 + 0:B0 + 1], cc(C_INV1024), None, op0=OP.mult),
                                waits=[t_s2, t_m2], inc="vS")
                    sq_free[bi] = t_va
                    t_vb = E.op("v", lambda v: v.tensor_tensor(stat[:, B0 + 5:B0 + 6], stat[:, B0 + 4:B0 + 5], stat[:, B0 + 3:B0 + 4], op=OP.subtract),
                                waits=[t_va], inc="vS")
                    t_ln = E.op("s", lambda s: s.activation(stat[:, B0 + 6:B0 + 7], stat[:, B0 + 5:B0 + 6], AF.Ln, bias=cc(C_EPS)),
                                waits=[t_vb], inc="sS")
                    t_rs = E.op("s", lambda s: s.activation(stat[:, B0 + 7:B0 + 8], stat[:, B0 + 6:B0 + 7], AF.Exp, scale=cc(C_NEGHALF)),
                                waits=[t_ln], inc="sS")
                    t_xn = E.op("v", lambda v, bi=bi: v.tensor_scalar(xnf[bi][:], xbuf[bi][:], stat[:, B0 + 2:B0 + 3],
                                                                      stat[:, B0 + 7:B0 + 8], op0=OP.add, op1=OP.mult),
                                waits=[t_rs, xnf_free[bi]], inc="vS")
                    xbuf_free[bi] = t_xn

                    if BODY_LIM < 2:
                        E.mute = True
                    last_sc = None
                    last_vc = None
                    t_mm = None
                    for kc in range(KC):
                        tb_ = kc % 2
                        pbank = pb[7] if tb_ == 0 else pb[4]
                        t_tp = E.op("t", lambda t, bi=bi, kc=kc, pk=pbank: t.transpose(
                            pk[:, 0:128], xnf[bi][:, kc * 128:(kc + 1) * 128], idf[:]),
                            waits=[t_xn, tok_smalls, tok_smalls2, tp_free[tb_]], inc="tS")
                        if BODY_LIM < 3:
                            E.mute = True
                        t_sc = E.op("s", lambda s, i=i, kc=kc, pk=pbank: s.copy(
                            xnT[:, kc * T + i * 128: kc * T + (i + 1) * 128], pk[:, 0:128]),
                            waits=[t_tp], inc="sS")
                        if BODY_LIM >= 2 and not (i >= NT_LIM or MAX_PHASE < 1):
                            E.mute = False
                        if BODY_LIM < 4:
                            E.mute = True
                        t_vc = E.op("s", lambda s, kc=kc, pk=pbank: s.copy(xtr[kc % 2][:], pk[:, 0:128]),
                                    waits=[t_tp, xtr_free[kc % 2]], inc="sS")
                        if BODY_LIM >= 2 and not (i >= NT_LIM or MAX_PHASE < 1):
                            E.mute = False
                        tp_free[tb_] = toks(t_sc, t_vc)
                        if BODY_LIM < 5:
                            E.mute = True
                        t_mm = E.op("t", lambda t, kc=kc: t.matmul(
                            pb[5][:, 0:8], xtr[kc % 2][:], wrs[:, kc * NE:(kc + 1) * NE],
                            start=(kc == 0), stop=(kc == KC - 1)),
                            waits=[t_vc, tok_smalls] + ([logit_psum_free] if kc == 0 else []), inc="tS")
                        if BODY_LIM >= 2 and not (i >= NT_LIM or MAX_PHASE < 1):
                            E.mute = False
                        xtr_free[kc % 2] = t_mm
                        last_sc, last_vc = t_sc, t_vc
                    xnT_tok[i] = last_sc
                    xnf_free[bi] = toks(last_sc, last_vc)

                    if BODY_LIM < 6:
                        E.mute = True
                    t_l = E.op("v", lambda v: v.tensor_copy(lg[bi][:, 0:8], pb[5][:, 0:8]),
                               waits=[t_mm, lg_free[bi]], inc="vS")
                    logit_psum_free = t_l
                    t_m1 = E.op("v", lambda v: v.reduce_max(stat[:, B0 + 8:B0 + 9], lg[bi][:, 0:8], axis=AX.X), waits=[t_l], inc="vS")
                    t_k1 = E.op("v", lambda v: v.tensor_scalar(lg[bi][:, 8:16], lg[bi][:, 0:8], stat[:, B0 + 8:B0 + 9], None, op0=OP.is_ge),
                                waits=[t_m1], inc="vS")
                    t_pen = E.op("v", lambda v: v.tensor_scalar(lg[bi][:, 16:24], lg[bi][:, 8:16], cc(C_BIG), None, op0=OP.mult),
                                 waits=[t_k1], inc="vS")
                    t_msk = E.op("v", lambda v: v.tensor_tensor(lg[bi][:, 24:32], lg[bi][:, 0:8], lg[bi][:, 16:24], op=OP.subtract),
                                 waits=[t_pen], inc="vS")
                    t_m2r = E.op("v", lambda v: v.reduce_max(stat[:, B0 + 9:B0 + 10], lg[bi][:, 24:32], axis=AX.X), waits=[t_msk], inc="vS")
                    t_k2 = E.op("v", lambda v: v.tensor_scalar(lg[bi][:, 32:40], lg[bi][:, 24:32], stat[:, B0 + 9:B0 + 10], None, op0=OP.is_ge),
                                waits=[t_m2r], inc="vS")
                    t_d = E.op("v", lambda v: v.tensor_tensor(stat[:, B0 + 10:B0 + 11], stat[:, B0 + 9:B0 + 10], stat[:, B0 + 8:B0 + 9], op=OP.subtract),
                               waits=[t_m2r], inc="vS")
                    t_e = E.op("s", lambda s: s.activation(stat[:, B0 + 11:B0 + 12], stat[:, B0 + 10:B0 + 11], AF.Exp), waits=[t_d], inc="sS")
                    t_e1 = E.op("v", lambda v: v.tensor_scalar(stat[:, B0 + 12:B0 + 13], stat[:, B0 + 11:B0 + 12], cc(C_ONE), None, op0=OP.add),
                                waits=[t_e], inc="vS")
                    t_p1 = E.op("v", lambda v: v.reciprocal(stat[:, B0 + 13:B0 + 14], stat[:, B0 + 12:B0 + 13]), waits=[t_e1], inc="vS")
                    t_p2 = E.op("v", lambda v: v.tensor_tensor(stat[:, B0 + 14:B0 + 15], stat[:, B0 + 11:B0 + 12], stat[:, B0 + 13:B0 + 14], op=OP.mult),
                                waits=[t_p1], inc="vS")
                    t_c1 = E.op("v", lambda v: v.tensor_scalar(lg[bi][:, 40:48], lg[bi][:, 8:16], stat[:, B0 + 13:B0 + 14], None, op0=OP.mult),
                                waits=[t_p1], inc="vS")
                    t_c2 = E.op("v", lambda v: v.tensor_scalar(lg[bi][:, 48:56], lg[bi][:, 32:40], stat[:, B0 + 14:B0 + 15], None, op0=OP.mult),
                                waits=[t_p2, t_k2], inc="vS")
                    t_cb = E.op("v", lambda v, i=i: v.tensor_tensor(
                        comb[:, i * 8:(i + 1) * 8], lg[bi][:, 40:48], lg[bi][:, 48:56], op=OP.add),
                        waits=[t_c1, t_c2], inc="vS")
                    lg_free[bi] = t_cb

                    if BODY_LIM < 7:
                        E.mute = True
                    def do_ct(i=i, t_cb=t_cb):
                        csl = i % 2
                        t_ct = E.op("t", lambda t, i=i: t.transpose(pb[6][0:8, 0:128], comb[:, i * 8:(i + 1) * 8], idf[:]),
                                    waits=[t_cb, tok_smalls2, cts_free[csl]], inc="tS")
                        t_cs = E.op("s", lambda s, csl=csl: s.copy(cts[csl][0:8, :], pb[6][0:8, 0:128]), waits=[t_ct], inc="sS")
                        combT_wr[i] = E.op("s", lambda s, i=i, csl=csl: s.dma_start(
                            combT_d.ap()[:, i * 128:(i + 1) * 128], cts[csl][0:8, :]),
                            waits=[t_cs], inc=f"cbt{csl}", inc_by=16)
                        cts_free[csl] = toks(t_cs, combT_wr[i])
                    pending_ct[0] = do_ct

                E.mute = MAX_PHASE < 1
                flush_ct()
                p1_end = [("vS", E.counts["vS"]), ("sS", E.counts["sS"]), ("tS", E.counts["tS"]),
                          ("cbt0", E.counts["cbt0"]), ("cbt1", E.counts["cbt1"]), ("x0", E.counts["x0"]), ("x1", E.counts["x1"])]
                if debug:
                    E.op("y", lambda y: y.dma_start(dbg["xnT"].ap(), xnT[:]), waits=p1_end, inc="dbq", inc_by=16)
                    E.op("y", lambda y: y.dma_start(dbg["comb"].ap(), comb[:]), waits=p1_end, inc="dbq", inc_by=16)

            E.mute = MAX_PHASE < 2
            with contextlib.ExitStack() as c2:
                sb2 = lambda n, sh, dt: c2.enter_context(nc.sbuf_tensor(n, sh, dt))
                qTb = sb2("qTb", [128, T], BF16)
                kTb = sb2("kTb", [128, 2 * T], BF16)
                vbuf = sb2("vbuf", [128, KB * 129], BF16)
                wqb = [sb2(f"wqb{i}", [128, KC * 128], BF16) for i in range(2)]
                wkb = [sb2(f"wkb{i}", [128, KC * 128], BF16) for i in range(2)]
                wvb = [sb2(f"wvb{i}", [128, KC * 128], BF16) for i in range(2)]
                e1b = [sb2(f"e1b{i}", [128, 512], BF16) for i in range(2)]
                e2b = [sb2(f"e2b{i}", [128, 512], BF16) for i in range(2)]
                oqb = [sb2(f"oqb{i}", [128, 512], F32) for i in range(2)]
                o1sb = sb2("o1sb", [128, 4 * 129], F32)
                o2sb = sb2("o2sb", [128, 4 * 129], F32)
                ot1 = sb2("ot1", [128, 128], F32)
                ot2 = sb2("ot2", [128, 128], F32)
                ojk = sb2("ojk", [128, 128], F32)

                E.barrier(p1_end)
                tok_ones = E.op("v", lambda v: v.memset(vbuf[:], 1.0), waits=p1_end, inc="vS")
                tok_kz = E.op("v", lambda v: v.memset(kTb[:], 0.0), waits=p1_end, inc="vS")

                wq_tok = [None, None]
                wk_tok = [None, None]
                wv_tok = [None, None]
                wq_free = [None, None]

                def load_pair_w(p):
                    sl = p % 2
                    E.op("g", lambda g, p=p, sl=sl: g.dma_start(
                        wqb[sl][:].rearrange("p (k q) -> p k q", k=KC), wq_d.ap()[p].rearrange("k p q -> p k q")),
                        waits=[wq_free[sl]], inc=f"qkv{sl}", inc_by=16)
                    E.op("g", lambda g, p=p, sl=sl: g.dma_start(
                        wkb[sl][:].rearrange("p (k q) -> p k q", k=KC), wk_d.ap()[p].rearrange("k p q -> p k q")), inc=f"qkv{sl}", inc_by=16)
                    tok3 = E.op("g", lambda g, p=p, sl=sl: g.dma_start(
                        wvb[sl][:].rearrange("p (k q) -> p k q", k=KC), wv_d.ap()[p].rearrange("k p q -> p k q")), inc=f"qkv{sl}", inc_by=16)
                    wq_tok[sl] = tok3
                    wk_tok[sl] = tok3
                    wv_tok[sl] = tok3

                load_pair_w(0)

                pending_tp = [None]
                pending_mg = [None]
                sync_mg = [None]

                def do_pending_mg():
                    if pending_mg[0] is not None:
                        f = pending_mg[0]
                        pending_mg[0] = None
                        f()

                def emit_tp():
                    if pending_tp[0] is None:
                        return
                    ob_, mtoks, p_, qb_ = pending_tp[0]
                    pending_tp[0] = None
                    tp_sc = [None, None]
                    for qs in range(4):
                        half = qs % 2
                        t_tp = E.op("t", lambda t, ob=ob_, qs=qs, half=half: t.transpose(
                            pb[2 + half][:, 0:128],
                            oqb[ob][:, qs * 128:(qs + 1) * 128], idf[:]),
                            waits=[mtoks[qs], bankw[2 + half], tp_sc[half]], inc="tS")
                        t_sc = E.op("s", lambda s, p=p_, qb=qb_, qs=qs, half=half: s.copy(
                            oT[:, p * T + qb * 512 + qs * 128: p * T + qb * 512 + (qs + 1) * 128],
                            pb[2 + half][:, 0:128]),
                            waits=[t_tp], inc="sS")
                        tp_sc[half] = t_sc
                        oT_tok.append(t_sc)
                    bankw[2] = [tp_sc[0]]
                    bankw[3] = [tp_sc[1]]
                    oqb_free[ob_] = toks(tp_sc[0], tp_sc[1])

                qT_tok = [None] * QB
                kT_tok = [None] * QB
                v_tok = [None] * KB
                bankw = {b: list(p1_end) for b in range(8)}
                oqb_free = [None, None]
                oT_tok = []

                for p in range(PAIRS):
                    sl = p % 2
                    for tq in range(QB):
                        bank = tq % 2
                        t_mm = None
                        for kc in range(KC):
                            t_mm = E.op("t", lambda t, kc=kc, tq=tq, sl=sl, bank=bank: t.matmul(
                                pb[bank][:, 0:512], wqb[sl][:, kc * 128:(kc + 1) * 128],
                                xnT[:, kc * T + tq * 512: kc * T + (tq + 1) * 512],
                                start=(kc == 0), stop=(kc == KC - 1)),
                                waits=[wq_tok[sl], xnT_tok[tq * 4 + 3], bankw[bank]] if kc == 0 else [],
                                inc="tS" if kc == KC - 1 else None)
                        t_ev = E.op("s", lambda s, tq=tq, bank=bank: s.activation(
                            qTb[:, tq * 512:(tq + 1) * 512], pb[bank][:, 0:512], AF.Copy, scale=cc(C_EIGHTH)),
                            waits=[t_mm, tok_smalls], inc="sS")
                        qT_tok[tq] = t_ev
                        bankw[bank] = [t_ev]
                    for tq in range(QB):
                        bank = 2 + tq % 2
                        t_mm = None
                        for kc in range(KC):
                            t_mm = E.op("t", lambda t, kc=kc, tq=tq, sl=sl, bank=bank: t.matmul(
                                pb[bank][:, 0:512], wkb[sl][:, kc * 128:(kc + 1) * 128],
                                xnT[:, kc * T + tq * 512: kc * T + (tq + 1) * 512],
                                start=(kc == 0), stop=(kc == KC - 1)),
                                waits=[wk_tok[sl], xnT_tok[tq * 4 + 3], bankw[bank]] if kc == 0 else [],
                                inc="tS" if kc == KC - 1 else None)
                        t_e1 = E.op("s", lambda s, tq=tq, bank=bank: s.copy(
                            kTb[0:64, tq * 512:(tq + 1) * 512], pb[bank][0:64, 0:512]),
                            waits=[t_mm, tok_kz], inc="sS")
                        t_e2 = E.op("s", lambda s, tq=tq, bank=bank: s.copy(
                            kTb[64:128, T + tq * 512: T + (tq + 1) * 512], pb[bank][64:128, 0:512]),
                            waits=[t_e1], inc="sS")
                        kT_tok[tq] = t_e2
                        bankw[bank] = [t_e2]
                    for tv in range(KB):
                        bank = 4 + tv % 2
                        t_mm = None
                        for kc in range(KC):
                            t_mm = E.op("t", lambda t, kc=kc, tv=tv, sl=sl, bank=bank: t.matmul(
                                pb[bank][:, 0:128], xnT[:, kc * T + tv * 128: kc * T + (tv + 1) * 128],
                                wvb[sl][:, kc * 128:(kc + 1) * 128],
                                start=(kc == 0), stop=(kc == KC - 1)),
                                waits=[wv_tok[sl], tok_ones, xnT_tok[tv // 4 * 4 + 3], bankw[bank]] if kc == 0 else [],
                                inc="tS" if kc == KC - 1 else None)
                        t_ev = E.op("s", lambda s, tv=tv, bank=bank: s.copy(
                            vbuf[:, tv * 129: tv * 129 + 128], pb[bank][:, 0:128]),
                            waits=[t_mm], inc="sS")
                        v_tok[tv] = t_ev
                        bankw[bank] = [t_ev]
                    if p + 1 < PAIRS:
                        wq_free[(p + 1) % 2] = [("tS", E.counts["tS"])]
                        load_pair_w(p + 1)

                    for qb in range(QB):
                        emit_conv(2)
                        o1_evac = []
                        pv_last = None
                        for stream in (1, 2):
                            ebuf = e1b if stream == 1 else e2b
                            plo = 0 if stream == 1 else 64
                            e_free = [None, None]
                            pv_tok_prev = [None, None]
                            pend = None
                            for kb in range(KB):
                                eb = kb % 2
                                sb_ = kb % 2
                                coff = 0 if stream == 1 else T
                                t_s = E.op("t", lambda t, kb=kb, qb=qb, sb_=sb_, coff=coff: t.matmul(
                                    pb[sb_][:, 0:512], kTb[:, coff + kb * 128: coff + (kb + 1) * 128],
                                    qTb[:, qb * 512:(qb + 1) * 512],
                                    start=True, stop=True),
                                    waits=[qT_tok[qb], kT_tok[kb // 4], bankw[sb_]], inc="tS")
                                t_x = E.op("s", lambda s, eb=eb, sb_=sb_, ebuf=ebuf: s.activation(
                                    ebuf[eb][:], pb[sb_][:, 0:512], AF.Exp, bias=cc(C_NEG12)),
                                    waits=[t_s, e_free[eb]], inc="sS")
                                bankw[sb_] = [t_x]
                                if pend is not None:
                                    pkb, peb, pt_x = pend
                                    t_pv = None
                                    for qs in range(4):
                                        bnk = 4 + qs
                                        w = [pt_x, v_tok[pkb]]
                                        if pkb == 0:
                                            w.append(bankw[bnk])
                                        t_pv = E.op("t", lambda t, pkb=pkb, peb=peb, qs=qs, ebuf=ebuf, bnk=bnk: t.matmul(
                                            pb[bnk][:, 0:129], ebuf[peb][:, qs * 128:(qs + 1) * 128],
                                            vbuf[:, pkb * 129:(pkb + 1) * 129],
                                            start=(pkb == 0), stop=(pkb == KB - 1)),
                                            waits=w, inc="tS" if qs == 3 else None)
                                    e_free[peb] = t_pv
                                pend = (kb, eb, t_x)
                                if kb == 10 and stream == 1:
                                    do_pending_mg()
                            pkb, peb, pt_x = pend
                            t_pv = None
                            for qs in range(4):
                                bnk = 4 + qs
                                t_pv = E.op("t", lambda t, pkb=pkb, peb=peb, qs=qs, ebuf=ebuf, bnk=bnk: t.matmul(
                                    pb[bnk][:, 0:129], ebuf[peb][:, qs * 128:(qs + 1) * 128],
                                    vbuf[:, pkb * 129:(pkb + 1) * 129],
                                    start=False, stop=True),
                                    waits=[pt_x, v_tok[pkb]], inc="tS" if qs == 3 else None)
                            e_free[peb] = t_pv
                            pv_last = t_pv
                            if stream == 2:
                                o2_evac = []
                                for qs in range(4):
                                    t_e2 = E.op("v", lambda v, qs=qs: v.tensor_copy(
                                        o2sb[:, qs * 129:(qs + 1) * 129], pb[4 + qs][:, 0:129]),
                                        waits=[pv_last, sync_mg[0]], inc="vS")
                                    o2_evac.append(t_e2)
                                for qs in range(4):
                                    bankw[4 + qs] = [o2_evac[qs]]
                            if stream == 1:
                                for qs in range(4):
                                    t_e1 = E.op("v", lambda v, qs=qs: v.tensor_copy(
                                        o1sb[:, qs * 129:(qs + 1) * 129], pb[4 + qs][:, 0:129]),
                                        waits=[pv_last, sync_mg[0]], inc="vS")
                                    o1_evac.append(t_e1)
                                for qs in range(4):
                                    bankw[4 + qs] = [o1_evac[qs]]
                                emit_tp()

                        ob = qb % 2
                        def do_merges(ob=ob, qb=qb, p=p, o1e=tuple(o1_evac), o2e=tuple(o2_evac)):
                            merge_toks = []
                            for qs in range(4):
                                w = [o1e[qs], o2e[qs], tok_lam] + ([oqb_free[ob]] if qs == 0 else [])
                                t_r1 = E.op("v", lambda v, qs=qs: v.reciprocal(
                                    stat[:, 16:17], o1sb[:, qs * 129 + 128:qs * 129 + 129]),
                                    waits=w, inc="vS")
                                t_r2 = E.op("v", lambda v, qs=qs: v.reciprocal(
                                    stat[:, 17:18], o2sb[:, qs * 129 + 128:qs * 129 + 129]),
                                    waits=[t_r1], inc="vS")
                                t_rl = E.op("v", lambda v: v.tensor_scalar(stat[:, 18:19], stat[:, 17:18], lamc[:], None, op0=OP.mult),
                                            waits=[t_r2], inc="vS")
                                t_t1 = E.op("v", lambda v, qs=qs: v.tensor_scalar(
                                    ot1[:], o1sb[:, qs * 129:qs * 129 + 128], stat[:, 16:17], None, op0=OP.mult),
                                    waits=[t_r1], inc="vS")
                                t_t2 = E.op("v", lambda v, qs=qs: v.tensor_scalar(
                                    ot2[:], o2sb[:, qs * 129:qs * 129 + 128], stat[:, 18:19], None, op0=OP.mult),
                                    waits=[t_rl], inc="vS")
                                t_om = E.op("v", lambda v, ob=ob, qs=qs: v.tensor_tensor(
                                    oqb[ob][:, qs * 128:(qs + 1) * 128], ot1[:], ot2[:], op=OP.subtract),
                                    waits=[t_t1, t_t2], inc="vS")
                                t_sq = E.op("v", lambda v, ob=ob, qs=qs: v.tensor_tensor(
                                    ot1[:], oqb[ob][:, qs * 128:(qs + 1) * 128],
                                    oqb[ob][:, qs * 128:(qs + 1) * 128], op=OP.mult),
                                    waits=[t_om], inc="vS")
                                t_ss = E.op("v", lambda v: v.reduce_sum(stat[:, 19:20], ot1[:], axis=AX.X),
                                            waits=[t_sq], inc="vS")
                                t_s1m = E.op("v", lambda v, ob=ob, qs=qs: v.reduce_sum(
                                    stat[:, 20:21], oqb[ob][:, qs * 128:(qs + 1) * 128], axis=AX.X),
                                    waits=[t_om], inc="vS")
                                t_nmu = E.op("v", lambda v: v.tensor_scalar(stat[:, 21:22], stat[:, 20:21], cc(C_NINV128), None, op0=OP.mult),
                                             waits=[t_s1m], inc="vS")
                                t_mm2 = E.op("v", lambda v: v.tensor_tensor(stat[:, 22:23], stat[:, 21:22], stat[:, 21:22], op=OP.mult),
                                             waits=[t_nmu], inc="vS")
                                t_va = E.op("v", lambda v: v.tensor_scalar(stat[:, 23:24], stat[:, 19:20], cc(C_INV128), None, op0=OP.mult),
                                            waits=[t_ss], inc="vS")
                                t_vb = E.op("v", lambda v: v.tensor_tensor(stat[:, 24:25], stat[:, 23:24], stat[:, 22:23], op=OP.subtract),
                                            waits=[t_va, t_mm2], inc="vS")
                                t_ln = E.op("s", lambda s: s.activation(stat[:, 25:26], stat[:, 24:25], AF.Ln, bias=cc(C_EPS)),
                                            waits=[t_vb], inc="sS")
                                t_rsd = E.op("s", lambda s: s.activation(stat[:, 26:27], stat[:, 25:26], AF.Exp, scale=cc(C_NEGHALF)),
                                             waits=[t_ln], inc="sS")
                                t_on = E.op("v", lambda v, ob=ob, qs=qs: v.tensor_scalar(
                                    ot1[:], oqb[ob][:, qs * 128:(qs + 1) * 128], stat[:, 21:22], stat[:, 26:27],
                                    op0=OP.add, op1=OP.mult),
                                    waits=[t_rsd], inc="vS")
                                t_lg = E.op("v", lambda v: v.tensor_tensor(ot2[:], ot1[:], lngt[:], op=OP.mult),
                                            waits=[t_on, tok_smalls2], inc="vS")
                                t_fin = E.op("v", lambda v, ob=ob, qs=qs: v.tensor_tensor(
                                    oqb[ob][:, qs * 128:(qs + 1) * 128], ot2[:], lnbt[:], op=OP.add),
                                    waits=[t_lg], inc="vS")
                                merge_toks.append(t_fin)
                            sync_mg[0] = toks(*merge_toks)
                            pending_tp[0] = (ob, merge_toks, p, qb)
                        pending_mg[0] = do_merges
                    do_pending_mg()
                    emit_tp()

                p2_end = [("tS", E.counts["tS"]), ("sS", E.counts["sS"]), ("vS", E.counts["vS"]),
                          ("qkv0", E.counts["qkv0"]), ("qkv1", E.counts["qkv1"])]
                if debug:
                    E.op("y", lambda y: y.dma_start(dbg["oT"].ap(), oT[:]), waits=p2_end, inc="dbq", inc_by=16)

            E.mute = MAX_PHASE < 3
            with contextlib.ExitStack() as c3:
                sb3 = lambda n, sh, dt: c3.enter_context(nc.sbuf_tensor(n, sh, dt))
                wob = sb3("wob", [128, KC * D], BF16)
                sst = [sb3(f"sst{i}", [128, D], BF16) for i in range(2)]

                emit_conv(10**6)
                E.barrier(p2_end)
                tok_wo = E.op("g", lambda g: g.dma_start(wob[:].rearrange("p (k d) -> p k d", k=KC), wo_d.ap().rearrange("k p d -> p k d")),
                              waits=p2_end, inc="woq", inc_by=16)
                sst_free = [None, None]
                pbw = [list(p2_end), list(p2_end)]
                for i in range(NT):
                    bi = i % 2
                    t_ev_last = None
                    for dc in range(2):
                        bank = dc
                        t_mm = None
                        for kc in range(KC):
                            t_mm = E.op("t", lambda t, kc=kc, i=i, dc=dc, bank=bank: t.matmul(
                                pb[bank][:, 0:512], oT[:, kc * T + i * 128: kc * T + (i + 1) * 128],
                                wob[:, kc * D + dc * 512: kc * D + dc * 512 + 512],
                                start=(kc == 0), stop=(kc == KC - 1)),
                                waits=[tok_wo, pbw[bank]] if kc == 0 else [],
                                inc="tS" if kc == KC - 1 else None)
                        t_ev = E.op("s", lambda s, dc=dc, bi=bi, bank=bank: s.copy(
                            sst[bi][:, dc * 512:dc * 512 + 512], pb[bank][:, 0:512]),
                            waits=[t_mm] + ([sst_free[bi]] if dc == 0 else []), inc="sS")
                        pbw[bank] = [t_ev]
                        t_ev_last = t_ev
                    s_wr[i] = E.op("s", lambda s, i=i, bi=bi: s.dma_start(
                        ssc_d.ap()[i * 128:(i + 1) * 128, :], sst[bi][:]),
                        waits=[t_ev_last], inc=f"sw{bi}", inc_by=16)
                    sst_free[bi] = s_wr[i]
                    if debug:
                        tdbg = E.op("s", lambda s, i=i, bi=bi: s.dma_start(
                            dbg["s"].ap()[i * 128:(i + 1) * 128, :], sst[bi][:]),
                            waits=[t_ev_last], inc=f"sw{bi}", inc_by=16)
                        sst_free[bi] = toks(s_wr[i], tdbg)

            p3_end = [("tS", E.counts["tS"]), ("sS", E.counts["sS"]), ("vS", E.counts["vS"]),
                      ("woq", E.counts["woq"]), ("sw0", E.counts["sw0"]), ("sw1", E.counts["sw1"]),
                      ("cin0", E.counts["cin0"]), ("cin1", E.counts["cin1"]),
                      ("cout0", E.counts["cout0"]), ("cout1", E.counts["cout1"]),
                      ("sm2", E.counts["sm2"]), ("dbq", E.counts["dbq"])]

        E.mute = MAX_PHASE < 4
        with contextlib.ExitStack() as c4:
            sb4 = lambda n, sh, dt: c4.enter_context(nc.sbuf_tensor(n, sh, dt))
            hbuf = sb4("hbuf", [128, HC * 512], BF16)
            w1g = [sb4(f"w1g{i}", [128, 4096], BF16) for i in range(2)]
            w2g = [sb4(f"w2g{i}", [128, 4096], BF16) for i in range(2)]
            gacc = sb4("gacc", [128, 4 * D], F32)
            stb = sb4("stb", [128, 4 * D], BF16)
            xtb = sb4("xtb", [128, 4 * D], F32)
            ostg = [sb4(f"ostg{i}", [128, D], F32) for i in range(2)]
            ytmp = [sb4(f"ytmp{i}", [128, 512], F32) for i in range(2)]
            gtmp = sb4("gtmp", [128, D], F32)
            cts4 = sb4("cts4", [8, T], BF16)

            x_tb_ap = x_d.ap().rearrange("(tb ts p) d -> tb p ts d", ts=4, p=128)
            s_tb_ap = ssc_d.ap().rearrange("(tb ts p) d -> tb p ts d", ts=4, p=128)
            xtb_v = xtb[:].rearrange("p (ts d) -> p ts d", ts=4)
            stb_v = stb[:].rearrange("p (ts d) -> p ts d", ts=4)

            E.barrier(p3_end)
            tok_ct4 = E.op("y", lambda y: y.dma_start(cts4[:], combT_d.ap()),
                           waits=[toks(*combT_wr), p3_end], inc="ctq", inc_by=16)

            w1_tok = [None] * 2
            w2_tok = [None] * 2
            w1_use = [list(p3_end) for _ in range(2)]
            w2_use = [list(p3_end) for _ in range(2)]
            pbw4 = {i: list(p3_end) for i in range(8)}

            def dma_w1(e, g):
                sl = g % 2
                w1_tok[sl] = E.op("y", lambda y, e=e, g=g, sl=sl: y.dma_start(
                    w1g[sl][:], wsc1_d.ap()[e, g]),
                    waits=[w1_use[sl], conv_done_w1[e]], inc=f"w1s{sl}", inc_by=16)

            def dma_w2(e, g):
                sl = g % 2
                w2_tok[sl] = E.op("y", lambda y, e=e, g=g, sl=sl: y.dma_start(
                    w2g[sl][:], wsc2_d.ap()[e, g]),
                    waits=[w2_use[sl], conv_done_w2[e]], inc=f"w2s{sl}", inc_by=16)

            gacc_free = None
            stb_free = None
            xtb_free = None
            ostg_free = [None, None]
            ytmp_free = [None, None]
            gtmp_free = None
            out_tok = []

            for tb in range(NTB):
                xtb_tok = E.op("g", lambda g, tb=tb: g.dma_start(xtb_v, x_tb_ap[tb]),
                               waits=[xtb_free], inc="xtq", inc_by=16)
                stb_tok = E.op("y", lambda y, tb=tb: y.dma_start(stb_v, s_tb_ap[tb]),
                               waits=[stb_free] + [s_wr[tb * 4 + j] for j in range(4)], inc="stq", inc_by=16)

                ginit = []
                for ts4 in range(4):
                    for dc in range(2):
                        bank = 6 + (ts4 * 2 + dc) % 2
                        t_mm = E.op("t", lambda t, tb=tb, ts4=ts4, dc=dc, bank=bank: t.matmul(
                            pb[bank][:, 0:512], cts4[0:8, tb * 512 + ts4 * 128: tb * 512 + (ts4 + 1) * 128],
                            b2s[0:8, dc * 512:dc * 512 + 512], start=True, stop=True),
                            waits=[tok_ct4, tok_smalls, pbw4[bank]], inc="tS")
                        t_cp = E.op("s", lambda s, ts4=ts4, dc=dc, bank=bank: s.copy(
                            gacc[:, ts4 * D + dc * 512: ts4 * D + dc * 512 + 512], pb[bank][:, 0:512]),
                            waits=[t_mm] + ([gacc_free] if (ts4 == 0 and dc == 0) else []), inc="sS")
                        pbw4[bank] = [t_cp]
                        ginit.append(t_cp)
                gacc_ready = toks(*ginit)

                for e in range(NE):
                    if e == 0 and tb == 0:
                        for g0 in range(2):
                            dma_w1(e, g0)
                        for g0 in range(2):
                            dma_w2(e, g0)

                    gelu_toks = []
                    for hc in range(HC):
                        g0 = hc // 4
                        sl = g0 % 2
                        bank = hc % 2
                        t_mm = None
                        for kc in range(KC):
                            t_mm = E.op("t", lambda t, hc=hc, kc=kc, tb=tb, sl=sl, bank=bank: t.matmul(
                                pb[bank][:, 0:512],
                                w1g[sl][:, (hc % 4) * 1024 + kc * 128: (hc % 4) * 1024 + (kc + 1) * 128],
                                xnT[:, kc * T + tb * 512: kc * T + (tb + 1) * 512],
                                start=(kc == 0), stop=(kc == KC - 1)),
                                waits=[w1_tok[sl], xnT_tok[NT - 1], pbw4[bank]] if kc == 0 else [],
                                inc="tS" if kc == KC - 1 else None)
                        if hc % 4 == 3:
                            w1_use[sl] = [t_mm]
                            gnext = g0 + 2
                            if gnext < NG:
                                dma_w1(e, gnext)
                            elif e + 1 < NE:
                                dma_w1(e + 1, gnext - NG)
                            elif tb + 1 < NTB:
                                dma_w1(0, gnext - NG)
                        t_ge = E.op("s", lambda s, hc=hc, e=e, bank=bank: s.activation(
                            hbuf[:, hc * 512:(hc + 1) * 512], pb[bank][:, 0:512], AF.Gelu,
                            bias=b1s[:, e * HC + hc: e * HC + hc + 1]),
                            waits=[t_mm, tok_smalls], inc="sS")
                        pbw4[bank] = [t_ge]
                        gelu_toks.append(t_ge)

                    t_mm2_last = {}
                    for hc in range(HC):
                        g0 = hc // 4
                        sl = g0 % 2
                        t_mm = None
                        for ts4 in range(4):
                            for dc in range(2):
                                bank = ts4 * 2 + dc
                                w = [gelu_toks[hc], w2_tok[sl]]
                                if hc == 0:
                                    w.append(pbw4[bank])
                                    w.append(gacc_ready)
                                give_inc = (hc == HC - 1) or (hc % 4 == 3 and ts4 == 3 and dc == 1)
                                t_mm = E.op("t", lambda t, hc=hc, ts4=ts4, dc=dc, sl=sl, bank=bank: t.matmul(
                                    pb[bank][:, 0:512],
                                    hbuf[:, hc * 512 + ts4 * 128: hc * 512 + (ts4 + 1) * 128],
                                    w2g[sl][:, (hc % 4) * 1024 + dc * 512: (hc % 4) * 1024 + dc * 512 + 512],
                                    start=(hc == 0), stop=(hc == HC - 1)),
                                    waits=w, inc="tS" if give_inc else None)
                                if hc == HC - 1:
                                    t_mm2_last[bank] = t_mm
                        if hc % 4 == 3:
                            w2_use[sl] = [t_mm]
                            gnext = g0 + 2
                            if gnext < NG:
                                dma_w2(e, gnext)
                            elif e + 1 < NE:
                                dma_w2(e + 1, gnext - NG)
                            elif tb + 1 < NTB:
                                dma_w2(0, gnext - NG)

                    new_ready = list(gacc_ready)
                    for bank in range(8):
                        ts4, dc = bank // 2, bank % 2
                        yb = bank % 2
                        ci = (tb * 4 + ts4) * 8 + e
                        t_sc = E.op("s", lambda s, bank=bank, ci=ci, yb=yb: s.activation(
                            ytmp[yb][:], pb[bank][:, 0:512], AF.Copy, scale=comb[:, ci:ci + 1]),
                            waits=[t_mm2_last[bank], ytmp_free[yb]], inc="sS")
                        t_ad = E.op("v", lambda v, ts4=ts4, dc=dc, yb=yb: v.tensor_tensor(
                            gacc[:, ts4 * D + dc * 512: ts4 * D + dc * 512 + 512],
                            gacc[:, ts4 * D + dc * 512: ts4 * D + dc * 512 + 512],
                            ytmp[yb][:], op=OP.add),
                            waits=[t_sc, gacc_ready], inc="vS")
                        ytmp_free[yb] = t_ad
                        pbw4[bank] = toks(t_sc, t_ad)
                        new_ready.append(t_ad)
                    gacc_ready = toks(*new_ready)

                tail_v = []
                t_si_last = None
                for ts4 in range(4):
                    ob = ts4 % 2
                    t_si = E.op("s", lambda s, ts4=ts4: s.activation(
                        gtmp[:], gacc[:, ts4 * D:(ts4 + 1) * D], AF.Silu),
                        waits=[gacc_ready, gtmp_free], inc="sS")
                    t_mu = E.op("v", lambda v, ts4=ts4: v.tensor_tensor(
                        gtmp[:], gtmp[:], stb[:, ts4 * D:(ts4 + 1) * D], op=OP.mult),
                        waits=[t_si, stb_tok], inc="vS")
                    t_ou = E.op("v", lambda v, ts4=ts4, ob=ob: v.tensor_tensor(
                        ostg[ob][:], gtmp[:], xtb[:, ts4 * D:(ts4 + 1) * D], op=OP.add),
                        waits=[t_mu, xtb_tok, ostg_free[ob]], inc="vS")
                    t_wr = E.op("g", lambda g, tb=tb, ts4=ts4, ob=ob: g.dma_start(
                        out_d.ap()[(tb * 4 + ts4) * 128:(tb * 4 + ts4 + 1) * 128, :], ostg[ob][:]),
                        waits=[t_ou], inc=f"ow{ob}", inc_by=16)
                    ostg_free[ob] = t_wr
                    gtmp_free = t_ou
                    t_si_last = t_si
                    tail_v.append(t_ou)
                    out_tok.append(t_wr)
                gacc_free = toks(t_si_last)
                xtb_free = toks(*tail_v)
                stb_free = toks(*tail_v)

        E.mute = False
        fin = toks(*out_tok) + [("dbq", E.counts["dbq"]), ("sw0", E.counts["sw0"]),
                                ("sw1", E.counts["sw1"]), ("cbt0", E.counts["cbt0"]),
                                ("cbt1", E.counts["cbt1"])]
        for sem, val in toks(*[t for t in fin if t[1] > 0]):
            h = E.sems[sem]
            E.ops["y"].append(lambda e, h=h, v=val: e.wait_ge(h, v))

        block = ctx.enter_context(nc.Block())
        E.emit(block)

    return nc


def _prep_core_inputs(x_core, inp):
    W1 = inp["L0_ffn_W1"]; W2 = inp["L0_ffn_W2"]
    w1t = np.ascontiguousarray(
        W1.reshape(NE, KC, 128, HC, 128)
          .transpose(0, 3, 2, 1, 4)
          .reshape(NE, NG, 4, 128, KC, 128)
          .transpose(0, 1, 3, 2, 4, 5)
          .reshape(NE, NG, 128, 4096))
    w2t = np.ascontiguousarray(
        W2.reshape(NE, NG, 4, 128, D)
          .transpose(0, 1, 3, 2, 4)
          .reshape(NE, NG, 128, 4096))
    def qk_t(W):
        return np.ascontiguousarray(W.reshape(KC, 128, PAIRS, 128).transpose(2, 0, 1, 3))
    consts = np.zeros((128, NCONST), np.float32)
    consts[:, C_NEG12] = -12.0
    consts[:, C_EIGHTH] = 0.125
    consts[:, C_NINV1024] = -1.0 / 1024
    consts[:, C_INV1024] = 1.0 / 1024
    consts[:, C_NINV128] = -1.0 / 128
    consts[:, C_INV128] = 1.0 / 128
    consts[:, C_EPS] = 1e-5
    consts[:, C_NEGHALF] = -0.5
    consts[:, C_ONE] = 1.0
    consts[:, C_BIG] = 1e30
    consts[:, C_LAMI] = LAM_INIT
    lrow = np.concatenate([inp["L0_attn_lq1"], inp["L0_attn_lk1"],
                           inp["L0_attn_lq2"], inp["L0_attn_lk2"]]).reshape(1, 256).astype(np.float32)
    return dict(
        x=np.ascontiguousarray(x_core),
        wq_t=qk_t(inp["L0_attn_Wq"]),
        wk_t=qk_t(inp["L0_attn_Wk"]),
        wv_t=qk_t(inp["L0_attn_Wv"]),
        wo_t=np.ascontiguousarray(inp["L0_attn_Wo"].reshape(KC, 128, D)),
        wr_t=np.ascontiguousarray(inp["L0_ffn_Wr"].reshape(KC, 128, NE)),
        w1_t=w1t, w2_t=w2t,
        b1_t=np.ascontiguousarray(inp["L0_ffn_b1"].reshape(NE, HC, 128).transpose(2, 0, 1).reshape(128, NE * HC)),
        b2_t=np.ascontiguousarray(inp["L0_ffn_b2"]),
        lng_t=np.ascontiguousarray(np.tile(inp["L0_attn_ln_g"].reshape(1, 128), (128, 1))),
        lnb_t=np.ascontiguousarray(np.tile(inp["L0_attn_ln_b"].reshape(1, 128), (128, 1))),
        lrow=lrow,
        consts=consts,
        idf=np.eye(128, dtype=np.float32),
        onesr=np.ones((1, 128), np.float32),
    )


def kernel(**inputs):
    x = np.asarray(inputs["L0_attn_x"], np.float32)
    inp = {k: np.asarray(v, np.float32) for k, v in inputs.items()}
    nc = build_program(debug=DEBUG)
    in_maps = [_prep_core_inputs(x[c], inp) for c in range(N_CORES)]
    res = run_bass_kernel_spmd(nc, in_maps, core_ids=list(range(N_CORES)))
    out = np.stack([np.asarray(res.results[c]["out"], np.float32) for c in range(N_CORES)])
    return out


# version 07
# speedup vs baseline: 1.1080x; 1.0379x over previous
import contextlib
import numpy as np
import ml_dtypes

import concourse.bass as bass
import concourse.mybir as mybir
from concourse.bass_utils import run_bass_kernel_spmd

F32 = mybir.dt.float32
BF16 = mybir.dt.bfloat16
AF = mybir.ActivationFunctionType
OP = mybir.AluOpType
AX = mybir.AxisListType
bf16 = ml_dtypes.bfloat16

T = 4096
D = 1024
NT = 32
KC = 8
PAIRS = 8
QB = 8
KB = 32
NE = 8
HC = 32
NG = 8
NTB = 8
LAM_INIT = float(0.8 - 0.6 * np.exp(-0.3 * 12))

(C_NEG12, C_EIGHTH, C_NINV1024, C_INV1024, C_NINV128, C_INV128, C_EPS,
 C_NEGHALF, C_ONE, C_BIG, C_LAMI) = range(11)
NCONST = 16

DEBUG = False
MAX_PHASE = 4
NT_LIM = NT
BODY_LIM = 7
N_CORES = 8


def _flat(waits):
    out = []
    for w in waits:
        if w is None:
            continue
        if isinstance(w, list):
            out.extend(t for t in w if t is not None)
        else:
            out.append(w)
    return out


def toks(*ws):
    best = {}
    for t in _flat(ws):
        s, v = t
        if best.get(s, -1) < v:
            best[s] = v
    return [(s, v) for s, v in best.items()]


class Emitter:
    ENGS = ("t", "v", "s", "g", "y")

    def __init__(self):
        self.mute = False
        self.ops = {e: [] for e in self.ENGS}
        self.counts = {}
        self.watermark = {}
        self.sems = {}

    def add_sem(self, name, handle):
        self.sems[name] = handle
        self.counts[name] = 0

    def op(self, eng, fn, waits=(), inc=None, inc_by=1):
        if self.mute:
            return (inc, self.counts[inc]) if inc else None
        for sem, val in toks(*waits):
            assert val <= self.counts[sem], f"wait on future: {sem} {val} > {self.counts[sem]}"
            key = (eng, sem)
            if self.watermark.get(key, 0) >= val:
                continue
            self.watermark[key] = val
            h = self.sems[sem]
            self.ops[eng].append(lambda e, h=h, v=val: e.wait_ge(h, v))
        if inc is None:
            self.ops[eng].append(fn)
            return None
        h = self.sems[inc]
        def wrapped(e, fn=fn, h=h, n=inc_by):
            fn(e).then_inc(h, n)
        self.ops[eng].append(wrapped)
        self.counts[inc] += inc_by
        return (inc, self.counts[inc])

    def barrier(self, waits):
        if self.mute:
            return
        for eng in self.ENGS:
            for sem, val in toks(*waits):
                key = (eng, sem)
                if self.watermark.get(key, 0) >= val:
                    continue
                self.watermark[key] = val
                h = self.sems[sem]
                self.ops[eng].append(lambda e, h=h, v=val: e.wait_ge(h, v))

    def emit(self, block):
        ops = self.ops
        @block.tensor
        def _(t):
            for f in ops["t"]: f(t)
        @block.vector
        def _(v):
            for f in ops["v"]: f(v)
        @block.scalar
        def _(s):
            for f in ops["s"]: f(s)
        @block.gpsimd
        def _(g):
            for f in ops["g"]: f(g)
        @block.sync
        def _(y):
            for f in ops["y"]: f(y)


def build_program(debug=False):
    nc = bass.Bass("TRN2", target_bir_lowering=False)

    x_d = nc.dram_tensor("x", [T, D], F32, kind="ExternalInput")
    wq_d = nc.dram_tensor("wq_t", [PAIRS, KC, 128, 128], F32, kind="ExternalInput")
    wk_d = nc.dram_tensor("wk_t", [PAIRS, KC, 128, 128], F32, kind="ExternalInput")
    wv_d = nc.dram_tensor("wv_t", [PAIRS, KC, 128, 128], F32, kind="ExternalInput")
    wo_d = nc.dram_tensor("wo_t", [KC, 128, D], F32, kind="ExternalInput")
    wr_d = nc.dram_tensor("wr_t", [KC, 128, NE], F32, kind="ExternalInput")
    w1_d = nc.dram_tensor("w1_t", [NE, NG, 128, 4096], F32, kind="ExternalInput")
    w2_d = nc.dram_tensor("w2_t", [NE, NG, 128, 4096], F32, kind="ExternalInput")
    b1_d = nc.dram_tensor("b1_t", [128, NE * HC], F32, kind="ExternalInput")
    b2_d = nc.dram_tensor("b2_t", [NE, D], F32, kind="ExternalInput")
    lng_d = nc.dram_tensor("lng_t", [128, 128], F32, kind="ExternalInput")
    lnb_d = nc.dram_tensor("lnb_t", [128, 128], F32, kind="ExternalInput")
    lrow_d = nc.dram_tensor("lrow", [1, 256], F32, kind="ExternalInput")
    consts_d = nc.dram_tensor("consts", [128, NCONST], F32, kind="ExternalInput")
    idf_d = nc.dram_tensor("idf", [128, 128], F32, kind="ExternalInput")
    onesr_d = nc.dram_tensor("onesr", [1, 128], F32, kind="ExternalInput")

    out_d = nc.dram_tensor("out", [T, D], F32, kind="ExternalOutput")

    wsc1_d = nc.dram_tensor("wsc1", [NE, NG, 128, 4096], BF16, kind="Internal")
    wsc2_d = nc.dram_tensor("wsc2", [NE, NG, 128, 4096], BF16, kind="Internal")
    ssc_d = nc.dram_tensor("ssc", [T, D], BF16, kind="Internal")
    combT_d = nc.dram_tensor("combT_sc", [8, T], BF16, kind="Internal")

    dbg = {}
    if debug:
        dbg["xnT"] = nc.dram_tensor("dbg_xnT", [128, KC * T], BF16, kind="ExternalOutput")
        dbg["comb"] = nc.dram_tensor("dbg_comb", [128, NT * 8], F32, kind="ExternalOutput")
        dbg["lam"] = nc.dram_tensor("dbg_lam", [128, 1], F32, kind="ExternalOutput")
        dbg["s"] = nc.dram_tensor("dbg_s", [T, D], BF16, kind="ExternalOutput")
        dbg["oT"] = nc.dram_tensor("dbg_oT", [128, PAIRS * T], BF16, kind="ExternalOutput")

    E = Emitter()

    with contextlib.ExitStack() as ctx:
        sbuf = lambda n, sh, dt: ctx.enter_context(nc.sbuf_tensor(n, sh, dt))
        psum = lambda n, sh, dt: ctx.enter_context(nc.psum_tensor(n, sh, dt))
        _sems = ["tS", "sS", "vS", "sm1", "sm2", "cbt0", "cbt1", "woq", "stq", "xtq", "ctq", "dbq",
                 "cin0", "cin1", "cout0", "cout1", "x0", "x1", "qkv0", "qkv1",
                 "sw0", "sw1", "ow0", "ow1",
                 "w1s0", "w1s1", "w1s2", "w1s3", "w2s0", "w2s1", "w2s2", "w2s3"]
        for sname in _sems:
            E.add_sem(sname, ctx.enter_context(nc.semaphore(sname)))

        pb = [psum(f"pb{i}", [128, 512], F32) for i in range(8)]

        xnT = sbuf("xnT", [128, KC * T], BF16)
        comb = sbuf("comb", [128, NT * 8], F32)
        b1s = sbuf("b1s", [128, NE * HC], F32)
        b2s = sbuf("b2s", [NE, D], BF16)
        wrs = sbuf("wrs", [128, KC * NE], F32)
        cst = sbuf("cst", [128, NCONST], F32)
        lamc = sbuf("lamc", [128, 1], F32)
        lrow = sbuf("lrow_s", [1, 256], F32)
        onesr = sbuf("onesr_s", [1, 128], F32)
        lprod = sbuf("lprod", [1, 128], F32)
        l11 = sbuf("l11", [1, 16], F32)
        stat = sbuf("stat", [128, 64], F32)

        cc = lambda i: cst[:, i:i + 1]

        for dst, src in ((b1s, b1_d), (cst, consts_d), (lrow, lrow_d), (onesr, onesr_d)):
            E.op("g", lambda g, d=dst, s=src: g.dma_start(d[:], s.ap()), inc="sm1", inc_by=16)
        E.op("g", lambda g: g.dma_start(wrs[:].rearrange("p (k e) -> p k e", k=KC), wr_d.ap().rearrange("k p e -> p k e")), inc="sm1", inc_by=16)
        E.op("g", lambda g: g.dma_start(b2s[:], b2_d.ap()), inc="sm1", inc_by=16)
        tok_smalls = ("sm1", E.counts["sm1"])

        t1 = E.op("v", lambda v: v.tensor_tensor(lprod[0:1, 0:64], lrow[0:1, 0:64], lrow[0:1, 64:128], op=OP.mult),
                  waits=[tok_smalls], inc="vS")
        t2 = E.op("v", lambda v: v.tensor_tensor(lprod[0:1, 64:128], lrow[0:1, 128:192], lrow[0:1, 192:256], op=OP.mult),
                  waits=[t1], inc="vS")
        t3 = E.op("v", lambda v: v.reduce_sum(l11[0:1, 0:1], lprod[0:1, 0:64], axis=AX.X), waits=[t2], inc="vS")
        t4 = E.op("v", lambda v: v.reduce_sum(l11[0:1, 1:2], lprod[0:1, 64:128], axis=AX.X), waits=[t3], inc="vS")
        t5 = E.op("s", lambda s: s.activation(l11[0:1, 2:3], l11[0:1, 0:1], AF.Exp), waits=[t3, tok_smalls], inc="sS")
        t6 = E.op("s", lambda s: s.activation(l11[0:1, 3:4], l11[0:1, 1:2], AF.Exp), waits=[t4, t5], inc="sS")
        t7 = E.op("v", lambda v: v.tensor_tensor(l11[0:1, 4:5], l11[0:1, 2:3], l11[0:1, 3:4], op=OP.subtract),
                  waits=[t6], inc="vS")
        t8 = E.op("v", lambda v: v.tensor_scalar(l11[0:1, 5:6], l11[0:1, 4:5], cst[0:1, C_LAMI:C_LAMI + 1], None, op0=OP.add),
                  waits=[t7], inc="vS")
        t9 = E.op("t", lambda t: t.matmul(pb[7][:, 0:1], onesr[0:1, :], l11[0:1, 5:6], start=True, stop=True),
                  waits=[t8, tok_smalls], inc="tS")
        tok_lam = E.op("v", lambda v: v.tensor_copy(lamc[:], pb[7][:, 0:1]), waits=[t9], inc="vS")
        if debug:
            E.op("y", lambda y: y.dma_start(dbg["lam"].ap(), lamc[:]), waits=[tok_lam], inc="dbq", inc_by=16)

        E.mute = MAX_PHASE < 1
        with contextlib.ExitStack() as c23:
            sb23 = lambda n, sh, dt: c23.enter_context(nc.sbuf_tensor(n, sh, dt))
            oT = sb23("oT", [128, PAIRS * T], BF16)
            lngt = sb23("lngt", [128, 128], F32)
            lnbt = sb23("lnbt", [128, 128], F32)
            idf = sb23("idf_s", [128, 128], F32)
            convb = [sb23(f"convb{i}", [128, 2048], BF16) for i in range(2)]
            for dst_, src_ in ((lngt, lng_d), (lnbt, lnb_d), (idf, idf_d)):
                E.op("g", lambda g, d=dst_, s=src_: g.dma_start(d[:], s.ap()), inc="sm2", inc_by=16)
            tok_smalls2 = ("sm2", E.counts["sm2"])

            conv_done_w1 = [None] * NE
            conv_done_w2 = [None] * NE
            cvt_slot = [None, None]
            conv_steps = []
            for ei in range(NE):
                for which in range(2):
                    for gi in range(NG):
                        for hf in range(2):
                            conv_steps.append((which, ei, gi, hf))
            conv_idx = [0]

            def emit_conv(n):
                saved = E.mute
                E.mute = MAX_PHASE < 4
                for _ in range(n):
                    if conv_idx[0] >= len(conv_steps):
                        break
                    which, e_, gi, hf = conv_steps[conv_idx[0]]
                    sl = conv_idx[0] % 2
                    conv_idx[0] += 1
                    src_, dstd = ((w1_d, wsc1_d), (w2_d, wsc2_d))[which]
                    tin = E.op("g", lambda g, s=src_, e=e_, gg=gi, sl=sl, hf=hf:
                               g.dma_start(convb[sl][:], s.ap()[e, gg][:, hf * 2048:(hf + 1) * 2048]),
                               waits=[cvt_slot[sl]], inc=f"cin{sl}", inc_by=16)
                    tout = E.op("y", lambda y, d=dstd, e=e_, gg=gi, sl=sl, hf=hf:
                                y.dma_start(d.ap()[e, gg][:, hf * 2048:(hf + 1) * 2048], convb[sl][:]),
                                waits=[tin], inc=f"cout{sl}", inc_by=16)
                    cvt_slot[sl] = tout
                    if gi == NG - 1 and hf == 1:
                        done = toks(cvt_slot[0], cvt_slot[1])
                        if which == 0:
                            conv_done_w1[e_] = done
                        else:
                            conv_done_w2[e_] = done
                E.mute = saved

            E.mute = MAX_PHASE < 1
            xnT_tok = [None] * NT
            combT_wr = [None] * NT
            s_wr = [None] * NT
            with contextlib.ExitStack() as c1:
                sb1 = lambda n, sh, dt: c1.enter_context(nc.sbuf_tensor(n, sh, dt))
                xbuf = [sb1(f"xbuf{i}", [128, D], F32) for i in range(2)]
                xnf = [sb1(f"xnf{i}", [128, D], F32) for i in range(2)]
                xtr = [sb1(f"xtr{i}", [128, 128], F32) for i in range(2)]
                xsq = [sb1(f"xsq{i}", [128, D], F32) for i in range(2)]
                lg = [sb1(f"lg{i}", [128, 64], F32) for i in range(2)]
                cts = [sb1(f"cts{i}", [8, 128], BF16) for i in range(2)]

                x_tiles = x_d.ap().rearrange("(n p) d -> n p d", p=128)
                xbuf_free = [None, None]
                xnf_free = [None, None]
                xtr_free = [None, None]
                tp_free = [[tok_lam], None]
                sq_free = [None, None]
                lg_free = [None, None]
                cts_free = [None, None]
                logit_psum_free = None
                prev_sq = [None, None]

                pending_ct = [None]

                def flush_ct():
                    if pending_ct[0] is not None:
                        f = pending_ct[0]
                        pending_ct[0] = None
                        f()

                for i in range(NT):
                    E.mute = (i >= NT_LIM) or (MAX_PHASE < 1)
                    emit_conv(4)
                    flush_ct()
                    bi = i % 2
                    B0 = 32 * bi
                    tok_x = E.op("g", lambda g, i=i, bi=bi: g.dma_start(xbuf[bi][:], x_tiles[i]),
                                 waits=[xbuf_free[bi]], inc=f"x{bi}", inc_by=16)
                    t_s2 = E.op("s", lambda s, bi=bi: s.activation(xsq[bi][:], xbuf[bi][:], AF.Square,
                                                                   accum_out=stat[:, B0 + 0:B0 + 1]),
                                waits=[tok_x, tok_smalls, sq_free[bi], prev_sq[bi]], inc="sS")
                    prev_sq[bi] = t_s2
                    t_s1 = E.op("v", lambda v, bi=bi: v.reduce_sum(stat[:, B0 + 1:B0 + 2], xbuf[bi][:], axis=AX.X),
                                waits=[tok_x], inc="vS")
                    t_nmu = E.op("v", lambda v: v.tensor_scalar(stat[:, B0 + 2:B0 + 3], stat[:, B0 + 1:B0 + 2], cc(C_NINV1024), None, op0=OP.mult),
                                 waits=[t_s1], inc="vS")
                    t_m2 = E.op("v", lambda v: v.tensor_tensor(stat[:, B0 + 3:B0 + 4], stat[:, B0 + 2:B0 + 3], stat[:, B0 + 2:B0 + 3], op=OP.mult),
                                waits=[t_nmu], inc="vS")
                    t_va = E.op("v", lambda v: v.tensor_scalar(stat[:, B0 + 4:B0 + 5], stat[:, B0 + 0:B0 + 1], cc(C_INV1024), None, op0=OP.mult),
                                waits=[t_s2, t_m2], inc="vS")
                    sq_free[bi] = t_va
                    t_vb = E.op("v", lambda v: v.tensor_tensor(stat[:, B0 + 5:B0 + 6], stat[:, B0 + 4:B0 + 5], stat[:, B0 + 3:B0 + 4], op=OP.subtract),
                                waits=[t_va], inc="vS")
                    t_ln = E.op("s", lambda s: s.activation(stat[:, B0 + 6:B0 + 7], stat[:, B0 + 5:B0 + 6], AF.Ln, bias=cc(C_EPS)),
                                waits=[t_vb], inc="sS")
                    t_rs = E.op("s", lambda s: s.activation(stat[:, B0 + 7:B0 + 8], stat[:, B0 + 6:B0 + 7], AF.Exp, scale=cc(C_NEGHALF)),
                                waits=[t_ln], inc="sS")
                    t_xn = E.op("v", lambda v, bi=bi: v.tensor_scalar(xnf[bi][:], xbuf[bi][:], stat[:, B0 + 2:B0 + 3],
                                                                      stat[:, B0 + 7:B0 + 8], op0=OP.add, op1=OP.mult),
                                waits=[t_rs, xnf_free[bi]], inc="vS")
                    xbuf_free[bi] = t_xn

                    if BODY_LIM < 2:
                        E.mute = True
                    last_sc = None
                    last_vc = None
                    t_mm = None
                    for kc in range(KC):
                        tb_ = kc % 2
                        pbank = pb[7] if tb_ == 0 else pb[4]
                        t_tp = E.op("t", lambda t, bi=bi, kc=kc, pk=pbank: t.transpose(
                            pk[:, 0:128], xnf[bi][:, kc * 128:(kc + 1) * 128], idf[:]),
                            waits=[t_xn, tok_smalls, tok_smalls2, tp_free[tb_]], inc="tS")
                        if BODY_LIM < 3:
                            E.mute = True
                        t_sc = E.op("s", lambda s, i=i, kc=kc, pk=pbank: s.copy(
                            xnT[:, kc * T + i * 128: kc * T + (i + 1) * 128], pk[:, 0:128]),
                            waits=[t_tp], inc="sS")
                        if BODY_LIM >= 2 and not (i >= NT_LIM or MAX_PHASE < 1):
                            E.mute = False
                        if BODY_LIM < 4:
                            E.mute = True
                        t_vc = E.op("s", lambda s, kc=kc, pk=pbank: s.copy(xtr[kc % 2][:], pk[:, 0:128]),
                                    waits=[t_tp, xtr_free[kc % 2]], inc="sS")
                        if BODY_LIM >= 2 and not (i >= NT_LIM or MAX_PHASE < 1):
                            E.mute = False
                        tp_free[tb_] = toks(t_sc, t_vc)
                        if BODY_LIM < 5:
                            E.mute = True
                        t_mm = E.op("t", lambda t, kc=kc: t.matmul(
                            pb[5][:, 0:8], xtr[kc % 2][:], wrs[:, kc * NE:(kc + 1) * NE],
                            start=(kc == 0), stop=(kc == KC - 1)),
                            waits=[t_vc, tok_smalls] + ([logit_psum_free] if kc == 0 else []), inc="tS")
                        if BODY_LIM >= 2 and not (i >= NT_LIM or MAX_PHASE < 1):
                            E.mute = False
                        xtr_free[kc % 2] = t_mm
                        last_sc, last_vc = t_sc, t_vc
                    xnT_tok[i] = last_sc
                    xnf_free[bi] = toks(last_sc, last_vc)

                    if BODY_LIM < 6:
                        E.mute = True
                    t_l = E.op("v", lambda v: v.tensor_copy(lg[bi][:, 0:8], pb[5][:, 0:8]),
                               waits=[t_mm, lg_free[bi]], inc="vS")
                    logit_psum_free = t_l
                    t_m1 = E.op("v", lambda v: v.reduce_max(stat[:, B0 + 8:B0 + 9], lg[bi][:, 0:8], axis=AX.X), waits=[t_l], inc="vS")
                    t_k1 = E.op("v", lambda v: v.tensor_scalar(lg[bi][:, 8:16], lg[bi][:, 0:8], stat[:, B0 + 8:B0 + 9], None, op0=OP.is_ge),
                                waits=[t_m1], inc="vS")
                    t_pen = E.op("v", lambda v: v.tensor_scalar(lg[bi][:, 16:24], lg[bi][:, 8:16], cc(C_BIG), None, op0=OP.mult),
                                 waits=[t_k1], inc="vS")
                    t_msk = E.op("v", lambda v: v.tensor_tensor(lg[bi][:, 24:32], lg[bi][:, 0:8], lg[bi][:, 16:24], op=OP.subtract),
                                 waits=[t_pen], inc="vS")
                    t_m2r = E.op("v", lambda v: v.reduce_max(stat[:, B0 + 9:B0 + 10], lg[bi][:, 24:32], axis=AX.X), waits=[t_msk], inc="vS")
                    t_k2 = E.op("v", lambda v: v.tensor_scalar(lg[bi][:, 32:40], lg[bi][:, 24:32], stat[:, B0 + 9:B0 + 10], None, op0=OP.is_ge),
                                waits=[t_m2r], inc="vS")
                    t_d = E.op("v", lambda v: v.tensor_tensor(stat[:, B0 + 10:B0 + 11], stat[:, B0 + 9:B0 + 10], stat[:, B0 + 8:B0 + 9], op=OP.subtract),
                               waits=[t_m2r], inc="vS")
                    t_e = E.op("s", lambda s: s.activation(stat[:, B0 + 11:B0 + 12], stat[:, B0 + 10:B0 + 11], AF.Exp), waits=[t_d], inc="sS")
                    t_e1 = E.op("v", lambda v: v.tensor_scalar(stat[:, B0 + 12:B0 + 13], stat[:, B0 + 11:B0 + 12], cc(C_ONE), None, op0=OP.add),
                                waits=[t_e], inc="vS")
                    t_p1 = E.op("v", lambda v: v.reciprocal(stat[:, B0 + 13:B0 + 14], stat[:, B0 + 12:B0 + 13]), waits=[t_e1], inc="vS")
                    t_p2 = E.op("v", lambda v: v.tensor_tensor(stat[:, B0 + 14:B0 + 15], stat[:, B0 + 11:B0 + 12], stat[:, B0 + 13:B0 + 14], op=OP.mult),
                                waits=[t_p1], inc="vS")
                    t_c1 = E.op("v", lambda v: v.tensor_scalar(lg[bi][:, 40:48], lg[bi][:, 8:16], stat[:, B0 + 13:B0 + 14], None, op0=OP.mult),
                                waits=[t_p1], inc="vS")
                    t_c2 = E.op("v", lambda v: v.tensor_scalar(lg[bi][:, 48:56], lg[bi][:, 32:40], stat[:, B0 + 14:B0 + 15], None, op0=OP.mult),
                                waits=[t_p2, t_k2], inc="vS")
                    t_cb = E.op("v", lambda v, i=i: v.tensor_tensor(
                        comb[:, i * 8:(i + 1) * 8], lg[bi][:, 40:48], lg[bi][:, 48:56], op=OP.add),
                        waits=[t_c1, t_c2], inc="vS")
                    lg_free[bi] = t_cb

                    if BODY_LIM < 7:
                        E.mute = True
                    def do_ct(i=i, t_cb=t_cb):
                        csl = i % 2
                        t_ct = E.op("t", lambda t, i=i: t.transpose(pb[6][0:8, 0:128], comb[:, i * 8:(i + 1) * 8], idf[:]),
                                    waits=[t_cb, tok_smalls2, cts_free[csl]], inc="tS")
                        t_cs = E.op("s", lambda s, csl=csl: s.copy(cts[csl][0:8, :], pb[6][0:8, 0:128]), waits=[t_ct], inc="sS")
                        combT_wr[i] = E.op("s", lambda s, i=i, csl=csl: s.dma_start(
                            combT_d.ap()[:, i * 128:(i + 1) * 128], cts[csl][0:8, :]),
                            waits=[t_cs], inc=f"cbt{csl}", inc_by=16)
                        cts_free[csl] = toks(t_cs, combT_wr[i])
                    pending_ct[0] = do_ct

                E.mute = MAX_PHASE < 1
                flush_ct()
                p1_end = [("vS", E.counts["vS"]), ("sS", E.counts["sS"]), ("tS", E.counts["tS"]),
                          ("cbt0", E.counts["cbt0"]), ("cbt1", E.counts["cbt1"]), ("x0", E.counts["x0"]), ("x1", E.counts["x1"])]
                if debug:
                    E.op("y", lambda y: y.dma_start(dbg["xnT"].ap(), xnT[:]), waits=p1_end, inc="dbq", inc_by=16)
                    E.op("y", lambda y: y.dma_start(dbg["comb"].ap(), comb[:]), waits=p1_end, inc="dbq", inc_by=16)

            E.mute = MAX_PHASE < 2
            with contextlib.ExitStack() as c2:
                sb2 = lambda n, sh, dt: c2.enter_context(nc.sbuf_tensor(n, sh, dt))
                qTb = sb2("qTb", [128, T], BF16)
                kTb = sb2("kTb", [128, 2 * T], BF16)
                vbuf = sb2("vbuf", [128, KB * 129], BF16)
                wqb = [sb2(f"wqb{i}", [128, KC * 128], BF16) for i in range(2)]
                wkb = [sb2(f"wkb{i}", [128, KC * 128], BF16) for i in range(2)]
                wvb = [sb2(f"wvb{i}", [128, KC * 128], BF16) for i in range(2)]
                e1b = [sb2(f"e1b{i}", [128, 512], BF16) for i in range(4)]
                e2b = [sb2(f"e2b{i}", [128, 512], BF16) for i in range(4)]
                oqb = [sb2(f"oqb{i}", [128, 512], F32) for i in range(2)]
                o1sb = sb2("o1sb", [128, 4 * 129], F32)
                o2sb = sb2("o2sb", [128, 4 * 129], F32)
                ot1 = sb2("ot1", [128, 128], F32)
                ot2 = sb2("ot2", [128, 128], F32)
                ojk = sb2("ojk", [128, 128], F32)

                E.barrier(p1_end)
                tok_ones = E.op("v", lambda v: v.memset(vbuf[:], 1.0), waits=p1_end, inc="vS")
                tok_kz = E.op("v", lambda v: v.memset(kTb[:], 0.0), waits=p1_end, inc="vS")

                wq_tok = [None, None]
                wk_tok = [None, None]
                wv_tok = [None, None]
                wq_free = [None, None]

                def load_pair_w(p):
                    sl = p % 2
                    E.op("g", lambda g, p=p, sl=sl: g.dma_start(
                        wqb[sl][:].rearrange("p (k q) -> p k q", k=KC), wq_d.ap()[p].rearrange("k p q -> p k q")),
                        waits=[wq_free[sl]], inc=f"qkv{sl}", inc_by=16)
                    E.op("g", lambda g, p=p, sl=sl: g.dma_start(
                        wkb[sl][:].rearrange("p (k q) -> p k q", k=KC), wk_d.ap()[p].rearrange("k p q -> p k q")), inc=f"qkv{sl}", inc_by=16)
                    tok3 = E.op("g", lambda g, p=p, sl=sl: g.dma_start(
                        wvb[sl][:].rearrange("p (k q) -> p k q", k=KC), wv_d.ap()[p].rearrange("k p q -> p k q")), inc=f"qkv{sl}", inc_by=16)
                    wq_tok[sl] = tok3
                    wk_tok[sl] = tok3
                    wv_tok[sl] = tok3

                load_pair_w(0)

                pending_tp = [None]
                pending_mg = [None]
                sync_mg = [None]

                def do_pending_mg():
                    if pending_mg[0] is not None:
                        f = pending_mg[0]
                        pending_mg[0] = None
                        f()

                def emit_tp():
                    if pending_tp[0] is None:
                        return
                    ob_, mtoks, p_, qb_ = pending_tp[0]
                    pending_tp[0] = None
                    tp_sc = [None, None]
                    for qs in range(4):
                        half = qs % 2
                        t_tp = E.op("t", lambda t, ob=ob_, qs=qs, half=half: t.transpose(
                            pb[2 + half][:, 0:128],
                            oqb[ob][:, qs * 128:(qs + 1) * 128], idf[:]),
                            waits=[mtoks[qs], bankw[2 + half], tp_sc[half]], inc="tS")
                        t_sc = E.op("s", lambda s, p=p_, qb=qb_, qs=qs, half=half: s.copy(
                            oT[:, p * T + qb * 512 + qs * 128: p * T + qb * 512 + (qs + 1) * 128],
                            pb[2 + half][:, 0:128]),
                            waits=[t_tp], inc="sS")
                        tp_sc[half] = t_sc
                        oT_tok.append(t_sc)
                    bankw[2] = [tp_sc[0]]
                    bankw[3] = [tp_sc[1]]
                    oqb_free[ob_] = toks(tp_sc[0], tp_sc[1])

                qT_tok = [None] * QB
                kT_tok = [None] * QB
                v_tok = [None] * KB
                bankw = {b: list(p1_end) for b in range(8)}
                oqb_free = [None, None]
                oT_tok = []

                for p in range(PAIRS):
                    sl = p % 2
                    for tq in range(QB):
                        bank = tq % 2
                        t_mm = None
                        for kc in range(KC):
                            t_mm = E.op("t", lambda t, kc=kc, tq=tq, sl=sl, bank=bank: t.matmul(
                                pb[bank][:, 0:512], wqb[sl][:, kc * 128:(kc + 1) * 128],
                                xnT[:, kc * T + tq * 512: kc * T + (tq + 1) * 512],
                                start=(kc == 0), stop=(kc == KC - 1)),
                                waits=[wq_tok[sl], xnT_tok[tq * 4 + 3], bankw[bank]] if kc == 0 else [],
                                inc="tS" if kc == KC - 1 else None)
                        t_ev = E.op("s", lambda s, tq=tq, bank=bank: s.activation(
                            qTb[:, tq * 512:(tq + 1) * 512], pb[bank][:, 0:512], AF.Copy, scale=cc(C_EIGHTH)),
                            waits=[t_mm, tok_smalls], inc="sS")
                        qT_tok[tq] = t_ev
                        bankw[bank] = [t_ev]
                    for tq in range(QB):
                        bank = 2 + tq % 2
                        t_mm = None
                        for kc in range(KC):
                            t_mm = E.op("t", lambda t, kc=kc, tq=tq, sl=sl, bank=bank: t.matmul(
                                pb[bank][:, 0:512], wkb[sl][:, kc * 128:(kc + 1) * 128],
                                xnT[:, kc * T + tq * 512: kc * T + (tq + 1) * 512],
                                start=(kc == 0), stop=(kc == KC - 1)),
                                waits=[wk_tok[sl], xnT_tok[tq * 4 + 3], bankw[bank]] if kc == 0 else [],
                                inc="tS" if kc == KC - 1 else None)
                        t_e1 = E.op("s", lambda s, tq=tq, bank=bank: s.copy(
                            kTb[0:64, tq * 512:(tq + 1) * 512], pb[bank][0:64, 0:512]),
                            waits=[t_mm, tok_kz], inc="sS")
                        t_e2 = E.op("s", lambda s, tq=tq, bank=bank: s.copy(
                            kTb[64:128, T + tq * 512: T + (tq + 1) * 512], pb[bank][64:128, 0:512]),
                            waits=[t_e1], inc="sS")
                        kT_tok[tq] = t_e2
                        bankw[bank] = [t_e2]
                    for tv in range(KB):
                        bank = 4 + tv % 2
                        t_mm = None
                        for kc in range(KC):
                            t_mm = E.op("t", lambda t, kc=kc, tv=tv, sl=sl, bank=bank: t.matmul(
                                pb[bank][:, 0:128], xnT[:, kc * T + tv * 128: kc * T + (tv + 1) * 128],
                                wvb[sl][:, kc * 128:(kc + 1) * 128],
                                start=(kc == 0), stop=(kc == KC - 1)),
                                waits=[wv_tok[sl], tok_ones, xnT_tok[tv // 4 * 4 + 3], bankw[bank]] if kc == 0 else [],
                                inc="tS" if kc == KC - 1 else None)
                        t_ev = E.op("s", lambda s, tv=tv, bank=bank: s.copy(
                            vbuf[:, tv * 129: tv * 129 + 128], pb[bank][:, 0:128]),
                            waits=[t_mm], inc="sS")
                        v_tok[tv] = t_ev
                        bankw[bank] = [t_ev]
                    if p + 1 < PAIRS:
                        wq_free[(p + 1) % 2] = [("tS", E.counts["tS"])]
                        load_pair_w(p + 1)

                    for qb in range(QB):
                        emit_conv(2)
                        o1_evac = []
                        pv_last = None
                        for stream in (1, 2):
                            ebuf = e1b if stream == 1 else e2b
                            plo = 0 if stream == 1 else 64
                            e_free = [None, None, None, None]
                            pend = []

                            def emit_pv(kb_, eb_, tx_):
                                t_pv = None
                                for qs in range(4):
                                    bnk = 4 + qs
                                    w = [tx_, v_tok[kb_]]
                                    if kb_ == 0:
                                        w.append(bankw[bnk])
                                    t_pv = E.op("t", lambda t, kb=kb_, eb=eb_, qs=qs, ebuf=ebuf, bnk=bnk: t.matmul(
                                        pb[bnk][:, 0:129], ebuf[eb][:, qs * 128:(qs + 1) * 128],
                                        vbuf[:, kb * 129:(kb + 1) * 129],
                                        start=(kb == 0), stop=(kb == KB - 1)),
                                        waits=w, inc="tS" if qs == 3 else None)
                                e_free[eb_] = t_pv
                                return t_pv

                            for kb in range(KB):
                                eb = kb % 4
                                sb_ = kb % 4
                                coff = 0 if stream == 1 else T
                                t_s = E.op("t", lambda t, kb=kb, qb=qb, sb_=sb_, coff=coff: t.matmul(
                                    pb[sb_][:, 0:512], kTb[:, coff + kb * 128: coff + (kb + 1) * 128],
                                    qTb[:, qb * 512:(qb + 1) * 512],
                                    start=True, stop=True),
                                    waits=[qT_tok[qb], kT_tok[kb // 4], bankw[sb_]], inc="tS")
                                t_x = E.op("s", lambda s, eb=eb, sb_=sb_, ebuf=ebuf: s.activation(
                                    ebuf[eb][:], pb[sb_][:, 0:512], AF.Exp, bias=cc(C_NEG12)),
                                    waits=[t_s, e_free[eb]], inc="sS")
                                bankw[sb_] = [t_x]
                                pend.append((kb, eb, t_x))
                                if len(pend) > 2:
                                    t_pv = emit_pv(*pend.pop(0))
                                if kb == 10 and stream == 1:
                                    do_pending_mg()
                            while pend:
                                t_pv = emit_pv(*pend.pop(0))
                            pv_last = t_pv
                            if stream == 2:
                                o2_evac = []
                                for qs in range(4):
                                    t_e2 = E.op("v", lambda v, qs=qs: v.tensor_copy(
                                        o2sb[:, qs * 129:(qs + 1) * 129], pb[4 + qs][:, 0:129]),
                                        waits=[pv_last, sync_mg[0]], inc="vS")
                                    o2_evac.append(t_e2)
                                for qs in range(4):
                                    bankw[4 + qs] = [o2_evac[qs]]
                            if stream == 1:
                                for qs in range(4):
                                    t_e1 = E.op("v", lambda v, qs=qs: v.tensor_copy(
                                        o1sb[:, qs * 129:(qs + 1) * 129], pb[4 + qs][:, 0:129]),
                                        waits=[pv_last, sync_mg[0]], inc="vS")
                                    o1_evac.append(t_e1)
                                for qs in range(4):
                                    bankw[4 + qs] = [o1_evac[qs]]
                                emit_tp()

                        ob = qb % 2
                        def do_merges(ob=ob, qb=qb, p=p, o1e=tuple(o1_evac), o2e=tuple(o2_evac)):
                            merge_toks = []
                            for qs in range(4):
                                w = [o1e[qs], o2e[qs], tok_lam] + ([oqb_free[ob]] if qs == 0 else [])
                                t_r1 = E.op("v", lambda v, qs=qs: v.reciprocal(
                                    stat[:, 16:17], o1sb[:, qs * 129 + 128:qs * 129 + 129]),
                                    waits=w, inc="vS")
                                t_r2 = E.op("v", lambda v, qs=qs: v.reciprocal(
                                    stat[:, 17:18], o2sb[:, qs * 129 + 128:qs * 129 + 129]),
                                    waits=[t_r1], inc="vS")
                                t_rl = E.op("v", lambda v: v.tensor_scalar(stat[:, 18:19], stat[:, 17:18], lamc[:], None, op0=OP.mult),
                                            waits=[t_r2], inc="vS")
                                t_t1 = E.op("v", lambda v, qs=qs: v.tensor_scalar(
                                    ot1[:], o1sb[:, qs * 129:qs * 129 + 128], stat[:, 16:17], None, op0=OP.mult),
                                    waits=[t_r1], inc="vS")
                                t_t2 = E.op("v", lambda v, qs=qs: v.tensor_scalar(
                                    ot2[:], o2sb[:, qs * 129:qs * 129 + 128], stat[:, 18:19], None, op0=OP.mult),
                                    waits=[t_rl], inc="vS")
                                t_om = E.op("v", lambda v, ob=ob, qs=qs: v.tensor_tensor(
                                    oqb[ob][:, qs * 128:(qs + 1) * 128], ot1[:], ot2[:], op=OP.subtract),
                                    waits=[t_t1, t_t2], inc="vS")
                                t_sq = E.op("v", lambda v, ob=ob, qs=qs: v.tensor_tensor(
                                    ot1[:], oqb[ob][:, qs * 128:(qs + 1) * 128],
                                    oqb[ob][:, qs * 128:(qs + 1) * 128], op=OP.mult),
                                    waits=[t_om], inc="vS")
                                t_ss = E.op("v", lambda v: v.reduce_sum(stat[:, 19:20], ot1[:], axis=AX.X),
                                            waits=[t_sq], inc="vS")
                                t_s1m = E.op("v", lambda v, ob=ob, qs=qs: v.reduce_sum(
                                    stat[:, 20:21], oqb[ob][:, qs * 128:(qs + 1) * 128], axis=AX.X),
                                    waits=[t_om], inc="vS")
                                t_nmu = E.op("v", lambda v: v.tensor_scalar(stat[:, 21:22], stat[:, 20:21], cc(C_NINV128), None, op0=OP.mult),
                                             waits=[t_s1m], inc="vS")
                                t_mm2 = E.op("v", lambda v: v.tensor_tensor(stat[:, 22:23], stat[:, 21:22], stat[:, 21:22], op=OP.mult),
                                             waits=[t_nmu], inc="vS")
                                t_va = E.op("v", lambda v: v.tensor_scalar(stat[:, 23:24], stat[:, 19:20], cc(C_INV128), None, op0=OP.mult),
                                            waits=[t_ss], inc="vS")
                                t_vb = E.op("v", lambda v: v.tensor_tensor(stat[:, 24:25], stat[:, 23:24], stat[:, 22:23], op=OP.subtract),
                                            waits=[t_va, t_mm2], inc="vS")
                                t_ln = E.op("s", lambda s: s.activation(stat[:, 25:26], stat[:, 24:25], AF.Ln, bias=cc(C_EPS)),
                                            waits=[t_vb], inc="sS")
                                t_rsd = E.op("s", lambda s: s.activation(stat[:, 26:27], stat[:, 25:26], AF.Exp, scale=cc(C_NEGHALF)),
                                             waits=[t_ln], inc="sS")
                                t_on = E.op("v", lambda v, ob=ob, qs=qs: v.tensor_scalar(
                                    ot1[:], oqb[ob][:, qs * 128:(qs + 1) * 128], stat[:, 21:22], stat[:, 26:27],
                                    op0=OP.add, op1=OP.mult),
                                    waits=[t_rsd], inc="vS")
                                t_lg = E.op("v", lambda v: v.tensor_tensor(ot2[:], ot1[:], lngt[:], op=OP.mult),
                                            waits=[t_on, tok_smalls2], inc="vS")
                                t_fin = E.op("v", lambda v, ob=ob, qs=qs: v.tensor_tensor(
                                    oqb[ob][:, qs * 128:(qs + 1) * 128], ot2[:], lnbt[:], op=OP.add),
                                    waits=[t_lg], inc="vS")
                                merge_toks.append(t_fin)
                            sync_mg[0] = toks(*merge_toks)
                            pending_tp[0] = (ob, merge_toks, p, qb)
                        pending_mg[0] = do_merges
                    do_pending_mg()
                    emit_tp()

                p2_end = [("tS", E.counts["tS"]), ("sS", E.counts["sS"]), ("vS", E.counts["vS"]),
                          ("qkv0", E.counts["qkv0"]), ("qkv1", E.counts["qkv1"])]
                if debug:
                    E.op("y", lambda y: y.dma_start(dbg["oT"].ap(), oT[:]), waits=p2_end, inc="dbq", inc_by=16)

            E.mute = MAX_PHASE < 3
            with contextlib.ExitStack() as c3:
                sb3 = lambda n, sh, dt: c3.enter_context(nc.sbuf_tensor(n, sh, dt))
                wob = sb3("wob", [128, KC * D], BF16)
                sst = [sb3(f"sst{i}", [128, D], BF16) for i in range(2)]

                emit_conv(10**6)
                E.barrier(p2_end)
                tok_wo = E.op("g", lambda g: g.dma_start(wob[:].rearrange("p (k d) -> p k d", k=KC), wo_d.ap().rearrange("k p d -> p k d")),
                              waits=p2_end, inc="woq", inc_by=16)
                sst_free = [None, None]
                pbw = [list(p2_end), list(p2_end)]
                for i in range(NT):
                    bi = i % 2
                    t_ev_last = None
                    for dc in range(2):
                        bank = dc
                        t_mm = None
                        for kc in range(KC):
                            t_mm = E.op("t", lambda t, kc=kc, i=i, dc=dc, bank=bank: t.matmul(
                                pb[bank][:, 0:512], oT[:, kc * T + i * 128: kc * T + (i + 1) * 128],
                                wob[:, kc * D + dc * 512: kc * D + dc * 512 + 512],
                                start=(kc == 0), stop=(kc == KC - 1)),
                                waits=[tok_wo, pbw[bank]] if kc == 0 else [],
                                inc="tS" if kc == KC - 1 else None)
                        t_ev = E.op("s", lambda s, dc=dc, bi=bi, bank=bank: s.copy(
                            sst[bi][:, dc * 512:dc * 512 + 512], pb[bank][:, 0:512]),
                            waits=[t_mm] + ([sst_free[bi]] if dc == 0 else []), inc="sS")
                        pbw[bank] = [t_ev]
                        t_ev_last = t_ev
                    s_wr[i] = E.op("s", lambda s, i=i, bi=bi: s.dma_start(
                        ssc_d.ap()[i * 128:(i + 1) * 128, :], sst[bi][:]),
                        waits=[t_ev_last], inc=f"sw{bi}", inc_by=16)
                    sst_free[bi] = s_wr[i]
                    if debug:
                        tdbg = E.op("s", lambda s, i=i, bi=bi: s.dma_start(
                            dbg["s"].ap()[i * 128:(i + 1) * 128, :], sst[bi][:]),
                            waits=[t_ev_last], inc=f"sw{bi}", inc_by=16)
                        sst_free[bi] = toks(s_wr[i], tdbg)

            p3_end = [("tS", E.counts["tS"]), ("sS", E.counts["sS"]), ("vS", E.counts["vS"]),
                      ("woq", E.counts["woq"]), ("sw0", E.counts["sw0"]), ("sw1", E.counts["sw1"]),
                      ("cin0", E.counts["cin0"]), ("cin1", E.counts["cin1"]),
                      ("cout0", E.counts["cout0"]), ("cout1", E.counts["cout1"]),
                      ("sm2", E.counts["sm2"]), ("dbq", E.counts["dbq"])]

        E.mute = MAX_PHASE < 4
        with contextlib.ExitStack() as c4:
            sb4 = lambda n, sh, dt: c4.enter_context(nc.sbuf_tensor(n, sh, dt))
            hbuf = sb4("hbuf", [128, HC * 512], BF16)
            w1g = [sb4(f"w1g{i}", [128, 4096], BF16) for i in range(2)]
            w2g = [sb4(f"w2g{i}", [128, 4096], BF16) for i in range(2)]
            gacc = sb4("gacc", [128, 4 * D], F32)
            stb = sb4("stb", [128, 4 * D], BF16)
            xtb = sb4("xtb", [128, 4 * D], F32)
            ostg = [sb4(f"ostg{i}", [128, D], F32) for i in range(2)]
            ytmp = [sb4(f"ytmp{i}", [128, 512], F32) for i in range(2)]
            gtmp = sb4("gtmp", [128, D], F32)
            cts4 = sb4("cts4", [8, T], BF16)

            x_tb_ap = x_d.ap().rearrange("(tb ts p) d -> tb p ts d", ts=4, p=128)
            s_tb_ap = ssc_d.ap().rearrange("(tb ts p) d -> tb p ts d", ts=4, p=128)
            xtb_v = xtb[:].rearrange("p (ts d) -> p ts d", ts=4)
            stb_v = stb[:].rearrange("p (ts d) -> p ts d", ts=4)

            E.barrier(p3_end)
            tok_ct4 = E.op("y", lambda y: y.dma_start(cts4[:], combT_d.ap()),
                           waits=[toks(*combT_wr), p3_end], inc="ctq", inc_by=16)

            w1_tok = [None] * 2
            w2_tok = [None] * 2
            w1_use = [list(p3_end) for _ in range(2)]
            w2_use = [list(p3_end) for _ in range(2)]
            pbw4 = {i: list(p3_end) for i in range(8)}

            def dma_w1(e, g):
                sl = g % 2
                w1_tok[sl] = E.op("y", lambda y, e=e, g=g, sl=sl: y.dma_start(
                    w1g[sl][:], wsc1_d.ap()[e, g]),
                    waits=[w1_use[sl], conv_done_w1[e]], inc=f"w1s{sl}", inc_by=16)

            def dma_w2(e, g):
                sl = g % 2
                w2_tok[sl] = E.op("y", lambda y, e=e, g=g, sl=sl: y.dma_start(
                    w2g[sl][:], wsc2_d.ap()[e, g]),
                    waits=[w2_use[sl], conv_done_w2[e]], inc=f"w2s{sl}", inc_by=16)

            gacc_free = None
            stb_free = None
            xtb_free = None
            ostg_free = [None, None]
            ytmp_free = [None, None]
            gtmp_free = None
            out_tok = []

            for tb in range(NTB):
                xtb_tok = E.op("g", lambda g, tb=tb: g.dma_start(xtb_v, x_tb_ap[tb]),
                               waits=[xtb_free], inc="xtq", inc_by=16)
                stb_tok = E.op("y", lambda y, tb=tb: y.dma_start(stb_v, s_tb_ap[tb]),
                               waits=[stb_free] + [s_wr[tb * 4 + j] for j in range(4)], inc="stq", inc_by=16)

                ginit = []
                for ts4 in range(4):
                    for dc in range(2):
                        bank = 6 + (ts4 * 2 + dc) % 2
                        t_mm = E.op("t", lambda t, tb=tb, ts4=ts4, dc=dc, bank=bank: t.matmul(
                            pb[bank][:, 0:512], cts4[0:8, tb * 512 + ts4 * 128: tb * 512 + (ts4 + 1) * 128],
                            b2s[0:8, dc * 512:dc * 512 + 512], start=True, stop=True),
                            waits=[tok_ct4, tok_smalls, pbw4[bank]], inc="tS")
                        t_cp = E.op("s", lambda s, ts4=ts4, dc=dc, bank=bank: s.copy(
                            gacc[:, ts4 * D + dc * 512: ts4 * D + dc * 512 + 512], pb[bank][:, 0:512]),
                            waits=[t_mm] + ([gacc_free] if (ts4 == 0 and dc == 0) else []), inc="sS")
                        pbw4[bank] = [t_cp]
                        ginit.append(t_cp)
                gacc_ready = toks(*ginit)

                for e in range(NE):
                    if e == 0 and tb == 0:
                        for g0 in range(2):
                            dma_w1(e, g0)
                        for g0 in range(2):
                            dma_w2(e, g0)

                    gelu_toks = []
                    for hc in range(HC):
                        g0 = hc // 4
                        sl = g0 % 2
                        bank = hc % 2
                        t_mm = None
                        for kc in range(KC):
                            t_mm = E.op("t", lambda t, hc=hc, kc=kc, tb=tb, sl=sl, bank=bank: t.matmul(
                                pb[bank][:, 0:512],
                                w1g[sl][:, (hc % 4) * 1024 + kc * 128: (hc % 4) * 1024 + (kc + 1) * 128],
                                xnT[:, kc * T + tb * 512: kc * T + (tb + 1) * 512],
                                start=(kc == 0), stop=(kc == KC - 1)),
                                waits=[w1_tok[sl], xnT_tok[NT - 1], pbw4[bank]] if kc == 0 else [],
                                inc="tS" if kc == KC - 1 else None)
                        if hc % 4 == 3:
                            w1_use[sl] = [t_mm]
                            gnext = g0 + 2
                            if gnext < NG:
                                dma_w1(e, gnext)
                            elif e + 1 < NE:
                                dma_w1(e + 1, gnext - NG)
                            elif tb + 1 < NTB:
                                dma_w1(0, gnext - NG)
                        t_ge = E.op("s", lambda s, hc=hc, e=e, bank=bank: s.activation(
                            hbuf[:, hc * 512:(hc + 1) * 512], pb[bank][:, 0:512], AF.Gelu,
                            bias=b1s[:, e * HC + hc: e * HC + hc + 1]),
                            waits=[t_mm, tok_smalls], inc="sS")
                        pbw4[bank] = [t_ge]
                        gelu_toks.append(t_ge)

                    t_mm2_last = {}
                    for hc in range(HC):
                        g0 = hc // 4
                        sl = g0 % 2
                        t_mm = None
                        for ts4 in range(4):
                            for dc in range(2):
                                bank = ts4 * 2 + dc
                                w = [gelu_toks[hc], w2_tok[sl]]
                                if hc == 0:
                                    w.append(pbw4[bank])
                                    w.append(gacc_ready)
                                give_inc = (hc == HC - 1) or (hc % 4 == 3 and ts4 == 3 and dc == 1)
                                t_mm = E.op("t", lambda t, hc=hc, ts4=ts4, dc=dc, sl=sl, bank=bank: t.matmul(
                                    pb[bank][:, 0:512],
                                    hbuf[:, hc * 512 + ts4 * 128: hc * 512 + (ts4 + 1) * 128],
                                    w2g[sl][:, (hc % 4) * 1024 + dc * 512: (hc % 4) * 1024 + dc * 512 + 512],
                                    start=(hc == 0), stop=(hc == HC - 1)),
                                    waits=w, inc="tS" if give_inc else None)
                                if hc == HC - 1:
                                    t_mm2_last[bank] = t_mm
                        if hc % 4 == 3:
                            w2_use[sl] = [t_mm]
                            gnext = g0 + 2
                            if gnext < NG:
                                dma_w2(e, gnext)
                            elif e + 1 < NE:
                                dma_w2(e + 1, gnext - NG)
                            elif tb + 1 < NTB:
                                dma_w2(0, gnext - NG)

                    new_ready = list(gacc_ready)
                    for bank in range(8):
                        ts4, dc = bank // 2, bank % 2
                        yb = bank % 2
                        ci = (tb * 4 + ts4) * 8 + e
                        t_sc = E.op("s", lambda s, bank=bank, ci=ci, yb=yb: s.activation(
                            ytmp[yb][:], pb[bank][:, 0:512], AF.Copy, scale=comb[:, ci:ci + 1]),
                            waits=[t_mm2_last[bank], ytmp_free[yb]], inc="sS")
                        t_ad = E.op("v", lambda v, ts4=ts4, dc=dc, yb=yb: v.tensor_tensor(
                            gacc[:, ts4 * D + dc * 512: ts4 * D + dc * 512 + 512],
                            gacc[:, ts4 * D + dc * 512: ts4 * D + dc * 512 + 512],
                            ytmp[yb][:], op=OP.add),
                            waits=[t_sc, gacc_ready], inc="vS")
                        ytmp_free[yb] = t_ad
                        pbw4[bank] = toks(t_sc, t_ad)
                        new_ready.append(t_ad)
                    gacc_ready = toks(*new_ready)

                tail_v = []
                t_si_last = None
                for ts4 in range(4):
                    ob = ts4 % 2
                    t_si = E.op("s", lambda s, ts4=ts4: s.activation(
                        gtmp[:], gacc[:, ts4 * D:(ts4 + 1) * D], AF.Silu),
                        waits=[gacc_ready, gtmp_free], inc="sS")
                    t_mu = E.op("v", lambda v, ts4=ts4: v.tensor_tensor(
                        gtmp[:], gtmp[:], stb[:, ts4 * D:(ts4 + 1) * D], op=OP.mult),
                        waits=[t_si, stb_tok], inc="vS")
                    t_ou = E.op("v", lambda v, ts4=ts4, ob=ob: v.tensor_tensor(
                        ostg[ob][:], gtmp[:], xtb[:, ts4 * D:(ts4 + 1) * D], op=OP.add),
                        waits=[t_mu, xtb_tok, ostg_free[ob]], inc="vS")
                    t_wr = E.op("g", lambda g, tb=tb, ts4=ts4, ob=ob: g.dma_start(
                        out_d.ap()[(tb * 4 + ts4) * 128:(tb * 4 + ts4 + 1) * 128, :], ostg[ob][:]),
                        waits=[t_ou], inc=f"ow{ob}", inc_by=16)
                    ostg_free[ob] = t_wr
                    gtmp_free = t_ou
                    t_si_last = t_si
                    tail_v.append(t_ou)
                    out_tok.append(t_wr)
                gacc_free = toks(t_si_last)
                xtb_free = toks(*tail_v)
                stb_free = toks(*tail_v)

        E.mute = False
        fin = toks(*out_tok) + [("dbq", E.counts["dbq"]), ("sw0", E.counts["sw0"]),
                                ("sw1", E.counts["sw1"]), ("cbt0", E.counts["cbt0"]),
                                ("cbt1", E.counts["cbt1"])]
        for sem, val in toks(*[t for t in fin if t[1] > 0]):
            h = E.sems[sem]
            E.ops["y"].append(lambda e, h=h, v=val: e.wait_ge(h, v))

        block = ctx.enter_context(nc.Block())
        E.emit(block)

    return nc


def _prep_core_inputs(x_core, inp):
    W1 = inp["L0_ffn_W1"]; W2 = inp["L0_ffn_W2"]
    w1t = np.ascontiguousarray(
        W1.reshape(NE, KC, 128, HC, 128)
          .transpose(0, 3, 2, 1, 4)
          .reshape(NE, NG, 4, 128, KC, 128)
          .transpose(0, 1, 3, 2, 4, 5)
          .reshape(NE, NG, 128, 4096))
    w2t = np.ascontiguousarray(
        W2.reshape(NE, NG, 4, 128, D)
          .transpose(0, 1, 3, 2, 4)
          .reshape(NE, NG, 128, 4096))
    def qk_t(W):
        return np.ascontiguousarray(W.reshape(KC, 128, PAIRS, 128).transpose(2, 0, 1, 3))
    consts = np.zeros((128, NCONST), np.float32)
    consts[:, C_NEG12] = -12.0
    consts[:, C_EIGHTH] = 0.125
    consts[:, C_NINV1024] = -1.0 / 1024
    consts[:, C_INV1024] = 1.0 / 1024
    consts[:, C_NINV128] = -1.0 / 128
    consts[:, C_INV128] = 1.0 / 128
    consts[:, C_EPS] = 1e-5
    consts[:, C_NEGHALF] = -0.5
    consts[:, C_ONE] = 1.0
    consts[:, C_BIG] = 1e30
    consts[:, C_LAMI] = LAM_INIT
    lrow = np.concatenate([inp["L0_attn_lq1"], inp["L0_attn_lk1"],
                           inp["L0_attn_lq2"], inp["L0_attn_lk2"]]).reshape(1, 256).astype(np.float32)
    return dict(
        x=np.ascontiguousarray(x_core),
        wq_t=qk_t(inp["L0_attn_Wq"]),
        wk_t=qk_t(inp["L0_attn_Wk"]),
        wv_t=qk_t(inp["L0_attn_Wv"]),
        wo_t=np.ascontiguousarray(inp["L0_attn_Wo"].reshape(KC, 128, D)),
        wr_t=np.ascontiguousarray(inp["L0_ffn_Wr"].reshape(KC, 128, NE)),
        w1_t=w1t, w2_t=w2t,
        b1_t=np.ascontiguousarray(inp["L0_ffn_b1"].reshape(NE, HC, 128).transpose(2, 0, 1).reshape(128, NE * HC)),
        b2_t=np.ascontiguousarray(inp["L0_ffn_b2"]),
        lng_t=np.ascontiguousarray(np.tile(inp["L0_attn_ln_g"].reshape(1, 128), (128, 1))),
        lnb_t=np.ascontiguousarray(np.tile(inp["L0_attn_ln_b"].reshape(1, 128), (128, 1))),
        lrow=lrow,
        consts=consts,
        idf=np.eye(128, dtype=np.float32),
        onesr=np.ones((1, 128), np.float32),
    )


def kernel(**inputs):
    x = np.asarray(inputs["L0_attn_x"], np.float32)
    inp = {k: np.asarray(v, np.float32) for k, v in inputs.items()}
    nc = build_program(debug=DEBUG)
    in_maps = [_prep_core_inputs(x[c], inp) for c in range(N_CORES)]
    res = run_bass_kernel_spmd(nc, in_maps, core_ids=list(range(N_CORES)))
    out = np.stack([np.asarray(res.results[c]["out"], np.float32) for c in range(N_CORES)])
    return out


# version 08
# speedup vs baseline: 1.1950x; 1.0786x over previous
import contextlib
import numpy as np
import ml_dtypes

import concourse.bass as bass
import concourse.mybir as mybir
from concourse.bass_utils import run_bass_kernel_spmd

F32 = mybir.dt.float32
BF16 = mybir.dt.bfloat16
AF = mybir.ActivationFunctionType
OP = mybir.AluOpType
AX = mybir.AxisListType
bf16 = ml_dtypes.bfloat16

T = 4096
D = 1024
NT = 32
KC = 8
PAIRS = 8
QB = 8
KB = 32
NE = 8
HC = 32
NG = 8
NTB = 8
LAM_INIT = float(0.8 - 0.6 * np.exp(-0.3 * 12))

(C_NEG12, C_EIGHTH, C_NINV1024, C_INV1024, C_NINV128, C_INV128, C_EPS,
 C_NEGHALF, C_ONE, C_BIG, C_LAMI) = range(11)
NCONST = 16

DEBUG = False
MAX_PHASE = 4
NT_LIM = NT
BODY_LIM = 7
N_CORES = 8


def _flat(waits):
    out = []
    for w in waits:
        if w is None:
            continue
        if isinstance(w, list):
            out.extend(t for t in w if t is not None)
        else:
            out.append(w)
    return out


def toks(*ws):
    best = {}
    for t in _flat(ws):
        s, v = t
        if best.get(s, -1) < v:
            best[s] = v
    return [(s, v) for s, v in best.items()]


class Emitter:
    ENGS = ("t", "v", "s", "g", "y")

    def __init__(self):
        self.mute = False
        self.ops = {e: [] for e in self.ENGS}
        self.counts = {}
        self.watermark = {}
        self.sems = {}

    def add_sem(self, name, handle):
        self.sems[name] = handle
        self.counts[name] = 0

    def op(self, eng, fn, waits=(), inc=None, inc_by=1):
        if self.mute:
            return (inc, self.counts[inc]) if inc else None
        for sem, val in toks(*waits):
            assert val <= self.counts[sem], f"wait on future: {sem} {val} > {self.counts[sem]}"
            key = (eng, sem)
            if self.watermark.get(key, 0) >= val:
                continue
            self.watermark[key] = val
            h = self.sems[sem]
            self.ops[eng].append(lambda e, h=h, v=val: e.wait_ge(h, v))
        if inc is None:
            self.ops[eng].append(fn)
            return None
        h = self.sems[inc]
        def wrapped(e, fn=fn, h=h, n=inc_by):
            fn(e).then_inc(h, n)
        self.ops[eng].append(wrapped)
        self.counts[inc] += inc_by
        return (inc, self.counts[inc])

    def barrier(self, waits):
        if self.mute:
            return
        for eng in self.ENGS:
            for sem, val in toks(*waits):
                key = (eng, sem)
                if self.watermark.get(key, 0) >= val:
                    continue
                self.watermark[key] = val
                h = self.sems[sem]
                self.ops[eng].append(lambda e, h=h, v=val: e.wait_ge(h, v))

    def emit(self, block):
        ops = self.ops
        @block.tensor
        def _(t):
            for f in ops["t"]: f(t)
        @block.vector
        def _(v):
            for f in ops["v"]: f(v)
        @block.scalar
        def _(s):
            for f in ops["s"]: f(s)
        @block.gpsimd
        def _(g):
            for f in ops["g"]: f(g)
        @block.sync
        def _(y):
            for f in ops["y"]: f(y)


def build_program(debug=False):
    nc = bass.Bass("TRN2", target_bir_lowering=False)

    x_d = nc.dram_tensor("x", [T, D], F32, kind="ExternalInput")
    wq_d = nc.dram_tensor("wq_t", [PAIRS, KC, 128, 128], F32, kind="ExternalInput")
    wk_d = nc.dram_tensor("wk_t", [PAIRS, KC, 128, 128], F32, kind="ExternalInput")
    wv_d = nc.dram_tensor("wv_t", [PAIRS, KC, 128, 128], F32, kind="ExternalInput")
    wo_d = nc.dram_tensor("wo_t", [KC, 128, D], F32, kind="ExternalInput")
    wr_d = nc.dram_tensor("wr_t", [KC, 128, NE], F32, kind="ExternalInput")
    w1_d = nc.dram_tensor("w1_t", [NE, NG, 128, 4096], F32, kind="ExternalInput")
    w2_d = nc.dram_tensor("w2_t", [NE, NG, 128, 4096], F32, kind="ExternalInput")
    b1_d = nc.dram_tensor("b1_t", [128, NE * HC], F32, kind="ExternalInput")
    b2_d = nc.dram_tensor("b2_t", [NE, D], F32, kind="ExternalInput")
    lng_d = nc.dram_tensor("lng_t", [128, 128], F32, kind="ExternalInput")
    lnb_d = nc.dram_tensor("lnb_t", [128, 128], F32, kind="ExternalInput")
    lrow_d = nc.dram_tensor("lrow", [1, 256], F32, kind="ExternalInput")
    consts_d = nc.dram_tensor("consts", [128, NCONST], F32, kind="ExternalInput")
    idf_d = nc.dram_tensor("idf", [128, 128], F32, kind="ExternalInput")
    onesr_d = nc.dram_tensor("onesr", [1, 128], F32, kind="ExternalInput")

    out_d = nc.dram_tensor("out", [T, D], F32, kind="ExternalOutput")

    wsc1_d = nc.dram_tensor("wsc1", [NE, NG, 128, 4096], BF16, kind="Internal")
    wsc2_d = nc.dram_tensor("wsc2", [NE, NG, 128, 4096], BF16, kind="Internal")
    ssc_d = nc.dram_tensor("ssc", [T, D], BF16, kind="Internal")
    combT_d = nc.dram_tensor("combT_sc", [8, T], BF16, kind="Internal")

    dbg = {}
    if debug:
        dbg["xnT"] = nc.dram_tensor("dbg_xnT", [128, KC * T], BF16, kind="ExternalOutput")
        dbg["comb"] = nc.dram_tensor("dbg_comb", [128, NT * 8], F32, kind="ExternalOutput")
        dbg["lam"] = nc.dram_tensor("dbg_lam", [128, 1], F32, kind="ExternalOutput")
        dbg["s"] = nc.dram_tensor("dbg_s", [T, D], BF16, kind="ExternalOutput")
        dbg["oT"] = nc.dram_tensor("dbg_oT", [128, PAIRS * T], BF16, kind="ExternalOutput")

    E = Emitter()

    with contextlib.ExitStack() as ctx:
        sbuf = lambda n, sh, dt: ctx.enter_context(nc.sbuf_tensor(n, sh, dt))
        psum = lambda n, sh, dt: ctx.enter_context(nc.psum_tensor(n, sh, dt))
        _sems = ["tS", "sS", "vS", "sm1", "sm2", "cbt0", "cbt1", "woq", "stq", "xtq", "ctq", "dbq",
                 "cin0", "cin1", "cout0", "cout1", "x0", "x1", "qkv0", "qkv1",
                 "sw0", "sw1", "ow0", "ow1",
                 "w1s0", "w1s1", "w1s2", "w1s3", "w2s0", "w2s1", "w2s2", "w2s3"]
        for sname in _sems:
            E.add_sem(sname, ctx.enter_context(nc.semaphore(sname)))

        pb = [psum(f"pb{i}", [128, 512], F32) for i in range(8)]

        xnT = sbuf("xnT", [128, KC * T], BF16)
        comb = sbuf("comb", [128, NT * 8], F32)
        b1s = sbuf("b1s", [128, NE * HC], F32)
        b2s = sbuf("b2s", [NE, D], BF16)
        wrs = sbuf("wrs", [128, KC * NE], F32)
        cst = sbuf("cst", [128, NCONST], F32)
        lamc = sbuf("lamc", [128, 1], F32)
        lrow = sbuf("lrow_s", [1, 256], F32)
        onesr = sbuf("onesr_s", [1, 128], F32)
        lprod = sbuf("lprod", [1, 128], F32)
        l11 = sbuf("l11", [1, 16], F32)
        stat = sbuf("stat", [128, 64], F32)

        cc = lambda i: cst[:, i:i + 1]

        for dst, src in ((b1s, b1_d), (cst, consts_d), (lrow, lrow_d), (onesr, onesr_d)):
            E.op("g", lambda g, d=dst, s=src: g.dma_start(d[:], s.ap()), inc="sm1", inc_by=16)
        E.op("g", lambda g: g.dma_start(wrs[:].rearrange("p (k e) -> p k e", k=KC), wr_d.ap().rearrange("k p e -> p k e")), inc="sm1", inc_by=16)
        E.op("g", lambda g: g.dma_start(b2s[:], b2_d.ap()), inc="sm1", inc_by=16)
        tok_smalls = ("sm1", E.counts["sm1"])

        t1 = E.op("v", lambda v: v.tensor_tensor(lprod[0:1, 0:64], lrow[0:1, 0:64], lrow[0:1, 64:128], op=OP.mult),
                  waits=[tok_smalls], inc="vS")
        t2 = E.op("v", lambda v: v.tensor_tensor(lprod[0:1, 64:128], lrow[0:1, 128:192], lrow[0:1, 192:256], op=OP.mult),
                  waits=[t1], inc="vS")
        t3 = E.op("v", lambda v: v.reduce_sum(l11[0:1, 0:1], lprod[0:1, 0:64], axis=AX.X), waits=[t2], inc="vS")
        t4 = E.op("v", lambda v: v.reduce_sum(l11[0:1, 1:2], lprod[0:1, 64:128], axis=AX.X), waits=[t3], inc="vS")
        t5 = E.op("s", lambda s: s.activation(l11[0:1, 2:3], l11[0:1, 0:1], AF.Exp), waits=[t3, tok_smalls], inc="sS")
        t6 = E.op("s", lambda s: s.activation(l11[0:1, 3:4], l11[0:1, 1:2], AF.Exp), waits=[t4, t5], inc="sS")
        t7 = E.op("v", lambda v: v.tensor_tensor(l11[0:1, 4:5], l11[0:1, 2:3], l11[0:1, 3:4], op=OP.subtract),
                  waits=[t6], inc="vS")
        t8 = E.op("v", lambda v: v.tensor_scalar(l11[0:1, 5:6], l11[0:1, 4:5], cst[0:1, C_LAMI:C_LAMI + 1], None, op0=OP.add),
                  waits=[t7], inc="vS")
        t9 = E.op("t", lambda t: t.matmul(pb[7][:, 0:1], onesr[0:1, :], l11[0:1, 5:6], start=True, stop=True),
                  waits=[t8, tok_smalls], inc="tS")
        tok_lam = E.op("v", lambda v: v.tensor_copy(lamc[:], pb[7][:, 0:1]), waits=[t9], inc="vS")
        if debug:
            E.op("y", lambda y: y.dma_start(dbg["lam"].ap(), lamc[:]), waits=[tok_lam], inc="dbq", inc_by=16)

        E.mute = MAX_PHASE < 1
        with contextlib.ExitStack() as c23:
            sb23 = lambda n, sh, dt: c23.enter_context(nc.sbuf_tensor(n, sh, dt))
            oT = sb23("oT", [128, PAIRS * T], BF16)
            lngt = sb23("lngt", [128, 128], F32)
            lnbt = sb23("lnbt", [128, 128], F32)
            idf = sb23("idf_s", [128, 128], F32)
            convb = [sb23(f"convb{i}", [128, 2048], BF16) for i in range(2)]
            for dst_, src_ in ((lngt, lng_d), (lnbt, lnb_d), (idf, idf_d)):
                E.op("g", lambda g, d=dst_, s=src_: g.dma_start(d[:], s.ap()), inc="sm2", inc_by=16)
            tok_smalls2 = ("sm2", E.counts["sm2"])

            conv_done_w1 = [None] * NE
            conv_done_w2 = [None] * NE
            cvt_slot = [None, None]
            conv_steps = []
            for ei in range(NE):
                for which in range(2):
                    for gi in range(NG):
                        for hf in range(2):
                            conv_steps.append((which, ei, gi, hf))
            conv_idx = [0]

            def emit_conv(n):
                saved = E.mute
                E.mute = MAX_PHASE < 4
                for _ in range(n):
                    if conv_idx[0] >= len(conv_steps):
                        break
                    which, e_, gi, hf = conv_steps[conv_idx[0]]
                    sl = conv_idx[0] % 2
                    conv_idx[0] += 1
                    src_, dstd = ((w1_d, wsc1_d), (w2_d, wsc2_d))[which]
                    tin = E.op("g", lambda g, s=src_, e=e_, gg=gi, sl=sl, hf=hf:
                               g.dma_start(convb[sl][:], s.ap()[e, gg][:, hf * 2048:(hf + 1) * 2048]),
                               waits=[cvt_slot[sl]], inc=f"cin{sl}", inc_by=16)
                    tout = E.op("y", lambda y, d=dstd, e=e_, gg=gi, sl=sl, hf=hf:
                                y.dma_start(d.ap()[e, gg][:, hf * 2048:(hf + 1) * 2048], convb[sl][:]),
                                waits=[tin], inc=f"cout{sl}", inc_by=16)
                    cvt_slot[sl] = tout
                    if gi == NG - 1 and hf == 1:
                        done = toks(cvt_slot[0], cvt_slot[1])
                        if which == 0:
                            conv_done_w1[e_] = done
                        else:
                            conv_done_w2[e_] = done
                E.mute = saved

            E.mute = MAX_PHASE < 1
            xnT_tok = [None] * NT
            combT_wr = [None] * NT
            s_wr = [None] * NT
            with contextlib.ExitStack() as c1:
                sb1 = lambda n, sh, dt: c1.enter_context(nc.sbuf_tensor(n, sh, dt))
                xbuf = [sb1(f"xbuf{i}", [128, D], F32) for i in range(2)]
                xnf = [sb1(f"xnf{i}", [128, D], F32) for i in range(2)]
                xtr = [sb1(f"xtr{i}", [128, 128], F32) for i in range(2)]
                xsq = [sb1(f"xsq{i}", [128, D], F32) for i in range(2)]
                lg = [sb1(f"lg{i}", [128, 64], F32) for i in range(2)]
                cts = [sb1(f"cts{i}", [8, 128], BF16) for i in range(2)]

                x_tiles = x_d.ap().rearrange("(n p) d -> n p d", p=128)
                xbuf_free = [None, None]
                xnf_free = [None, None]
                xtr_free = [None, None]
                tp_free = [[tok_lam], None]
                sq_free = [None, None]
                lg_free = [None, None]
                cts_free = [None, None]
                logit_psum_free = None
                prev_sq = [None, None]

                pending_ct = [None]

                def flush_ct():
                    if pending_ct[0] is not None:
                        f = pending_ct[0]
                        pending_ct[0] = None
                        f()

                for i in range(NT):
                    E.mute = (i >= NT_LIM) or (MAX_PHASE < 1)
                    emit_conv(4)
                    flush_ct()
                    bi = i % 2
                    B0 = 32 * bi
                    tok_x = E.op("g", lambda g, i=i, bi=bi: g.dma_start(xbuf[bi][:], x_tiles[i]),
                                 waits=[xbuf_free[bi]], inc=f"x{bi}", inc_by=16)
                    t_s2 = E.op("s", lambda s, bi=bi: s.activation(xsq[bi][:], xbuf[bi][:], AF.Square,
                                                                   accum_out=stat[:, B0 + 0:B0 + 1]),
                                waits=[tok_x, tok_smalls, sq_free[bi], prev_sq[bi]], inc="sS")
                    prev_sq[bi] = t_s2
                    t_s1 = E.op("v", lambda v, bi=bi: v.reduce_sum(stat[:, B0 + 1:B0 + 2], xbuf[bi][:], axis=AX.X),
                                waits=[tok_x], inc="vS")
                    t_nmu = E.op("v", lambda v: v.tensor_scalar(stat[:, B0 + 2:B0 + 3], stat[:, B0 + 1:B0 + 2], cc(C_NINV1024), None, op0=OP.mult),
                                 waits=[t_s1], inc="vS")
                    t_m2 = E.op("v", lambda v: v.tensor_tensor(stat[:, B0 + 3:B0 + 4], stat[:, B0 + 2:B0 + 3], stat[:, B0 + 2:B0 + 3], op=OP.mult),
                                waits=[t_nmu], inc="vS")
                    t_va = E.op("v", lambda v: v.tensor_scalar(stat[:, B0 + 4:B0 + 5], stat[:, B0 + 0:B0 + 1], cc(C_INV1024), None, op0=OP.mult),
                                waits=[t_s2, t_m2], inc="vS")
                    sq_free[bi] = t_va
                    t_vb = E.op("v", lambda v: v.tensor_tensor(stat[:, B0 + 5:B0 + 6], stat[:, B0 + 4:B0 + 5], stat[:, B0 + 3:B0 + 4], op=OP.subtract),
                                waits=[t_va], inc="vS")
                    t_ln = E.op("s", lambda s: s.activation(stat[:, B0 + 6:B0 + 7], stat[:, B0 + 5:B0 + 6], AF.Ln, bias=cc(C_EPS)),
                                waits=[t_vb], inc="sS")
                    t_rs = E.op("s", lambda s: s.activation(stat[:, B0 + 7:B0 + 8], stat[:, B0 + 6:B0 + 7], AF.Exp, scale=cc(C_NEGHALF)),
                                waits=[t_ln], inc="sS")
                    t_xn = E.op("v", lambda v, bi=bi: v.tensor_scalar(xnf[bi][:], xbuf[bi][:], stat[:, B0 + 2:B0 + 3],
                                                                      stat[:, B0 + 7:B0 + 8], op0=OP.add, op1=OP.mult),
                                waits=[t_rs, xnf_free[bi]], inc="vS")
                    xbuf_free[bi] = t_xn

                    if BODY_LIM < 2:
                        E.mute = True
                    last_sc = None
                    last_vc = None
                    t_mm = None
                    for kc in range(KC):
                        tb_ = kc % 2
                        pbank = pb[7] if tb_ == 0 else pb[4]
                        t_tp = E.op("t", lambda t, bi=bi, kc=kc, pk=pbank: t.transpose(
                            pk[:, 0:128], xnf[bi][:, kc * 128:(kc + 1) * 128], idf[:]),
                            waits=[t_xn, tok_smalls, tok_smalls2, tp_free[tb_]], inc="tS")
                        if BODY_LIM < 3:
                            E.mute = True
                        t_sc = E.op("s", lambda s, i=i, kc=kc, pk=pbank: s.copy(
                            xnT[:, kc * T + i * 128: kc * T + (i + 1) * 128], pk[:, 0:128]),
                            waits=[t_tp], inc="sS")
                        if BODY_LIM >= 2 and not (i >= NT_LIM or MAX_PHASE < 1):
                            E.mute = False
                        if BODY_LIM < 4:
                            E.mute = True
                        t_vc = E.op("s", lambda s, kc=kc, pk=pbank: s.copy(xtr[kc % 2][:], pk[:, 0:128]),
                                    waits=[t_tp, xtr_free[kc % 2]], inc="sS")
                        if BODY_LIM >= 2 and not (i >= NT_LIM or MAX_PHASE < 1):
                            E.mute = False
                        tp_free[tb_] = toks(t_sc, t_vc)
                        if BODY_LIM < 5:
                            E.mute = True
                        t_mm = E.op("t", lambda t, kc=kc: t.matmul(
                            pb[5][:, 0:8], xtr[kc % 2][:], wrs[:, kc * NE:(kc + 1) * NE],
                            start=(kc == 0), stop=(kc == KC - 1)),
                            waits=[t_vc, tok_smalls] + ([logit_psum_free] if kc == 0 else []), inc="tS")
                        if BODY_LIM >= 2 and not (i >= NT_LIM or MAX_PHASE < 1):
                            E.mute = False
                        xtr_free[kc % 2] = t_mm
                        last_sc, last_vc = t_sc, t_vc
                    xnT_tok[i] = last_sc
                    xnf_free[bi] = toks(last_sc, last_vc)

                    if BODY_LIM < 6:
                        E.mute = True
                    t_l = E.op("v", lambda v: v.tensor_copy(lg[bi][:, 0:8], pb[5][:, 0:8]),
                               waits=[t_mm, lg_free[bi]], inc="vS")
                    logit_psum_free = t_l
                    t_m1 = E.op("v", lambda v: v.reduce_max(stat[:, B0 + 8:B0 + 9], lg[bi][:, 0:8], axis=AX.X), waits=[t_l], inc="vS")
                    t_k1 = E.op("v", lambda v: v.tensor_scalar(lg[bi][:, 8:16], lg[bi][:, 0:8], stat[:, B0 + 8:B0 + 9], None, op0=OP.is_ge),
                                waits=[t_m1], inc="vS")
                    t_pen = E.op("v", lambda v: v.tensor_scalar(lg[bi][:, 16:24], lg[bi][:, 8:16], cc(C_BIG), None, op0=OP.mult),
                                 waits=[t_k1], inc="vS")
                    t_msk = E.op("v", lambda v: v.tensor_tensor(lg[bi][:, 24:32], lg[bi][:, 0:8], lg[bi][:, 16:24], op=OP.subtract),
                                 waits=[t_pen], inc="vS")
                    t_m2r = E.op("v", lambda v: v.reduce_max(stat[:, B0 + 9:B0 + 10], lg[bi][:, 24:32], axis=AX.X), waits=[t_msk], inc="vS")
                    t_k2 = E.op("v", lambda v: v.tensor_scalar(lg[bi][:, 32:40], lg[bi][:, 24:32], stat[:, B0 + 9:B0 + 10], None, op0=OP.is_ge),
                                waits=[t_m2r], inc="vS")
                    t_d = E.op("v", lambda v: v.tensor_tensor(stat[:, B0 + 10:B0 + 11], stat[:, B0 + 9:B0 + 10], stat[:, B0 + 8:B0 + 9], op=OP.subtract),
                               waits=[t_m2r], inc="vS")
                    t_e = E.op("s", lambda s: s.activation(stat[:, B0 + 11:B0 + 12], stat[:, B0 + 10:B0 + 11], AF.Exp), waits=[t_d], inc="sS")
                    t_e1 = E.op("v", lambda v: v.tensor_scalar(stat[:, B0 + 12:B0 + 13], stat[:, B0 + 11:B0 + 12], cc(C_ONE), None, op0=OP.add),
                                waits=[t_e], inc="vS")
                    t_p1 = E.op("v", lambda v: v.reciprocal(stat[:, B0 + 13:B0 + 14], stat[:, B0 + 12:B0 + 13]), waits=[t_e1], inc="vS")
                    t_p2 = E.op("v", lambda v: v.tensor_tensor(stat[:, B0 + 14:B0 + 15], stat[:, B0 + 11:B0 + 12], stat[:, B0 + 13:B0 + 14], op=OP.mult),
                                waits=[t_p1], inc="vS")
                    t_c1 = E.op("v", lambda v: v.tensor_scalar(lg[bi][:, 40:48], lg[bi][:, 8:16], stat[:, B0 + 13:B0 + 14], None, op0=OP.mult),
                                waits=[t_p1], inc="vS")
                    t_c2 = E.op("v", lambda v: v.tensor_scalar(lg[bi][:, 48:56], lg[bi][:, 32:40], stat[:, B0 + 14:B0 + 15], None, op0=OP.mult),
                                waits=[t_p2, t_k2], inc="vS")
                    t_cb = E.op("v", lambda v, i=i: v.tensor_tensor(
                        comb[:, i * 8:(i + 1) * 8], lg[bi][:, 40:48], lg[bi][:, 48:56], op=OP.add),
                        waits=[t_c1, t_c2], inc="vS")
                    lg_free[bi] = t_cb

                    if BODY_LIM < 7:
                        E.mute = True
                    def do_ct(i=i, t_cb=t_cb):
                        csl = i % 2
                        t_ct = E.op("t", lambda t, i=i: t.transpose(pb[6][0:8, 0:128], comb[:, i * 8:(i + 1) * 8], idf[:]),
                                    waits=[t_cb, tok_smalls2, cts_free[csl]], inc="tS")
                        t_cs = E.op("s", lambda s, csl=csl: s.copy(cts[csl][0:8, :], pb[6][0:8, 0:128]), waits=[t_ct], inc="sS")
                        combT_wr[i] = E.op("s", lambda s, i=i, csl=csl: s.dma_start(
                            combT_d.ap()[:, i * 128:(i + 1) * 128], cts[csl][0:8, :]),
                            waits=[t_cs], inc=f"cbt{csl}", inc_by=16)
                        cts_free[csl] = toks(t_cs, combT_wr[i])
                    pending_ct[0] = do_ct

                E.mute = MAX_PHASE < 1
                flush_ct()
                p1_end = [("vS", E.counts["vS"]), ("sS", E.counts["sS"]), ("tS", E.counts["tS"]),
                          ("cbt0", E.counts["cbt0"]), ("cbt1", E.counts["cbt1"]), ("x0", E.counts["x0"]), ("x1", E.counts["x1"])]
                if debug:
                    E.op("y", lambda y: y.dma_start(dbg["xnT"].ap(), xnT[:]), waits=p1_end, inc="dbq", inc_by=16)
                    E.op("y", lambda y: y.dma_start(dbg["comb"].ap(), comb[:]), waits=p1_end, inc="dbq", inc_by=16)

            E.mute = MAX_PHASE < 2
            with contextlib.ExitStack() as c2:
                sb2 = lambda n, sh, dt: c2.enter_context(nc.sbuf_tensor(n, sh, dt))
                qTb = sb2("qTb", [128, T], BF16)
                kTb = sb2("kTb", [128, 2 * T], BF16)
                vbuf = sb2("vbuf", [128, KB * 129], BF16)
                wqb = [sb2(f"wqb{i}", [128, KC * 128], BF16) for i in range(2)]
                wkb = [sb2(f"wkb{i}", [128, KC * 128], BF16) for i in range(2)]
                wvb = [sb2(f"wvb{i}", [128, KC * 128], BF16) for i in range(2)]
                e1b = [sb2(f"e1b{i}", [128, 512], BF16) for i in range(4)]
                e2b = [sb2(f"e2b{i}", [128, 512], BF16) for i in range(4)]
                oqb = [sb2(f"oqb{i}", [128, 512], F32) for i in range(2)]
                o1sb = sb2("o1sb", [128, 4 * 129], F32)
                o2sb = sb2("o2sb", [128, 4 * 129], F32)
                ot1 = sb2("ot1", [128, 128], F32)
                ot2 = sb2("ot2", [128, 128], F32)
                ojk = sb2("ojk", [128, 128], F32)

                E.barrier(p1_end)
                tok_ones = E.op("v", lambda v: v.memset(vbuf[:], 1.0), waits=p1_end, inc="vS")
                tok_kz = E.op("v", lambda v: v.memset(kTb[:], 0.0), waits=p1_end, inc="vS")

                wq_tok = [None, None]
                wk_tok = [None, None]
                wv_tok = [None, None]
                wq_free = [None, None]

                def load_pair_w(p):
                    sl = p % 2
                    E.op("g", lambda g, p=p, sl=sl: g.dma_start(
                        wqb[sl][:].rearrange("p (k q) -> p k q", k=KC), wq_d.ap()[p].rearrange("k p q -> p k q")),
                        waits=[wq_free[sl]], inc=f"qkv{sl}", inc_by=16)
                    E.op("g", lambda g, p=p, sl=sl: g.dma_start(
                        wkb[sl][:].rearrange("p (k q) -> p k q", k=KC), wk_d.ap()[p].rearrange("k p q -> p k q")), inc=f"qkv{sl}", inc_by=16)
                    tok3 = E.op("g", lambda g, p=p, sl=sl: g.dma_start(
                        wvb[sl][:].rearrange("p (k q) -> p k q", k=KC), wv_d.ap()[p].rearrange("k p q -> p k q")), inc=f"qkv{sl}", inc_by=16)
                    wq_tok[sl] = tok3
                    wk_tok[sl] = tok3
                    wv_tok[sl] = tok3

                load_pair_w(0)

                pending_tp = [None]
                pending_mg = [None]
                pending_mgs = [None]
                sync_mg = [None]
                sync_mgv = [None]

                def do_pending_mg():
                    if pending_mg[0] is not None:
                        f = pending_mg[0]
                        pending_mg[0] = None
                        f()

                def do_pending_mgs():
                    do_pending_mg()
                    if pending_mgs[0] is not None:
                        f = pending_mgs[0]
                        pending_mgs[0] = None
                        f()

                def emit_tp():
                    if pending_tp[0] is None:
                        return
                    ob_, mtoks, p_, qb_ = pending_tp[0]
                    pending_tp[0] = None
                    tp_sc = [None, None]
                    for qs in range(4):
                        half = qs % 2
                        t_tp = E.op("t", lambda t, ob=ob_, qs=qs, half=half: t.transpose(
                            pb[2 + half][:, 0:128],
                            oqb[ob][:, qs * 128:(qs + 1) * 128], idf[:]),
                            waits=[mtoks[qs], bankw[2 + half], tp_sc[half]], inc="tS")
                        t_sc = E.op("s", lambda s, p=p_, qb=qb_, qs=qs, half=half: s.copy(
                            oT[:, p * T + qb * 512 + qs * 128: p * T + qb * 512 + (qs + 1) * 128],
                            pb[2 + half][:, 0:128]),
                            waits=[t_tp], inc="sS")
                        tp_sc[half] = t_sc
                        oT_tok.append(t_sc)
                    bankw[2] = [tp_sc[0]]
                    bankw[3] = [tp_sc[1]]
                    oqb_free[ob_] = toks(tp_sc[0], tp_sc[1])

                qT_tok = [None] * QB
                kT_tok = [None] * QB
                v_tok = [None] * KB
                bankw = {b: list(p1_end) for b in range(8)}
                oqb_free = [None, None]
                oT_tok = []

                for p in range(PAIRS):
                    sl = p % 2
                    for tq in range(QB):
                        bank = tq % 2
                        t_mm = None
                        for kc in range(KC):
                            t_mm = E.op("t", lambda t, kc=kc, tq=tq, sl=sl, bank=bank: t.matmul(
                                pb[bank][:, 0:512], wqb[sl][:, kc * 128:(kc + 1) * 128],
                                xnT[:, kc * T + tq * 512: kc * T + (tq + 1) * 512],
                                start=(kc == 0), stop=(kc == KC - 1)),
                                waits=[wq_tok[sl], xnT_tok[tq * 4 + 3], bankw[bank]] if kc == 0 else [],
                                inc="tS" if kc == KC - 1 else None)
                        t_ev = E.op("s", lambda s, tq=tq, bank=bank: s.activation(
                            qTb[:, tq * 512:(tq + 1) * 512], pb[bank][:, 0:512], AF.Copy, scale=cc(C_EIGHTH)),
                            waits=[t_mm, tok_smalls], inc="sS")
                        qT_tok[tq] = t_ev
                        bankw[bank] = [t_ev]
                    for tq in range(QB):
                        bank = 2 + tq % 2
                        t_mm = None
                        for kc in range(KC):
                            t_mm = E.op("t", lambda t, kc=kc, tq=tq, sl=sl, bank=bank: t.matmul(
                                pb[bank][:, 0:512], wkb[sl][:, kc * 128:(kc + 1) * 128],
                                xnT[:, kc * T + tq * 512: kc * T + (tq + 1) * 512],
                                start=(kc == 0), stop=(kc == KC - 1)),
                                waits=[wk_tok[sl], xnT_tok[tq * 4 + 3], bankw[bank]] if kc == 0 else [],
                                inc="tS" if kc == KC - 1 else None)
                        t_e1 = E.op("s", lambda s, tq=tq, bank=bank: s.copy(
                            kTb[0:64, tq * 512:(tq + 1) * 512], pb[bank][0:64, 0:512]),
                            waits=[t_mm, tok_kz], inc="sS")
                        t_e2 = E.op("s", lambda s, tq=tq, bank=bank: s.copy(
                            kTb[64:128, T + tq * 512: T + (tq + 1) * 512], pb[bank][64:128, 0:512]),
                            waits=[t_e1], inc="sS")
                        kT_tok[tq] = t_e2
                        bankw[bank] = [t_e2]
                    for tv in range(KB):
                        bank = 4 + tv % 2
                        t_mm = None
                        for kc in range(KC):
                            t_mm = E.op("t", lambda t, kc=kc, tv=tv, sl=sl, bank=bank: t.matmul(
                                pb[bank][:, 0:128], xnT[:, kc * T + tv * 128: kc * T + (tv + 1) * 128],
                                wvb[sl][:, kc * 128:(kc + 1) * 128],
                                start=(kc == 0), stop=(kc == KC - 1)),
                                waits=[wv_tok[sl], tok_ones, xnT_tok[tv // 4 * 4 + 3], bankw[bank]] if kc == 0 else [],
                                inc="tS" if kc == KC - 1 else None)
                        t_ev = E.op("v", lambda v, tv=tv, bank=bank: v.tensor_copy(
                            vbuf[:, tv * 129: tv * 129 + 128], pb[bank][:, 0:128]),
                            waits=[t_mm], inc="vS")
                        v_tok[tv] = t_ev
                        bankw[bank] = [t_ev]
                    if p + 1 < PAIRS:
                        wq_free[(p + 1) % 2] = [("tS", E.counts["tS"])]
                        load_pair_w(p + 1)

                    for qb in range(QB):
                        emit_conv(2)
                        o1_evac = []
                        pv_last = None
                        for stream in (1, 2):
                            ebuf = e1b if stream == 1 else e2b
                            plo = 0 if stream == 1 else 64
                            e_free = [None, None, None, None]
                            pend = []

                            def emit_pv(kb_, eb_, tx_):
                                t_pv = None
                                for qs in range(4):
                                    bnk = 4 + qs
                                    w = [tx_, v_tok[kb_]]
                                    if kb_ == 0:
                                        w.append(bankw[bnk])
                                    t_pv = E.op("t", lambda t, kb=kb_, eb=eb_, qs=qs, ebuf=ebuf, bnk=bnk: t.matmul(
                                        pb[bnk][:, 0:129], ebuf[eb][:, qs * 128:(qs + 1) * 128],
                                        vbuf[:, kb * 129:(kb + 1) * 129],
                                        start=(kb == 0), stop=(kb == KB - 1)),
                                        waits=w, inc="tS" if qs == 3 else None)
                                e_free[eb_] = t_pv
                                return t_pv

                            for kb in range(KB):
                                eb = kb % 4
                                sb_ = kb % 4
                                coff = 0 if stream == 1 else T
                                t_s = E.op("t", lambda t, kb=kb, qb=qb, sb_=sb_, coff=coff: t.matmul(
                                    pb[sb_][:, 0:512], kTb[:, coff + kb * 128: coff + (kb + 1) * 128],
                                    qTb[:, qb * 512:(qb + 1) * 512],
                                    start=True, stop=True),
                                    waits=[qT_tok[qb], kT_tok[kb // 4], bankw[sb_]], inc="tS")
                                t_x = E.op("s", lambda s, eb=eb, sb_=sb_, ebuf=ebuf: s.activation(
                                    ebuf[eb][:], pb[sb_][:, 0:512], AF.Exp, bias=cc(C_NEG12)),
                                    waits=[t_s, e_free[eb]], inc="sS")
                                bankw[sb_] = [t_x]
                                pend.append((kb, eb, t_x))
                                if len(pend) > 2:
                                    t_pv = emit_pv(*pend.pop(0))
                                if kb == 6 and stream == 1:
                                    do_pending_mg()
                                if kb == 18 and stream == 1:
                                    do_pending_mgs()
                            while pend:
                                t_pv = emit_pv(*pend.pop(0))
                            pv_last = t_pv
                            if stream == 2:
                                o2_evac = []
                                for qs in range(4):
                                    t_e2 = E.op("v", lambda v, qs=qs: v.tensor_copy(
                                        o2sb[:, qs * 129:(qs + 1) * 129], pb[4 + qs][:, 0:129]),
                                        waits=[pv_last, sync_mg[0]], inc="vS")
                                    o2_evac.append(t_e2)
                                for qs in range(4):
                                    bankw[4 + qs] = [o2_evac[qs]]
                            if stream == 1:
                                for qs in range(4):
                                    t_e1 = E.op("v", lambda v, qs=qs: v.tensor_copy(
                                        o1sb[:, qs * 129:(qs + 1) * 129], pb[4 + qs][:, 0:129]),
                                        waits=[pv_last, sync_mg[0]], inc="vS")
                                    o1_evac.append(t_e1)
                                for qs in range(4):
                                    bankw[4 + qs] = [o1_evac[qs]]
                                emit_tp()

                        ob = qb % 2

                        def mg_stage_v(ob=ob, o1e=tuple(o1_evac), o2e=tuple(o2_evac)):
                            toks_v = []
                            for qs in range(4):
                                b = 16 + qs * 8
                                o1 = o1sb[:, qs * 129:qs * 129 + 128]
                                o2 = o2sb[:, qs * 129:qs * 129 + 128]
                                oq = oqb[ob][:, qs * 128:(qs + 1) * 128]
                                w = [o1e[qs], o2e[qs], tok_lam] + ([oqb_free[ob]] if qs == 0 else [])
                                t1 = E.op("v", lambda v, qs=qs, b=b: v.reciprocal(
                                    stat[:, b:b + 1], o1sb[:, qs * 129 + 128:qs * 129 + 129]), waits=w, inc="vS")
                                t2 = E.op("v", lambda v, qs=qs, b=b: v.reciprocal(
                                    stat[:, b + 1:b + 2], o2sb[:, qs * 129 + 128:qs * 129 + 129]), waits=[t1], inc="vS")
                                t3 = E.op("v", lambda v, b=b: v.tensor_scalar(
                                    stat[:, b + 1:b + 2], stat[:, b + 1:b + 2], lamc[:], None, op0=OP.mult),
                                    waits=[t2], inc="vS")
                                t4 = E.op("v", lambda v, o1=o1, b=b: v.tensor_scalar(
                                    o1, o1, stat[:, b:b + 1], None, op0=OP.mult), waits=[t3], inc="vS")
                                t5 = E.op("v", lambda v, o2=o2, b=b: v.tensor_scalar(
                                    o2, o2, stat[:, b + 1:b + 2], None, op0=OP.mult), waits=[t4], inc="vS")
                                t6 = E.op("v", lambda v, oq=oq, o1=o1, o2=o2: v.tensor_tensor(
                                    oq, o1, o2, op=OP.subtract), waits=[t5], inc="vS")
                                t7 = E.op("v", lambda v, o2=o2, oq=oq: v.tensor_tensor(
                                    o2, oq, oq, op=OP.mult), waits=[t6], inc="vS")
                                t8 = E.op("v", lambda v, b=b, o2=o2: v.reduce_sum(
                                    stat[:, b + 2:b + 3], o2, axis=AX.X), waits=[t7], inc="vS")
                                t9 = E.op("v", lambda v, b=b, oq=oq: v.reduce_sum(
                                    stat[:, b + 3:b + 4], oq, axis=AX.X), waits=[t6], inc="vS")
                                t10 = E.op("v", lambda v, b=b: v.tensor_scalar(
                                    stat[:, b + 3:b + 4], stat[:, b + 3:b + 4], cc(C_NINV128), None, op0=OP.mult),
                                    waits=[t9], inc="vS")
                                t11 = E.op("v", lambda v, b=b: v.tensor_tensor(
                                    stat[:, b + 4:b + 5], stat[:, b + 3:b + 4], stat[:, b + 3:b + 4], op=OP.mult),
                                    waits=[t10], inc="vS")
                                t12 = E.op("v", lambda v, b=b: v.tensor_scalar(
                                    stat[:, b + 2:b + 3], stat[:, b + 2:b + 3], cc(C_INV128), None, op0=OP.mult),
                                    waits=[t8], inc="vS")
                                t13 = E.op("v", lambda v, b=b: v.tensor_tensor(
                                    stat[:, b + 5:b + 6], stat[:, b + 2:b + 3], stat[:, b + 4:b + 5], op=OP.subtract),
                                    waits=[t11, t12], inc="vS")
                                toks_v.append(t13)
                            sync_mgv[0] = toks_v

                        def mg_stage_s(ob=ob, qb=qb, p=p):
                            toks_v = sync_mgv[0]
                            rsd = []
                            for qs in range(4):
                                b = 16 + qs * 8
                                tl = E.op("s", lambda s, b=b: s.activation(
                                    stat[:, b + 6:b + 7], stat[:, b + 5:b + 6], AF.Ln, bias=cc(C_EPS)),
                                    waits=[toks_v[qs]], inc="sS")
                                tr = E.op("s", lambda s, b=b: s.activation(
                                    stat[:, b + 7:b + 8], stat[:, b + 6:b + 7], AF.Exp, scale=cc(C_NEGHALF)),
                                    waits=[tl], inc="sS")
                                rsd.append(tr)
                            merge_toks = []
                            for qs in range(4):
                                b = 16 + qs * 8
                                o1 = o1sb[:, qs * 129:qs * 129 + 128]
                                o2 = o2sb[:, qs * 129:qs * 129 + 128]
                                oq = oqb[ob][:, qs * 128:(qs + 1) * 128]
                                ta = E.op("v", lambda v, oq=oq, o1=o1, b=b: v.tensor_scalar(
                                    o1, oq, stat[:, b + 3:b + 4], stat[:, b + 7:b + 8], op0=OP.add, op1=OP.mult),
                                    waits=[rsd[qs]], inc="vS")
                                tb_ = E.op("v", lambda v, o1=o1, o2=o2: v.tensor_tensor(
                                    o2, o1, lngt[:], op=OP.mult), waits=[ta, tok_smalls2], inc="vS")
                                tc = E.op("v", lambda v, oq=oq, o2=o2: v.tensor_tensor(
                                    oq, o2, lnbt[:], op=OP.add), waits=[tb_], inc="vS")
                                merge_toks.append(tc)
                            sync_mg[0] = toks(*merge_toks)
                            pending_tp[0] = (ob, merge_toks, p, qb)

                        pending_mg[0] = mg_stage_v
                        pending_mgs[0] = mg_stage_s
                    do_pending_mgs()
                    emit_tp()

                p2_end = [("tS", E.counts["tS"]), ("sS", E.counts["sS"]), ("vS", E.counts["vS"]),
                          ("qkv0", E.counts["qkv0"]), ("qkv1", E.counts["qkv1"])]
                if debug:
                    E.op("y", lambda y: y.dma_start(dbg["oT"].ap(), oT[:]), waits=p2_end, inc="dbq", inc_by=16)

            E.mute = MAX_PHASE < 3
            with contextlib.ExitStack() as c3:
                sb3 = lambda n, sh, dt: c3.enter_context(nc.sbuf_tensor(n, sh, dt))
                wob = sb3("wob", [128, KC * D], BF16)
                sst = [sb3(f"sst{i}", [128, D], BF16) for i in range(2)]

                emit_conv(10**6)
                E.barrier(p2_end)
                tok_wo = E.op("g", lambda g: g.dma_start(wob[:].rearrange("p (k d) -> p k d", k=KC), wo_d.ap().rearrange("k p d -> p k d")),
                              waits=p2_end, inc="woq", inc_by=16)
                sst_free = [None, None]
                pbw = [list(p2_end), list(p2_end)]
                for i in range(NT):
                    bi = i % 2
                    t_ev_last = None
                    for dc in range(2):
                        bank = dc
                        t_mm = None
                        for kc in range(KC):
                            t_mm = E.op("t", lambda t, kc=kc, i=i, dc=dc, bank=bank: t.matmul(
                                pb[bank][:, 0:512], oT[:, kc * T + i * 128: kc * T + (i + 1) * 128],
                                wob[:, kc * D + dc * 512: kc * D + dc * 512 + 512],
                                start=(kc == 0), stop=(kc == KC - 1)),
                                waits=[tok_wo, pbw[bank]] if kc == 0 else [],
                                inc="tS" if kc == KC - 1 else None)
                        t_ev = E.op("s", lambda s, dc=dc, bi=bi, bank=bank: s.copy(
                            sst[bi][:, dc * 512:dc * 512 + 512], pb[bank][:, 0:512]),
                            waits=[t_mm] + ([sst_free[bi]] if dc == 0 else []), inc="sS")
                        pbw[bank] = [t_ev]
                        t_ev_last = t_ev
                    s_wr[i] = E.op("s", lambda s, i=i, bi=bi: s.dma_start(
                        ssc_d.ap()[i * 128:(i + 1) * 128, :], sst[bi][:]),
                        waits=[t_ev_last], inc=f"sw{bi}", inc_by=16)
                    sst_free[bi] = s_wr[i]
                    if debug:
                        tdbg = E.op("s", lambda s, i=i, bi=bi: s.dma_start(
                            dbg["s"].ap()[i * 128:(i + 1) * 128, :], sst[bi][:]),
                            waits=[t_ev_last], inc=f"sw{bi}", inc_by=16)
                        sst_free[bi] = toks(s_wr[i], tdbg)

            p3_end = [("tS", E.counts["tS"]), ("sS", E.counts["sS"]), ("vS", E.counts["vS"]),
                      ("woq", E.counts["woq"]), ("sw0", E.counts["sw0"]), ("sw1", E.counts["sw1"]),
                      ("cin0", E.counts["cin0"]), ("cin1", E.counts["cin1"]),
                      ("cout0", E.counts["cout0"]), ("cout1", E.counts["cout1"]),
                      ("sm2", E.counts["sm2"]), ("dbq", E.counts["dbq"])]

        E.mute = MAX_PHASE < 4
        with contextlib.ExitStack() as c4:
            sb4 = lambda n, sh, dt: c4.enter_context(nc.sbuf_tensor(n, sh, dt))
            hbuf = sb4("hbuf", [128, HC * 512], BF16)
            w1g = [sb4(f"w1g{i}", [128, 4096], BF16) for i in range(2)]
            w2g = [sb4(f"w2g{i}", [128, 4096], BF16) for i in range(2)]
            gacc = sb4("gacc", [128, 4 * D], F32)
            stb = sb4("stb", [128, 4 * D], BF16)
            xtb = sb4("xtb", [128, 4 * D], F32)
            ostg = [sb4(f"ostg{i}", [128, D], F32) for i in range(2)]
            ytmp = [sb4(f"ytmp{i}", [128, 512], F32) for i in range(2)]
            gtmp = sb4("gtmp", [128, D], F32)
            cts4 = sb4("cts4", [8, T], BF16)

            x_tb_ap = x_d.ap().rearrange("(tb ts p) d -> tb p ts d", ts=4, p=128)
            s_tb_ap = ssc_d.ap().rearrange("(tb ts p) d -> tb p ts d", ts=4, p=128)
            xtb_v = xtb[:].rearrange("p (ts d) -> p ts d", ts=4)
            stb_v = stb[:].rearrange("p (ts d) -> p ts d", ts=4)

            E.barrier(p3_end)
            tok_ct4 = E.op("y", lambda y: y.dma_start(cts4[:], combT_d.ap()),
                           waits=[toks(*combT_wr), p3_end], inc="ctq", inc_by=16)

            w1_tok = [None] * 2
            w2_tok = [None] * 2
            w1_use = [list(p3_end) for _ in range(2)]
            w2_use = [list(p3_end) for _ in range(2)]
            pbw4 = {i: list(p3_end) for i in range(8)}

            def dma_w1(e, g):
                sl = g % 2
                w1_tok[sl] = E.op("y", lambda y, e=e, g=g, sl=sl: y.dma_start(
                    w1g[sl][:], wsc1_d.ap()[e, g]),
                    waits=[w1_use[sl], conv_done_w1[e]], inc=f"w1s{sl}", inc_by=16)

            def dma_w2(e, g):
                sl = g % 2
                w2_tok[sl] = E.op("y", lambda y, e=e, g=g, sl=sl: y.dma_start(
                    w2g[sl][:], wsc2_d.ap()[e, g]),
                    waits=[w2_use[sl], conv_done_w2[e]], inc=f"w2s{sl}", inc_by=16)

            gacc_free = None
            stb_free = None
            xtb_free = None
            ostg_free = [None, None]
            ytmp_free = [None, None]
            gtmp_free = None
            out_tok = []

            for tb in range(NTB):
                xtb_tok = E.op("g", lambda g, tb=tb: g.dma_start(xtb_v, x_tb_ap[tb]),
                               waits=[xtb_free], inc="xtq", inc_by=16)
                stb_tok = E.op("y", lambda y, tb=tb: y.dma_start(stb_v, s_tb_ap[tb]),
                               waits=[stb_free] + [s_wr[tb * 4 + j] for j in range(4)], inc="stq", inc_by=16)

                ginit = []
                for ts4 in range(4):
                    for dc in range(2):
                        bank = 6 + (ts4 * 2 + dc) % 2
                        t_mm = E.op("t", lambda t, tb=tb, ts4=ts4, dc=dc, bank=bank: t.matmul(
                            pb[bank][:, 0:512], cts4[0:8, tb * 512 + ts4 * 128: tb * 512 + (ts4 + 1) * 128],
                            b2s[0:8, dc * 512:dc * 512 + 512], start=True, stop=True),
                            waits=[tok_ct4, tok_smalls, pbw4[bank]], inc="tS")
                        t_cp = E.op("s", lambda s, ts4=ts4, dc=dc, bank=bank: s.copy(
                            gacc[:, ts4 * D + dc * 512: ts4 * D + dc * 512 + 512], pb[bank][:, 0:512]),
                            waits=[t_mm] + ([gacc_free] if (ts4 == 0 and dc == 0) else []), inc="sS")
                        pbw4[bank] = [t_cp]
                        ginit.append(t_cp)
                gacc_ready = toks(*ginit)

                for e in range(NE):
                    if e == 0 and tb == 0:
                        for g0 in range(2):
                            dma_w1(e, g0)
                        for g0 in range(2):
                            dma_w2(e, g0)

                    gelu_toks = []
                    for hc in range(HC):
                        g0 = hc // 4
                        sl = g0 % 2
                        bank = hc % 2
                        t_mm = None
                        for kc in range(KC):
                            t_mm = E.op("t", lambda t, hc=hc, kc=kc, tb=tb, sl=sl, bank=bank: t.matmul(
                                pb[bank][:, 0:512],
                                w1g[sl][:, (hc % 4) * 1024 + kc * 128: (hc % 4) * 1024 + (kc + 1) * 128],
                                xnT[:, kc * T + tb * 512: kc * T + (tb + 1) * 512],
                                start=(kc == 0), stop=(kc == KC - 1)),
                                waits=[w1_tok[sl], xnT_tok[NT - 1], pbw4[bank]] if kc == 0 else [],
                                inc="tS" if kc == KC - 1 else None)
                        if hc % 4 == 3:
                            w1_use[sl] = [t_mm]
                            gnext = g0 + 2
                            if gnext < NG:
                                dma_w1(e, gnext)
                            elif e + 1 < NE:
                                dma_w1(e + 1, gnext - NG)
                            elif tb + 1 < NTB:
                                dma_w1(0, gnext - NG)
                        t_ge = E.op("s", lambda s, hc=hc, e=e, bank=bank: s.activation(
                            hbuf[:, hc * 512:(hc + 1) * 512], pb[bank][:, 0:512], AF.Gelu,
                            bias=b1s[:, e * HC + hc: e * HC + hc + 1]),
                            waits=[t_mm, tok_smalls], inc="sS")
                        pbw4[bank] = [t_ge]
                        gelu_toks.append(t_ge)

                    t_mm2_last = {}
                    for hc in range(HC):
                        g0 = hc // 4
                        sl = g0 % 2
                        t_mm = None
                        for ts4 in range(4):
                            for dc in range(2):
                                bank = ts4 * 2 + dc
                                w = [gelu_toks[hc], w2_tok[sl]]
                                if hc == 0:
                                    w.append(pbw4[bank])
                                    w.append(gacc_ready)
                                give_inc = (hc == HC - 1) or (hc % 4 == 3 and ts4 == 3 and dc == 1)
                                t_mm = E.op("t", lambda t, hc=hc, ts4=ts4, dc=dc, sl=sl, bank=bank: t.matmul(
                                    pb[bank][:, 0:512],
                                    hbuf[:, hc * 512 + ts4 * 128: hc * 512 + (ts4 + 1) * 128],
                                    w2g[sl][:, (hc % 4) * 1024 + dc * 512: (hc % 4) * 1024 + dc * 512 + 512],
                                    start=(hc == 0), stop=(hc == HC - 1)),
                                    waits=w, inc="tS" if give_inc else None)
                                if hc == HC - 1:
                                    t_mm2_last[bank] = t_mm
                        if hc % 4 == 3:
                            w2_use[sl] = [t_mm]
                            gnext = g0 + 2
                            if gnext < NG:
                                dma_w2(e, gnext)
                            elif e + 1 < NE:
                                dma_w2(e + 1, gnext - NG)
                            elif tb + 1 < NTB:
                                dma_w2(0, gnext - NG)

                    new_ready = list(gacc_ready)
                    for bank in range(8):
                        ts4, dc = bank // 2, bank % 2
                        yb = bank % 2
                        ci = (tb * 4 + ts4) * 8 + e
                        t_sc = E.op("s", lambda s, bank=bank, ci=ci, yb=yb: s.activation(
                            ytmp[yb][:], pb[bank][:, 0:512], AF.Copy, scale=comb[:, ci:ci + 1]),
                            waits=[t_mm2_last[bank], ytmp_free[yb]], inc="sS")
                        t_ad = E.op("v", lambda v, ts4=ts4, dc=dc, yb=yb: v.tensor_tensor(
                            gacc[:, ts4 * D + dc * 512: ts4 * D + dc * 512 + 512],
                            gacc[:, ts4 * D + dc * 512: ts4 * D + dc * 512 + 512],
                            ytmp[yb][:], op=OP.add),
                            waits=[t_sc, gacc_ready], inc="vS")
                        ytmp_free[yb] = t_ad
                        pbw4[bank] = toks(t_sc, t_ad)
                        new_ready.append(t_ad)
                    gacc_ready = toks(*new_ready)

                tail_v = []
                t_si_last = None
                for ts4 in range(4):
                    ob = ts4 % 2
                    t_si = E.op("s", lambda s, ts4=ts4: s.activation(
                        gtmp[:], gacc[:, ts4 * D:(ts4 + 1) * D], AF.Silu),
                        waits=[gacc_ready, gtmp_free], inc="sS")
                    t_mu = E.op("v", lambda v, ts4=ts4: v.tensor_tensor(
                        gtmp[:], gtmp[:], stb[:, ts4 * D:(ts4 + 1) * D], op=OP.mult),
                        waits=[t_si, stb_tok], inc="vS")
                    t_ou = E.op("v", lambda v, ts4=ts4, ob=ob: v.tensor_tensor(
                        ostg[ob][:], gtmp[:], xtb[:, ts4 * D:(ts4 + 1) * D], op=OP.add),
                        waits=[t_mu, xtb_tok, ostg_free[ob]], inc="vS")
                    t_wr = E.op("g", lambda g, tb=tb, ts4=ts4, ob=ob: g.dma_start(
                        out_d.ap()[(tb * 4 + ts4) * 128:(tb * 4 + ts4 + 1) * 128, :], ostg[ob][:]),
                        waits=[t_ou], inc=f"ow{ob}", inc_by=16)
                    ostg_free[ob] = t_wr
                    gtmp_free = t_ou
                    t_si_last = t_si
                    tail_v.append(t_ou)
                    out_tok.append(t_wr)
                gacc_free = toks(t_si_last)
                xtb_free = toks(*tail_v)
                stb_free = toks(*tail_v)

        E.mute = False
        fin = toks(*out_tok) + [("dbq", E.counts["dbq"]), ("sw0", E.counts["sw0"]),
                                ("sw1", E.counts["sw1"]), ("cbt0", E.counts["cbt0"]),
                                ("cbt1", E.counts["cbt1"])]
        for sem, val in toks(*[t for t in fin if t[1] > 0]):
            h = E.sems[sem]
            E.ops["y"].append(lambda e, h=h, v=val: e.wait_ge(h, v))

        block = ctx.enter_context(nc.Block())
        E.emit(block)

    return nc


def _prep_core_inputs(x_core, inp):
    W1 = inp["L0_ffn_W1"]; W2 = inp["L0_ffn_W2"]
    w1t = np.ascontiguousarray(
        W1.reshape(NE, KC, 128, HC, 128)
          .transpose(0, 3, 2, 1, 4)
          .reshape(NE, NG, 4, 128, KC, 128)
          .transpose(0, 1, 3, 2, 4, 5)
          .reshape(NE, NG, 128, 4096))
    w2t = np.ascontiguousarray(
        W2.reshape(NE, NG, 4, 128, D)
          .transpose(0, 1, 3, 2, 4)
          .reshape(NE, NG, 128, 4096))
    def qk_t(W):
        return np.ascontiguousarray(W.reshape(KC, 128, PAIRS, 128).transpose(2, 0, 1, 3))
    consts = np.zeros((128, NCONST), np.float32)
    consts[:, C_NEG12] = -12.0
    consts[:, C_EIGHTH] = 0.125
    consts[:, C_NINV1024] = -1.0 / 1024
    consts[:, C_INV1024] = 1.0 / 1024
    consts[:, C_NINV128] = -1.0 / 128
    consts[:, C_INV128] = 1.0 / 128
    consts[:, C_EPS] = 1e-5
    consts[:, C_NEGHALF] = -0.5
    consts[:, C_ONE] = 1.0
    consts[:, C_BIG] = 1e30
    consts[:, C_LAMI] = LAM_INIT
    lrow = np.concatenate([inp["L0_attn_lq1"], inp["L0_attn_lk1"],
                           inp["L0_attn_lq2"], inp["L0_attn_lk2"]]).reshape(1, 256).astype(np.float32)
    return dict(
        x=np.ascontiguousarray(x_core),
        wq_t=qk_t(inp["L0_attn_Wq"]),
        wk_t=qk_t(inp["L0_attn_Wk"]),
        wv_t=qk_t(inp["L0_attn_Wv"]),
        wo_t=np.ascontiguousarray(inp["L0_attn_Wo"].reshape(KC, 128, D)),
        wr_t=np.ascontiguousarray(inp["L0_ffn_Wr"].reshape(KC, 128, NE)),
        w1_t=w1t, w2_t=w2t,
        b1_t=np.ascontiguousarray(inp["L0_ffn_b1"].reshape(NE, HC, 128).transpose(2, 0, 1).reshape(128, NE * HC)),
        b2_t=np.ascontiguousarray(inp["L0_ffn_b2"]),
        lng_t=np.ascontiguousarray(np.tile(inp["L0_attn_ln_g"].reshape(1, 128), (128, 1))),
        lnb_t=np.ascontiguousarray(np.tile(inp["L0_attn_ln_b"].reshape(1, 128), (128, 1))),
        lrow=lrow,
        consts=consts,
        idf=np.eye(128, dtype=np.float32),
        onesr=np.ones((1, 128), np.float32),
    )


def kernel(**inputs):
    x = np.asarray(inputs["L0_attn_x"], np.float32)
    inp = {k: np.asarray(v, np.float32) for k, v in inputs.items()}
    nc = build_program(debug=DEBUG)
    in_maps = [_prep_core_inputs(x[c], inp) for c in range(N_CORES)]
    res = run_bass_kernel_spmd(nc, in_maps, core_ids=list(range(N_CORES)))
    out = np.stack([np.asarray(res.results[c]["out"], np.float32) for c in range(N_CORES)])
    return out
